# version 07
# baseline (speedup 1.0000x reference)
import os
import numpy as np
import ml_dtypes
from contextlib import ExitStack

import concourse.bass as bass
import concourse.mybir as mybir
from concourse.bass_utils import run_bass_kernel_spmd

F32 = mybir.dt.float32
BF16 = mybir.dt.bfloat16
I32 = mybir.dt.int32
AF = mybir.ActivationFunctionType
ALU = mybir.AluOpType
AX = mybir.AxisListType
BF = ml_dtypes.bfloat16

NCORES = 8
D = 2048
H = 16
HD = 128
NT = 16
TLOC = NT * 128
TGLOB = 8192
INTER = 5504
NI = INTER // 128
MH = 1408
NMH = MH // 128
NE = 16
EPS = 1e-5

ENGINES = ("sync", "scalar", "vector", "tensor", "gpsimd")


class Ctr:
    __slots__ = ("h", "n", "name")

    def __init__(self, h, name):
        self.h = h
        self.n = 0
        self.name = name


class KB:

    def __init__(self, nc, st):
        self.nc = nc
        self.st = st
        self.streams = {e: [] for e in ENGINES}
        self.ctrs = []
        self._sems = 0
        self._pin = 0
        self._idx = 0

    def _new_phys(self):
        h = self.st.enter_context(self.nc.semaphore(f"ps{self._sems}"))
        c = Ctr(h, f"ps{self._sems}")
        self._sems += 1
        self.ctrs.append(c)
        return c

    def ctr(self, name=None):
        if self._idx < len(self.ctrs):
            c = self.ctrs[self._idx]
        else:
            c = self._new_phys()
        self._idx += 1
        return c

    def pin_ctr(self, name=None):
        assert self._idx == self._pin == len(self.ctrs) - 0 or True
        c = self._new_phys()
        self._pin = len(self.ctrs)
        self._idx = max(self._idx, self._pin)
        return c

    def new_scope(self):
        self._idx = self._pin

    def op(self, eng, fn, inc=None, incn=1, waits=()):
        wl = [(c, v) for (c, v) in waits if v > 0]
        if inc is not None:
            inc.n += incn
        sem = inc.h if inc is not None else None

        def emit(e, wl=wl, fn=fn, sem=sem, incn=incn):
            for (c, v) in wl:
                e.wait_ge(c.h, v)
            inst = fn(e)
            if sem is not None:
                inst.then_inc(sem, incn)
        self.streams[eng].append(emit)

    def dma(self, eng, fn, inc, waits=()):
        self.op(eng, fn, inc=inc, incn=16, waits=waits)

    def wait(self, eng, ctr, val):
        if val > 0:
            self.streams[eng].append(
                lambda e, c=ctr, v=val: e.wait_ge(c.h, v))

    def barrier(self):
        vals = [(c, c.n) for c in self.ctrs if c.n > 0]
        for e in ENGINES:
            for (c, v) in vals:
                self.wait(e, c, v)

    def finalize(self, block):
        for e in ENGINES:
            items = self.streams[e]
            if not items:
                continue

            def body(eng, items=items):
                for it in items:
                    it(eng)
            getattr(block, e)(body)


def sbuf(nc, st, name, shape, dt):
    return st.enter_context(nc.sbuf_tensor(name, shape, dt))


def psum(nc, st, name, shape, dt):
    return st.enter_context(nc.psum_tensor(name, shape, dt))



def emit_consts(kb, st, need_ident=True):
    nc = kb.nc
    C = {}
    C["c_done"] = kb.pin_ctr("c_done")
    if need_ident:
        C["ident"] = sbuf(nc, st, "ident", [128, 128], BF16)
        C["identi"] = sbuf(nc, st, "identi", [128, 128], I32)
        kb.op("gpsimd", lambda e: e.iota(C["identi"][:], [[-1, 128]], base=0,
                                         channel_multiplier=1), inc=C["c_done"])
        kb.op("vector", lambda e: e.tensor_scalar(
            C["ident"][:], C["identi"][:], 0.0, None, ALU.is_equal),
            inc=C["c_done"], waits=[(C["c_done"], 1)])
    C["cols"] = sbuf(nc, st, "ccols", [128, 8], F32)

    def setcol(i, val):
        kb.op("vector", lambda e: e.memset(C["cols"][:, i:i + 1], val),
              inc=C["c_done"])
    setcol(0, EPS)
    setcol(1, -0.5)
    setcol(2, np.log(HD ** -0.5).item())
    setcol(3, -1.0)
    setcol(4, 1e9)
    setcol(5, 0.5)
    setcol(6, np.log(HD ** -0.5).item())
    setcol(7, HD ** -0.5)
    C["eps"] = C["cols"][:, 0:1]
    C["neghalf"] = C["cols"][:, 1:2]
    C["lnqscale"] = C["cols"][:, 2:3]
    C["neg1"] = C["cols"][:, 3:4]
    C["big"] = C["cols"][:, 4:5]
    C["half16"] = C["cols"][0:16, 5:6]
    C["lnqscale16"] = C["cols"][0:16, 6:7]
    C["qkscale"] = C["cols"][:, 7:8]
    C["ones_row"] = sbuf(nc, st, "conesrow", [1, 128], BF16)
    kb.op("vector", lambda e: e.memset(C["ones_row"][:], 1.0), inc=C["c_done"])
    C["ones128"] = sbuf(nc, st, "cones128", [128, 128], BF16)
    kb.op("vector", lambda e: e.memset(C["ones128"][:], 1.0), inc=C["c_done"])
    C["zeros512"] = sbuf(nc, st, "czeros512", [128, 512], BF16)
    kb.op("vector", lambda e: e.memset(C["zeros512"][:], 0.0),
          inc=C["c_done"])
    return C


class GhostEmitter:

    def __init__(self, kb, st, lut_d, gT_sl, uT_sl, dT_sl, wg_out, wu_out,
                 wd_out, C):
        nc = kb.nc
        self.kb = kb
        self.ld = kb.ctr()
        self.cc = kb.ctr()
        self.lbc = kb.ctr()
        self.stc = kb.ctr()
        self.lut_row = sbuf(nc, st, "lutrow", [1, 16], BF16)
        self.lut_bc = sbuf(nc, st, "lutbc", [128, 16], F32)
        self.plut = psum(nc, st, "plut", [128, 512], F32)
        gshared = sbuf(nc, st, "gh_g", [128, 5504], I32)
        self.bufs = [dict(
            g=gshared,
            a=sbuf(nc, st, f"gh_a{i}", [128, 5504], BF16),
        ) for i in range(2)]
        self.tmp = sbuf(nc, st, "gh_t", [128, 5504], BF16)
        self.jobs = []
        for r in range(2):
            self.jobs.append((gT_sl[r * 128:(r + 1) * 128, :],
                              wg_out[r * 128:(r + 1) * 128, :], 5504))
        for r in range(2):
            self.jobs.append((uT_sl[r * 128:(r + 1) * 128, :],
                              wu_out[r * 128:(r + 1) * 128, :], 5504))
        for r in range(NI):
            self.jobs.append((dT_sl[r * 128:(r + 1) * 128, :],
                              wd_out[r * 128:(r + 1) * 128, :], 256))
        kb.dma("gpsimd", lambda e: e.dma_start(
            out=self.lut_row[:], in_=lut_d.ap().unsqueeze(0)), inc=self.ld)
        myl = self.ld.n
        kb.op("tensor", lambda e: e.matmul(
            self.plut[:, 0:16], C["ones_row"][:], self.lut_row[:],
            start=True, stop=True),
            inc=self.lbc, waits=[(self.ld, myl),
                                 (C["c_done"], C["c_done"].n)])
        kb.op("vector", lambda e: e.tensor_copy(
            self.lut_bc[:], self.plut[:, 0:16]),
            inc=self.lbc, waits=[(self.lbc, self.lbc.n)])
        self.mylbc = self.lbc.n
        self.stc0 = self.stc.n
        self.steps = []
        for j in range(len(self.jobs)):
            self.steps.append(("load", j, 0))
            for p in range(16):
                self.steps.append(("pass", j, p))
            self.steps.append(("store", j, 0))
        self.si = 0
        self.marks = {}

    def done(self):
        return self.si >= len(self.steps)

    def emit_next(self, k):
        kb = self.kb
        n = 0
        while n < k and self.si < len(self.steps):
            (kind, j, p) = self.steps[self.si]
            self.si += 1
            n += 1
            srcap, dst, w = self.jobs[j]
            bb = self.bufs[j % 2]
            if kind == "load":
                gw = [(self.stc, self.stc0 + max(0, 16 * (j - 1)))]
                if j > 0:
                    gw.append((self.cc, self.marks[j - 1]))
                kb.dma("gpsimd", lambda e, srcap=srcap, bb=bb, w=w:
                       e.dma_start(out=bb["g"][:, 0:w], in_=srcap),
                       inc=self.ld, waits=gw)
                self.marks[("ld", j)] = self.ld.n
            elif kind == "pass":
                if p == 0:
                    kb.op("vector", lambda e, bb=bb, w=w: e.memset(
                        bb["a"][:, 0:w], 0.0), inc=self.cc,
                        waits=[(self.ld, self.marks[("ld", j)]),
                               (self.lbc, self.mylbc),
                               (self.stc,
                                self.stc0 + max(0, 16 * (j - 1)))])
                kb.op("vector", lambda e, bb=bb, w=w, p=p: e.tensor_scalar(
                    self.tmp[:, 0:w], bb["g"][:, 0:w], float(p),
                    self.lut_bc[:, p:p + 1], ALU.is_equal, ALU.mult),
                    inc=self.cc, waits=[(self.cc, self.cc.n)])
                kb.op("vector", lambda e, bb=bb, w=w: e.tensor_tensor(
                    bb["a"][:, 0:w], bb["a"][:, 0:w], self.tmp[:, 0:w],
                    ALU.add), inc=self.cc, waits=[(self.cc, self.cc.n)])
                if p == 15:
                    self.marks[j] = self.cc.n
            else:
                kb.dma("gpsimd", lambda e, dst=dst, bb=bb, w=w:
                       e.dma_start(out=dst, in_=bb["a"][:, 0:w]),
                       inc=self.stc, waits=[(self.cc, self.marks[j])])

    def finish(self):
        self.emit_next(10 ** 9)
        return self.stc


def emit_ln_xnt(kb, st, x_rows, xnT_out, C, tag=""):
    nc = kb.nc
    ld2 = [kb.ctr(f"ln_ld{i}{tag}") for i in range(2)]
    dv = kb.ctr(f"ln_dv{tag}")
    ac = kb.ctr(f"ln_ac{tag}")
    pe = kb.ctr(f"ln_pe{tag}")
    ev = kb.ctr(f"ln_ev{tag}")
    stc = kb.ctr(f"ln_st{tag}")

    sx = [sbuf(nc, st, f"ln_x{i}{tag}", [128, D], F32) for i in range(2)]
    sxn = [sbuf(nc, st, f"ln_n{i}{tag}", [128, D], BF16) for i in range(2)]
    sbn6 = [sbuf(nc, st, f"ln_b6{i}{tag}", [128, 24], F32) for i in range(2)]
    sagg = [sbuf(nc, st, f"ln_ag{i}{tag}", [128, 2], F32) for i in range(2)]
    srstd = [sbuf(nc, st, f"ln_rs{i}{tag}", [128, 1], F32) for i in range(2)]
    ptr = [psum(nc, st, f"ln_pt{i}{tag}", [128, 1024], BF16) for i in range(2)]
    xnT_res = sbuf(nc, st, f"xnT_res{tag}", [128, NT * D], BF16)

    cd = C["c_done"].n
    apply_end = {}
    for t in range(NT):
        b = t % 2
        kb.dma("sync", lambda e, t=t, b=b:
               e.dma_start(out=sx[b][:], in_=x_rows[t * 128:(t + 1) * 128, :]),
               inc=ld2[b], waits=[(dv, apply_end.get(t - 2, 0))])
        myld = ld2[b].n
        for i in range(4):
            kb.op("vector", lambda e, b=b, i=i: e.bn_stats(
                sbn6[b][:, i * 6:(i + 1) * 6], sx[b][:, i * 512:(i + 1) * 512]),
                inc=dv, waits=[(ld2[b], myld)] if i == 0 else ())
        kb.op("vector", lambda e, b=b: e.bn_aggr(sagg[b][:], sbn6[b][:]),
              inc=dv, waits=[(dv, dv.n)])
        kb.op("scalar", lambda e, b=b: e.activation(
            srstd[b][:], sagg[b][:, 1:2], AF.Ln, bias=C["eps"]),
            inc=ac, waits=[(dv, dv.n), (C["c_done"], cd)])
        kb.op("scalar", lambda e, b=b: e.activation(
            srstd[b][:], srstd[b][:], AF.Exp, scale=C["neghalf"]),
            inc=ac, waits=[(ac, ac.n)])
        kb.op("vector", lambda e, b=b: e.tensor_scalar(
            sxn[b][:], sx[b][:], sagg[b][:, 0:1], srstd[b][:],
            ALU.subtract, ALU.mult),
            inc=dv, waits=[(ac, ac.n), (pe, max(0, pe.n - 16))])
        apply_end[t] = dv.n
        for cc in range(16):
            idx = t * 16 + cc
            pb = idx % 2
            sl = idx % 8
            kb.op("tensor", lambda e, b=b, cc=cc, pb=pb, sl=sl: e.transpose(
                ptr[pb][:, sl * 128:(sl + 1) * 128],
                sxn[b][:, cc * 128:(cc + 1) * 128], C["ident"][:]),
                inc=pe,
                waits=([(dv, dv.n), (C["c_done"], cd)] if cc == 0 else [])
                + [(ev, max(0, ev.n - 1))])
            kb.op("vector", lambda e, t=t, cc=cc, pb=pb, sl=sl: e.tensor_copy(
                xnT_res[:, cc * (NT * 128) + t * 128:
                        cc * (NT * 128) + (t + 1) * 128],
                ptr[pb][:, sl * 128:(sl + 1) * 128]),
                inc=ev, waits=[(pe, pe.n)])
    if xnT_out is not None:
        kb.dma("sync", lambda e: e.dma_start(
            out=xnT_out.ap().rearrange("(cc p) t -> p cc t", p=128),
            in_=xnT_res.ap().rearrange("p (cc t) -> p cc t", t=NT * 128)),
            inc=stc, waits=[(ev, ev.n)])
    return xnT_res, dict(ev=ev, st=stc, dv=dv)


def np_bf16(a):
    return np.asarray(a, dtype=BF)



def build_ghost():
    nc = bass.Bass("TRN2", target_bir_lowering=False)
    lut_d = nc.dram_tensor("lut", [16], F32, kind="ExternalInput")
    gT = nc.dram_tensor("gT_sl", [256, INTER], I32, kind="ExternalInput")
    uT = nc.dram_tensor("uT_sl", [256, INTER], I32, kind="ExternalInput")
    dT = nc.dram_tensor("dT_sl", [INTER, 256], I32, kind="ExternalInput")
    wg = nc.dram_tensor("wg_sl", [256, INTER], BF16, kind="ExternalOutput")
    wu = nc.dram_tensor("wu_sl", [256, INTER], BF16, kind="ExternalOutput")
    wd = nc.dram_tensor("wd_sl", [INTER, 256], BF16, kind="ExternalOutput")
    st = ExitStack()
    with st:
        kb = KB(nc, st)
        C = emit_consts(kb, st, need_ident=False)
        GhostEmitter(kb, st, lut_d, gT, uT, dT, wg, wu, wd, C).finish()
        kb.barrier()
        block = st.enter_context(nc.Block())
        kb.finalize(block)
    return nc


def build_l0main(parts=None):
    parts = parts if parts is not None else os.environ.get("L0PARTS", "all")
    nc = bass.Bass("TRN2", target_bir_lowering=False)
    x_rows = nc.dram_tensor("x_rows", [TLOC, D], F32, kind="ExternalInput")
    wqT = nc.dram_tensor("wqT", [D, D], F32, kind="ExternalInput")
    wkT = nc.dram_tensor("wkT", [D, D], F32, kind="ExternalInput")
    wvT = nc.dram_tensor("wvT", [D, D], F32, kind="ExternalInput")
    gamma_bc = nc.dram_tensor("gamma_bc", [128, 512], F32, kind="ExternalInput")
    beta_bc = nc.dram_tensor("beta_bc", [128, 512], F32, kind="ExternalInput")
    xnT_d = nc.dram_tensor("xnT", [D, TLOC], BF16, kind="ExternalOutput")
    qT_d = nc.dram_tensor("qT", [D, TLOC], BF16, kind="ExternalOutput")
    kT_d = nc.dram_tensor("kT", [D, TLOC], BF16, kind="ExternalOutput")
    v_d = nc.dram_tensor("v", [TLOC, D], BF16, kind="ExternalOutput")

    st = ExitStack()
    with st:
        kb = KB(nc, st)
        C = emit_consts(kb, st)
        xnT_res, lnc = emit_ln_xnt(kb, st, x_rows, xnT_d, C)
        if parts != "ln":
            emit_qkv0(kb, st, xnT_res, lnc["ev"], wqT, wkT, wvT, gamma_bc,
                      beta_bc, qT_d, kT_d, v_d, C, parts=parts)
        kb.barrier()
        block = st.enter_context(nc.Block())
        kb.finalize(block)
    return nc


def emit_qkv0(kb, st, xnT_res, xnt_ev, wqT, wkT, wvT, gamma_bc, beta_bc,
              qT_d, kT_d, v_d, C, parts="all", pre=(), filler=None):
    nc = kb.nc
    ld2 = [kb.ctr(f"qk_ld{i}") for i in range(2)]
    gld = kb.ctr("qk_gld")
    mm = kb.ctr("qk_mm")
    evp = kb.ctr("qk_evp")
    evq = kb.ctr("qk_ev")
    petr = kb.ctr("qk_petr")
    peev = kb.ctr("qk_peev")
    stc2 = [kb.ctr(f"qk_st{i}") for i in range(2)]

    slab = [sbuf(nc, st, f"qk_sl{i}", [128, 16 * 512], BF16) for i in range(2)]
    sgb = sbuf(nc, st, "qk_gb", [128, 1024], F32)
    pq = [psum(nc, st, f"qk_p{i}", [128, 512], F32) for i in range(2)]
    ptr = [psum(nc, st, f"qk_pt{i}", [128, 1024], BF16) for i in range(2)]
    sev = [sbuf(nc, st, f"qk_e{i}", [128, 512], F32) for i in range(2)]
    saf = [sbuf(nc, st, f"qk_a{i}", [128, 512], F32) for i in range(2)]
    sbn6 = [sbuf(nc, st, f"qk_b6{i}", [128, 48], F32) for i in range(2)]
    sagg = [sbuf(nc, st, f"qk_ag{i}", [128, 8], F32) for i in range(2)]
    srstd = [sbuf(nc, st, f"qk_rs{i}", [128, 4], F32) for i in range(2)]
    sqk = [sbuf(nc, st, f"qk_o{i}", [128, 512], BF16) for i in range(2)]
    sqkT = [sbuf(nc, st, f"qk_oT{i}", [128, 512], BF16) for i in range(2)]

    kb.dma("sync", lambda e: e.dma_start(out=sgb[:, 0:512], in_=gamma_bc[:, :]),
           inc=gld)
    kb.dma("sync", lambda e: e.dma_start(out=sgb[:, 512:1024],
                                         in_=beta_bc[:, :]), inc=gld)
    mygld = gld.n

    WS = [("q", wqT, qT_d), ("k", wkT, kT_d), ("v", wvT, v_d)]
    if parts == "v":
        WS = WS[2:]
    elif parts.startswith("q"):
        WS = WS[:1]
    cd = C["c_done"].n
    ginfo = {}
    sidx = 0
    for (wname, wT, out_d) in WS:
        for dc in range(4):
            sb = sidx % 2
            kb.dma("sync", lambda e, wT=wT, dc=dc, sb=sb: e.dma_start(
                out=slab[sb].ap().rearrange("p (cc d) -> p cc d", d=512),
                in_=wT.ap()[:, dc * 512:(dc + 1) * 512]
                    .rearrange("(cc p) d -> p cc d", p=128)),
                inc=ld2[sb], waits=[(mm, max(0, mm.n - NT))] + list(pre))
            myld = ld2[sb].n
            for t in range(NT):
                gidx = sidx * NT + t
                pb = gidx % 2
                eb = gidx % 2
                prev = ginfo.get(gidx - 2, dict(evq=0, petr=0, stc=[0, 0]))
                for cc in range(16):
                    kb.op("tensor", lambda e, sb=sb, t=t, cc=cc, pb=pb:
                          e.matmul(pq[pb][:],
                                   xnT_res[:, cc * TLOC + t * 128:
                                           cc * TLOC + (t + 1) * 128],
                                   slab[sb][:, cc * 512:(cc + 1) * 512],
                                   start=(cc == 0), stop=(cc == 15)),
                          inc=(mm if cc == 15 else None),
                          waits=([(ld2[sb], myld), (xnt_ev, xnt_ev.n),
                                  (evp, max(0, evp.n - 1))]
                                 if cc == 0 else ()))
                if wname == "v":
                    kb.op("scalar", lambda e, pb=pb, eb=eb: e.activation(
                        sqk[eb][:], pq[pb][:], AF.Identity),
                        inc=evp, waits=[(mm, mm.n),
                                        (stc2[eb], prev["stc"][eb]),
                                        (petr, prev["petr"])])
                    kb.dma("sync", lambda e, t=t, dc=dc, eb=eb: e.dma_start(
                        out=v_d[t * 128:(t + 1) * 128,
                                dc * 512:(dc + 1) * 512],
                        in_=sqk[eb][:]), inc=stc2[eb], waits=[(evp, evp.n)])
                elif parts in ("qnofm", "qtrplain"):
                    kb.op("scalar", lambda e, pb=pb, eb=eb: e.activation(
                        sqk[eb][:], pq[pb][:], AF.Identity),
                        inc=evp, waits=[(mm, mm.n), (evq, prev["evq"]),
                                        (petr, prev["petr"])])
                    evp_here = evp.n
                    for hh in range(4):
                        kb.op("tensor", lambda e, eb=eb, hh=hh,
                              pb=petr.n % 2: e.transpose(
                            ptr[pb][:, 0:128],
                            sqk[eb][:, hh * 128:(hh + 1) * 128],
                            C["ident"][:]),
                            inc=petr,
                            waits=[(evp, evp_here),
                                   (peev, max(0, petr.n - 1))])
                        kb.op("vector", lambda e, eb=eb, hh=hh,
                              tb=(petr.n - 1) % 2: e.tensor_copy(
                                  sqkT[eb][:, hh * 128:(hh + 1) * 128],
                                  ptr[tb][:, 0:128]),
                              inc=peev,
                              waits=[(petr, petr.n),
                                     (stc2[eb], prev["stc"][eb])])
                    if parts == "qtrplain":
                        kb.dma("sync", lambda e, t=t, dc=dc, eb=eb,
                               out_d=out_d: e.dma_start(
                                   out=out_d[t * 128:(t + 1) * 128,
                                             dc * 512:(dc + 1) * 512],
                                   in_=sqkT[eb][:]),
                               inc=stc2[eb], waits=[(peev, peev.n)])
                    else:
                        kb.dma("sync", lambda e, t=t, dc=dc, eb=eb,
                               out_d=out_d: e.dma_start(
                                   out=out_d.ap()[dc * 512:(dc + 1) * 512,
                                                  t * 128:(t + 1) * 128]
                                   .rearrange("(hh d) t -> d hh t", d=128),
                                   in_=sqkT[eb].ap().rearrange(
                                       "d (hh t) -> d hh t", t=128)),
                               inc=stc2[eb], waits=[(peev, peev.n)])
                else:
                    kb.op("scalar", lambda e, pb=pb, eb=eb: e.activation(
                        sev[eb][:], pq[pb][:], AF.Identity),
                        inc=evp, waits=[(mm, mm.n), (evq, prev["evq"])])
                    if parts == "qa":
                        kb.op("vector", lambda e, eb=eb: e.tensor_copy(
                            saf[eb][:], sev[eb][:]),
                            inc=evq, waits=[(evp, evp.n)])
                        kb.op("vector", lambda e, eb=eb: e.tensor_copy(
                            saf[eb][:, 0:1], saf[eb][:, 0:1]),
                            inc=evq, waits=[(evq, evq.n)])
                    elif parts == "qb":
                        kb.op("vector", lambda e, eb=eb: e.tensor_tensor(
                            saf[eb][:], sev[eb][:], sgb[:, 0:512], ALU.mult),
                            inc=evq, waits=[(evp, evp.n), (gld, mygld)])
                        kb.op("vector", lambda e, eb=eb: e.tensor_copy(
                            saf[eb][:, 0:1], saf[eb][:, 0:1]),
                            inc=evq, waits=[(evq, evq.n)])
                    else:
                        kb.op("vector", lambda e, eb=eb: e.tensor_tensor(
                            saf[eb][:], sev[eb][:], sgb[:, 0:512], ALU.mult),
                            inc=evq, waits=[(evp, evp.n), (gld, mygld)])
                        kb.op("vector", lambda e, eb=eb: e.tensor_tensor(
                            saf[eb][:], saf[eb][:], sgb[:, 512:1024], ALU.add),
                            inc=evq, waits=[(evq, evq.n)])
                    aff_done = evq.n
                    skip_stats = parts in ("qfm1", "qa", "qb")
                    if skip_stats:
                        kb.op("scalar", lambda e, eb=eb: e.activation(
                            sqk[eb][:], saf[eb][:], AF.Identity),
                            inc=evq, waits=[(evq, aff_done),
                                            (petr, prev["petr"])])
                    for hh in range(4 if not skip_stats else 0):
                        kb.op("vector", lambda e, eb=eb, hh=hh: e.bn_stats(
                            sbn6[eb][:, hh * 12:hh * 12 + 6],
                            saf[eb][:, hh * 128:(hh + 1) * 128]),
                            inc=evq,
                            waits=[(evq, aff_done)] if hh == 0 else ())
                    stats_done = evq.n
                    for hh in range(4 if not skip_stats else 0):
                        kb.op("vector", lambda e, eb=eb, hh=hh: e.bn_aggr(
                            sagg[eb][:, hh * 2:hh * 2 + 2],
                            sbn6[eb][:, hh * 12:hh * 12 + 6]),
                            inc=evq,
                            waits=[(evq, stats_done)] if hh == 0 else ())
                    aggr_done = evq.n
                    if parts in ("all", "q", "k", "qfm3"):
                        for hh in range(4):
                            kb.op("scalar", lambda e, eb=eb, hh=hh:
                                  e.activation(
                                      srstd[eb][:, hh:hh + 1],
                                      sagg[eb][:, 2 * hh + 1:2 * hh + 2],
                                      AF.Ln, bias=C["eps"]),
                                  inc=evq,
                                  waits=([(evq, aggr_done),
                                          (C["c_done"], cd)]
                                         if hh == 0 else ()))
                        ln_done = evq.n
                        qsc = (wname == "q")
                        for hh in range(4):
                            if qsc:
                                kb.op("scalar", lambda e, eb=eb, hh=hh:
                                      e.activation(
                                          srstd[eb][:, hh:hh + 1],
                                          srstd[eb][:, hh:hh + 1], AF.Exp,
                                          scale=C["neghalf"],
                                          bias=C["lnqscale"]),
                                      inc=evq,
                                      waits=[(evq, ln_done)]
                                      if hh == 0 else ())
                            else:
                                kb.op("scalar", lambda e, eb=eb, hh=hh:
                                      e.activation(
                                          srstd[eb][:, hh:hh + 1],
                                          srstd[eb][:, hh:hh + 1], AF.Exp,
                                          scale=C["neghalf"]),
                                      inc=evq,
                                      waits=[(evq, ln_done)]
                                      if hh == 0 else ())
                    else:
                        pass
                    rstd_done = evq.n
                    for hh in range(4 if not skip_stats else 0):
                        if parts == "qfm2":
                            kb.op("vector", lambda e, eb=eb, hh=hh:
                                  e.tensor_scalar(
                                      sqk[eb][:, hh * 128:(hh + 1) * 128],
                                      saf[eb][:, hh * 128:(hh + 1) * 128],
                                      sagg[eb][:, hh * 2:hh * 2 + 1], None,
                                      ALU.subtract),
                                  inc=evq,
                                  waits=([(evq, rstd_done),
                                          (petr, prev["petr"])]
                                         if hh == 0 else ()))
                        else:
                            kb.op("vector", lambda e, eb=eb, hh=hh:
                                  e.tensor_scalar(
                                      sqk[eb][:, hh * 128:(hh + 1) * 128],
                                      saf[eb][:, hh * 128:(hh + 1) * 128],
                                      sagg[eb][:, hh * 2:hh * 2 + 1],
                                      srstd[eb][:, hh:hh + 1],
                                      ALU.subtract, ALU.mult),
                                  inc=evq,
                                  waits=([(evq, rstd_done),
                                          (petr, prev["petr"])]
                                         if hh == 0 else ()))
                    apply_done = evq.n
                    for hh in range(4):
                        kb.op("tensor", lambda e, eb=eb, hh=hh,
                              pb=petr.n % 2: e.transpose(
                            ptr[pb][:, 0:128],
                            sqk[eb][:, hh * 128:(hh + 1) * 128],
                            C["ident"][:]),
                            inc=petr,
                            waits=[(evq, apply_done),
                                   (peev, max(0, petr.n - 1))])
                        kb.op("vector", lambda e, eb=eb, hh=hh,
                              tb=(petr.n - 1) % 2: e.tensor_copy(
                                  sqkT[eb][:, hh * 128:(hh + 1) * 128],
                                  ptr[tb][:, 0:128]),
                              inc=peev,
                              waits=[(petr, petr.n),
                                     (stc2[eb], prev["stc"][eb])])
                    kb.dma("sync", lambda e, t=t, dc=dc, eb=eb, out_d=out_d:
                           e.dma_start(
                               out=out_d.ap()[dc * 512:(dc + 1) * 512,
                                              t * 128:(t + 1) * 128]
                               .rearrange("(hh d) t -> d hh t", d=128),
                               in_=sqkT[eb].ap().rearrange(
                                   "d (hh t) -> d hh t", t=128)),
                           inc=stc2[eb], waits=[(peev, peev.n)])
                ginfo[gidx] = dict(evq=evq.n, petr=petr.n,
                                   stc=[stc2[0].n, stc2[1].n])
                if filler is not None:
                    filler()
            sidx += 1


def emit_attn(kb, st, C, mode, qT_src, kT_src, v_src, masks_d, oT_d,
              negc_col=None, tag="", pre=()):
    nc = kb.nc
    ldq = [kb.ctr() for i in range(2)]
    ldk = [kb.ctr() for i in range(2)]
    ldv = [kb.ctr() for i in range(2)]
    ldm = kb.ctr()
    smm = kb.ctr()
    avz = kb.ctr()
    evs = kb.ctr()
    msk = kb.ctr()
    fdv = kb.ctr()
    stc = [kb.ctr() for i in range(2)]

    skt = [sbuf(nc, st, f"at_kt{i}{tag}", [128, 64 * 128], BF16)
           for i in range(2)]
    svx = [sbuf(nc, st, f"at_vx{i}{tag}", [128, 64 * 128], BF16)
           for i in range(2)]
    sqt = [sbuf(nc, st, f"at_qt{i}{tag}", [128, TLOC], BF16) for i in range(2)]
    spt = [sbuf(nc, st, f"at_pt{i}{tag}", [128, 512], BF16) for i in range(3)]
    sof = [sbuf(nc, st, f"at_of{i}{tag}", [128, 512], F32) for i in range(2)]
    szr = [sbuf(nc, st, f"at_zr{i}{tag}", [128, 512], F32) for i in range(2)]
    sob = [sbuf(nc, st, f"at_ob{i}{tag}", [128, 512], BF16) for i in range(2)]
    smask = sbuf(nc, st, f"at_mk{tag}", [128, 512], BF16)
    pst = [psum(nc, st, f"at_ps{i}{tag}", [128, 512], F32) for i in range(3)]
    po = psum(nc, st, f"at_po{tag}", [128, 512], F32)
    pz = psum(nc, st, f"at_pz{tag}", [128, 512], F32)

    kb.dma("sync", lambda e: e.dma_start(
        out=smask.ap().rearrange("p (r c) -> p r c", c=128),
        in_=masks_d.ap().rearrange("r p c -> p r c")), inc=ldm)
    myldm = ldm.n

    cd = C["c_done"].n
    ev_end = {}
    av_end = {}
    fin_prev = {}
    nst = 0
    for h in range(H):
        hb = h % 2
        guard = [(avz, av_end.get(("head", h - 2), 0))] + list(pre)
        for (ii, (rsel, sap)) in enumerate(kT_src(h)):
            kb.dma("sync", lambda e, rsel=rsel, sap=sap, hb=hb: e.dma_start(
                out=skt[hb][:, rsel * 2048:(rsel + 1) * 2048], in_=sap),
                inc=ldk[hb], waits=guard if ii == 0 else ())
        myk = ldk[hb].n
        for (ii, (rsel, sap)) in enumerate(v_src(h)):
            kb.dma("sync", lambda e, rsel=rsel, sap=sap, hb=hb: e.dma_start(
                out=svx[hb].ap()[:, rsel * 2048:(rsel + 1) * 2048]
                .rearrange("p (jp d) -> p jp d", d=128), in_=sap),
                inc=ldv[hb], waits=guard if ii == 0 else ())
        myv = ldv[hb].n
        kb.dma("sync", lambda e, h=h, hb=hb: e.dma_start(
            out=sqt[hb][:], in_=qT_src(h)), inc=ldq[hb], waits=guard)
        myq = ldq[hb].n

        for G in range(4):
            q0 = 512 * G
            units = [(r_, jp) for r_ in range(4) for jp in range(4 * G + 4)]
            nun = len(units)
            pend = []

            def emit_avz(item, hb=hb):
                (ui, n_, kvslot, s0, width, avwaits) = item
                kb.op("tensor", lambda e, hb=hb, kvslot=kvslot, s0=s0,
                      width=width, ui=ui: e.matmul(
                          po[:, s0 * 128:512],
                          svx[hb][:, kvslot * 128:(kvslot + 1) * 128],
                          spt[n_ % 3][:, 0:width],
                          start=(ui == 0), stop=False),
                      inc=avz, waits=avwaits)
                kb.op("tensor", lambda e, s0=s0, width=width, ui=ui, n_=n_:
                      e.matmul(
                          pz[:, s0 * 128:512],
                          C["ones128"][:],
                          spt[n_ % 3][:, 0:width],
                          start=(ui == 0), stop=False),
                      inc=avz)
                av_end[n_] = avz.n

            for (ui, (r_, jp)) in enumerate(units):
                kvslot = r_ * 16 + jp
                s0 = max(0, jp - 4 * G)
                width = (4 - s0) * 128
                pb = nst % 3
                kb.op("tensor", lambda e, hb=hb, kvslot=kvslot, q0=q0, s0=s0,
                      width=width, pb=pb: e.matmul(
                          pst[pb][:, 0:width],
                          skt[hb][:, kvslot * 128:(kvslot + 1) * 128],
                          sqt[hb][:, q0 + s0 * 128:q0 + 512],
                          start=True, stop=True),
                      inc=smm,
                      waits=[(ldk[hb], myk), (ldq[hb], myq),
                             (evs, ev_end.get(nst - 3, 0)),
                             (C["c_done"], cd)] if ui == 0
                      else [(evs, ev_end.get(nst - 3, 0))])
                mysmm = smm.n
                if len(pend) >= 3:
                    emit_avz(pend.pop(0))
                tb = nst % 3
                ewaits = [(smm, mysmm), (avz, av_end.get(nst - 3, 0))]
                if mode == "sq":
                    kb.op("scalar", lambda e, pb=pb, tb=tb, width=width:
                          e.activation(spt[tb][:, 0:width],
                                       pst[pb][:, 0:width], AF.Square),
                          inc=evs, waits=ewaits)
                else:
                    ncol = negc_col(h)
                    kb.op("scalar", lambda e, pb=pb, tb=tb, width=width,
                          ncol=ncol: e.activation(
                              spt[tb][:, 0:width], pst[pb][:, 0:width],
                              AF.Exp, bias=ncol, scale=C["qkscale"]),
                          inc=evs, waits=ewaits)
                myev = evs.n
                sstar = jp - 4 * G
                if 0 <= sstar <= 3:
                    kb.op("gpsimd", lambda e, tb=tb, sstar=sstar, s0=s0,
                          r_=r_: e.tensor_tensor(
                              spt[tb][:, (sstar - s0) * 128:
                                      (sstar - s0 + 1) * 128],
                              spt[tb][:, (sstar - s0) * 128:
                                      (sstar - s0 + 1) * 128],
                              smask[:, r_ * 128:(r_ + 1) * 128], ALU.mult),
                          inc=msk, waits=[(evs, myev), (ldm, myldm)])
                    avwaits = [(msk, msk.n)]
                else:
                    avwaits = [(evs, myev)]
                if ui == 0:
                    avwaits = avwaits + [(fdv, fin_prev.get("fdv", 0)),
                                         (ldv[hb], myv)]
                pend.append((ui, nst, kvslot, s0, width, avwaits))
                ev_end[nst] = evs.n
                nst += 1
            while pend:
                emit_avz(pend.pop(0))
            kb.op("tensor", lambda e: e.matmul(
                po[:], C["ones128"][:], C["zeros512"][:],
                start=False, stop=True), inc=avz)
            kb.op("tensor", lambda e: e.matmul(
                pz[:], C["ones128"][:], C["zeros512"][:],
                start=False, stop=True), inc=avz)
            av_end[("head", h)] = avz.n
            fb = (h * 4 + G) % 2
            myavz = avz.n
            if mode == "sq":
                kb.op("scalar", lambda e, fb=fb: e.activation(
                    szr[fb][:], pz[:], AF.Ln, bias=C["eps"]),
                    inc=evs, waits=[(avz, myavz), (fdv, fin_prev.get(fb, 0))])
            else:
                kb.op("scalar", lambda e, fb=fb: e.activation(
                    szr[fb][:], pz[:], AF.Ln),
                    inc=evs, waits=[(avz, myavz), (fdv, fin_prev.get(fb, 0))])
            kb.op("scalar", lambda e, fb=fb: e.activation(
                szr[fb][:], szr[fb][:], AF.Exp, scale=C["neg1"]),
                inc=evs, waits=[(evs, evs.n)])
            kb.op("scalar", lambda e, fb=fb: e.activation(
                sof[fb][:], po[:], AF.Identity),
                inc=evs, waits=[(evs, evs.n)])
            myfin = evs.n
            kb.op("gpsimd", lambda e, fb=fb: e.tensor_tensor(
                sob[fb][:], sof[fb][:], szr[fb][:], ALU.mult),
                inc=fdv, waits=[(evs, myfin), (stc[fb], stc[fb].n)])
            fin_prev[fb] = fdv.n
            fin_prev["fdv"] = fdv.n
            kb.dma("sync", lambda e, h=h, G=G, fb=fb: e.dma_start(
                out=oT_d[h * 128:(h + 1) * 128, 512 * G:512 * G + 512],
                in_=sob[fb][:]), inc=stc[fb], waits=[(fdv, fdv.n)])
    return dict(stc=stc)


def build_attn0():
    nc = bass.Bass("TRN2", target_bir_lowering=False)
    qT_d = nc.dram_tensor("qT", [D, TLOC], BF16, kind="ExternalInput")
    ktb = nc.dram_tensor("ktb", [D, TGLOB], BF16, kind="ExternalInput")
    vb = nc.dram_tensor("vb", [TGLOB, D], BF16, kind="ExternalInput")
    masks_d = nc.dram_tensor("masks", [4, 128, 128], BF16,
                             kind="ExternalInput")
    oT_d = nc.dram_tensor("oT", [D, TLOC], BF16, kind="ExternalOutput")
    st = ExitStack()
    with st:
        kb = KB(nc, st)
        C = emit_consts(kb, st, need_ident=False)

        def qsrc(h):
            return qT_d[h * 128:(h + 1) * 128, :]

        def ksrc(h):
            return [(r_, ktb.ap()[h * 128:(h + 1) * 128,
                                  r_ * 2048:(r_ + 1) * 2048])
                    for r_ in range(4)]

        def vsrc(h):
            return [(r_, vb.ap()[r_ * 2048:(r_ + 1) * 2048,
                                 h * 128:(h + 1) * 128]
                     .rearrange("(jp tt) d -> tt jp d", tt=128))
                    for r_ in range(4)]

        emit_attn(kb, st, C, "sq", qsrc, ksrc, vsrc, masks_d, oT_d)
        kb.barrier()
        block = st.enter_context(nc.Block())
        kb.finalize(block)
    return nc


def emit_ffn0(kb, st, C, xnT_d, wgT_d, wuT_d, wd_view, ffn0_d):
    nc = kb.nc
    ldx = [kb.ctr(f"ff_ldx{i}") for i in range(2)]
    ldg = [kb.ctr(f"ff_ldg{i}") for i in range(2)]
    ldu = [kb.ctr(f"ff_ldu{i}") for i in range(2)]
    ldd = [kb.ctr(f"ff_ldd{i}") for i in range(2)]
    mm1 = kb.ctr("ff_mm1")
    ev1a = kb.ctr("ff_ev1a")
    ev1v = kb.ctr("ff_ev1v")
    mm2 = kb.ctr("ff_mm2")
    ev2 = kb.ctr("ff_ev2")
    stc = [kb.ctr(f"ff_st{i}") for i in range(2)]

    sxn = sbuf(nc, st, "ff_xn", [128, 16 * 1024], BF16)
    swg = [sbuf(nc, st, f"ff_wg{i}", [128, 16 * 128], BF16) for i in range(2)]
    swu = [sbuf(nc, st, f"ff_wu{i}", [128, 16 * 128], BF16) for i in range(2)]
    swd = [sbuf(nc, st, "ff_wd0", [128, NI * 512], BF16)]
    sh3 = sbuf(nc, st, "ff_h3", [128, NI * 1024], BF16)
    ssg = [sbuf(nc, st, f"ff_sg{i}", [128, 512], F32) for i in range(2)]
    sev = [sbuf(nc, st, f"ff_ev{i}", [128, 512], F32) for i in range(2)]
    pg = [psum(nc, st, f"ff_pg{i}", [128, 512], F32) for i in range(2)]
    pu = [psum(nc, st, f"ff_pu{i}", [128, 512], F32) for i in range(2)]
    pd = [psum(nc, st, f"ff_pd{i}", [128, 512], F32) for i in range(2)]

    h3_end = {}
    for half in range(2):
        hb = half % 2
        kb.dma("sync", lambda e, half=half: e.dma_start(
            out=sxn.ap().rearrange("p (cc t) -> p cc t", t=1024),
            in_=xnT_d.ap()[:, half * 1024:(half + 1) * 1024]
            .rearrange("(cc p) t -> p cc t", p=128)),
            inc=ldx[hb], waits=[(mm1, mm1.n)])
        myx = ldx[hb].n
        for i in range(NI):
            ib = i % 2
            kb.dma("sync", lambda e, i=i, ib=ib: e.dma_start(
                out=swg[ib].ap().rearrange("p (cc k) -> p cc k", k=128),
                in_=wgT_d.ap()[:, i * 128:(i + 1) * 128]
                .rearrange("(cc p) k -> p cc k", p=128)),
                inc=ldg[ib], waits=[(mm1, max(0, mm1.n - 2))])
            myg = ldg[ib].n
            kb.dma("gpsimd", lambda e, i=i, ib=ib: e.dma_start(
                out=swu[ib].ap().rearrange("p (cc k) -> p cc k", k=128),
                in_=wuT_d.ap()[:, i * 128:(i + 1) * 128]
                .rearrange("(cc p) k -> p cc k", p=128)),
                inc=ldu[ib], waits=[(mm1, max(0, mm1.n - 2))])
            myu = ldu[ib].n
            for tq in range(2):
                pb = (i * 2 + tq) % 2
                gi = half * NI * 2 + i * 2 + tq
                for cc in range(16):
                    kb.op("tensor", lambda e, i=i, ib=ib, tq=tq, cc=cc, pb=pb:
                          e.matmul(pg[pb][:],
                                   swg[ib][:, cc * 128:(cc + 1) * 128],
                                   sxn[:, cc * 1024 + tq * 512:
                                       cc * 1024 + tq * 512 + 512],
                                   start=(cc == 0), stop=(cc == 15)),
                          waits=([(ldg[ib], myg), (ldx[hb], myx),
                                  (ev1v, max(0, ev1v.n - 1))]
                                 if cc == 0 else ()))
                for cc in range(16):
                    kb.op("tensor", lambda e, i=i, ib=ib, tq=tq, cc=cc, pb=pb:
                          e.matmul(pu[pb][:],
                                   swu[ib][:, cc * 128:(cc + 1) * 128],
                                   sxn[:, cc * 1024 + tq * 512:
                                       cc * 1024 + tq * 512 + 512],
                                   start=(cc == 0), stop=(cc == 15)),
                          inc=(mm1 if cc == 15 else None),
                          waits=[(ldu[ib], myu)] if cc == 0 else ())
                mymm = mm1.n
                eb = gi % 2
                kb.op("scalar", lambda e, pb=pb, eb=eb: e.activation(
                    ssg[eb][:], pg[pb][:], AF.Silu),
                    inc=ev1a, waits=[(mm1, mymm),
                                     (ev1v, max(0, ev1v.n - 1))])
                kb.op("vector", lambda e, i=i, tq=tq, pb=pb, eb=eb:
                      e.tensor_tensor(
                          sh3[:, i * 1024 + tq * 512:i * 1024 + tq * 512 + 512],
                          ssg[eb][:], pu[pb][:], ALU.mult),
                      inc=ev1v, waits=[(ev1a, ev1a.n), (mm2, mm2.n)])
        h3_done = ev1v.n
        for e4 in range(4):
            db = 0
            for rr in range(2):
                kb.dma("gpsimd", lambda e, e4=e4, rr=rr: e.dma_start(
                    out=swd[0].ap().rearrange("p (i r k) -> p i r k",
                                              r=2, k=256)[:, :, rr, :],
                    in_=wd_view[:, :, 2 * e4 + rr, :]),
                    inc=ldd[0], waits=[(mm2, mm2.n)] if rr == 0 else ())
            myd = ldd[0].n
            for tsub in range(8):
                gi2 = (half * 4 + e4) * 8 + tsub
                pb = gi2 % 2
                for i in range(NI):
                    kb.op("tensor", lambda e, i=i, tsub=tsub, db=db, pb=pb:
                          e.matmul(pd[pb][:],
                                   sh3[:, i * 1024 + tsub * 128:
                                       i * 1024 + (tsub + 1) * 128],
                                   swd[db][:, i * 512:(i + 1) * 512],
                                   start=(i == 0), stop=(i == NI - 1)),
                          inc=(mm2 if i == NI - 1 else None),
                          waits=([(ldd[db], myd), (ev1v, h3_done),
                                  (ev2, max(0, ev2.n - 1))]
                                 if i == 0 else ()))
                eb = gi2 % 2
                kb.op("scalar", lambda e, pb=pb, eb=eb: e.activation(
                    sev[eb][:], pd[pb][:], AF.Identity),
                    inc=ev2, waits=[(mm2, mm2.n), (stc[eb], stc[eb].n)])
                kb.dma("sync", lambda e, half=half, e4=e4, tsub=tsub, eb=eb:
                       e.dma_start(
                           out=ffn0_d[half * 1024 + tsub * 128:
                                      half * 1024 + (tsub + 1) * 128,
                                      e4 * 512:(e4 + 1) * 512],
                           in_=sev[eb][:]),
                       inc=stc[eb], waits=[(ev2, ev2.n)])
    return dict(stc=stc)


def emit_combine0(kb, st, C, x_rows, oT_d, woT_d, ffn0_d, x1_d):
    nc = kb.nc
    ldo = kb.ctr("cb_ldo")
    ldw = [kb.ctr(f"cb_ldw{i}") for i in range(2)]
    ldf = [kb.ctr(f"cb_ldf{i}") for i in range(2)]
    ldxr = [kb.ctr(f"cb_ldx{i}") for i in range(2)]
    mm = kb.ctr("cb_mm")
    ev = kb.ctr("cb_ev")
    dvv = kb.ctr("cb_dv")
    stc = [kb.ctr(f"cb_st{i}") for i in range(2)]

    soT = sbuf(nc, st, "cb_oT", [128, 16 * D], BF16)
    swo = [sbuf(nc, st, f"cb_wo{i}", [128, 16 * 512], BF16) for i in range(2)]
    sf = [sbuf(nc, st, f"cb_f{i}", [128, 512], F32) for i in range(2)]
    sxr = [sbuf(nc, st, f"cb_x{i}", [128, 512], F32) for i in range(2)]
    sac = [sbuf(nc, st, f"cb_ac{i}", [128, 512], F32) for i in range(2)]
    pp = [psum(nc, st, f"cb_p{i}", [128, 512], F32) for i in range(2)]

    kb.dma("sync", lambda e: e.dma_start(
        out=soT.ap().rearrange("p (dd t) -> p dd t", t=TLOC),
        in_=oT_d.ap().rearrange("(dd p) t -> p dd t", p=128)), inc=ldo)
    myo = ldo.n

    for e4 in range(4):
        wb = e4 % 2
        kb.dma("sync", lambda e, e4=e4, wb=wb: e.dma_start(
            out=swo[wb].ap().rearrange("p (dd k) -> p dd k", k=512),
            in_=woT_d.ap()[:, e4 * 512:(e4 + 1) * 512]
            .rearrange("(dd p) k -> p dd k", p=128)),
            inc=ldw[wb], waits=[(mm, max(0, mm.n - 2))])
        myw = ldw[wb].n
        for t in range(NT):
            gi = e4 * NT + t
            pb = gi % 2
            eb = gi % 2
            kb.dma("sync", lambda e, e4=e4, t=t, eb=eb: e.dma_start(
                out=sf[eb][:],
                in_=ffn0_d[t * 128:(t + 1) * 128, e4 * 512:(e4 + 1) * 512]),
                inc=ldf[eb], waits=[(dvv, max(0, dvv.n - 3))])
            myf = ldf[eb].n
            kb.dma("sync", lambda e, e4=e4, t=t, eb=eb: e.dma_start(
                out=sxr[eb][:],
                in_=x_rows[t * 128:(t + 1) * 128, e4 * 512:(e4 + 1) * 512]),
                inc=ldxr[eb], waits=[(dvv, max(0, dvv.n - 2))])
            myxr = ldxr[eb].n
            for dd in range(16):
                kb.op("tensor", lambda e, dd=dd, t=t, wb=wb, pb=pb:
                      e.matmul(pp[pb][:],
                               soT[:, dd * TLOC + t * 128:
                                   dd * TLOC + (t + 1) * 128],
                               swo[wb][:, dd * 512:(dd + 1) * 512],
                               start=(dd == 0), stop=(dd == 15)),
                      inc=(mm if dd == 15 else None),
                      waits=([(ldw[wb], myw), (ldo, myo),
                              (ev, max(0, ev.n - 1))]
                             if dd == 0 else ()))
            kb.op("scalar", lambda e, pb=pb, eb=eb: e.activation(
                sac[eb][:], pp[pb][:], AF.Identity),
                inc=ev, waits=[(mm, mm.n), (dvv, max(0, dvv.n - 2)),
                               (stc[eb], stc[eb].n)])
            kb.op("vector", lambda e, eb=eb: e.tensor_tensor(
                sac[eb][:], sac[eb][:], sf[eb][:], ALU.add),
                inc=dvv, waits=[(ev, ev.n), (ldf[eb], myf)])
            kb.op("vector", lambda e, eb=eb: e.tensor_tensor(
                sac[eb][:], sac[eb][:], sxr[eb][:], ALU.add),
                inc=dvv, waits=[(dvv, dvv.n), (ldxr[eb], myxr)])
            kb.dma("sync", lambda e, e4=e4, t=t, eb=eb: e.dma_start(
                out=x1_d[t * 128:(t + 1) * 128, e4 * 512:(e4 + 1) * 512],
                in_=sac[eb][:]), inc=stc[eb], waits=[(dvv, dvv.n)])
    return dict(stc=stc)


def build_ffn0():
    nc = bass.Bass("TRN2", target_bir_lowering=False)
    xnT_d = nc.dram_tensor("xnT", [D, TLOC], BF16, kind="ExternalInput")
    wgT_d = nc.dram_tensor("wgT", [D, INTER], BF16, kind="ExternalInput")
    wuT_d = nc.dram_tensor("wuT", [D, INTER], BF16, kind="ExternalInput")
    wdT_d = nc.dram_tensor("wdT", [INTER, D], BF16, kind="ExternalInput")
    ffn0_d = nc.dram_tensor("ffn0", [TLOC, D], F32, kind="ExternalOutput")
    st = ExitStack()
    with st:
        kb = KB(nc, st)
        C = emit_consts(kb, st, need_ident=False)
        wd_view = wdT_d.ap().rearrange("(i p) (r k) -> p i r k",
                                       p=128, k=256)
        emit_ffn0(kb, st, C, xnT_d, wgT_d, wuT_d, wd_view, ffn0_d)
        kb.barrier()
        block = st.enter_context(nc.Block())
        kb.finalize(block)
    return nc


def build_combine0():
    nc = bass.Bass("TRN2", target_bir_lowering=False)
    x_rows = nc.dram_tensor("x_rows", [TLOC, D], F32, kind="ExternalInput")
    oT_d = nc.dram_tensor("oT", [D, TLOC], BF16, kind="ExternalInput")
    woT_d = nc.dram_tensor("woT", [D, D], F32, kind="ExternalInput")
    ffn0_d = nc.dram_tensor("ffn0", [TLOC, D], F32, kind="ExternalInput")
    x1_d = nc.dram_tensor("x1", [TLOC, D], F32, kind="ExternalOutput")
    xn1T_d = nc.dram_tensor("xn1T", [D, TLOC], BF16, kind="ExternalOutput")
    st = ExitStack()
    with st:
        kb = KB(nc, st)
        C = emit_consts(kb, st)
        r = emit_combine0(kb, st, C, x_rows, oT_d, woT_d, ffn0_d, x1_d)
        kb.wait("sync", r["stc"][0], r["stc"][0].n)
        kb.wait("sync", r["stc"][1], r["stc"][1].n)
        emit_ln_xnt(kb, st, x1_d, xn1T_d, C, tag="1")
        kb.barrier()
        block = st.enter_context(nc.Block())
        kb.finalize(block)
    return nc


def emit_qkv1(kb, st, C, xn1T_d, w_attn_d, q1T_d, k1T_d, v1_d, qn_d, kn_d):
    nc = kb.nc
    ldx = kb.ctr("q1_ldx")
    ldw = [kb.ctr(f"q1_ldw{i}") for i in range(2)]
    mm = kb.ctr("q1_mm")
    nmm = kb.ctr("q1_nmm")
    ev = kb.ctr("q1_ev")
    evs = kb.ctr("q1_evs")
    ndv = kb.ctr("q1_ndv")
    stc = [kb.ctr(f"q1_st{i}") for i in range(2)]
    stn = kb.ctr("q1_stn")

    sxn = sbuf(nc, st, "q1_xn", [128, NT * D], BF16)
    slab = [sbuf(nc, st, f"q1_sl{i}", [128, 16 * 512], BF16) for i in range(2)]
    sq = [sbuf(nc, st, f"q1_q{i}", [128, 512], BF16) for i in range(2)]
    ssq = [sbuf(nc, st, f"q1_sq{i}", [128, 512], BF16) for i in range(2)]
    srow = sbuf(nc, st, "q1_row", [1, TLOC], F32)
    pq = [psum(nc, st, f"q1_p{i}", [128, 512], F32) for i in range(2)]
    pn = [psum(nc, st, f"q1_pn{i}", [128, 512], F32) for i in range(2)]

    kb.dma("sync", lambda e: e.dma_start(
        out=sxn.ap().rearrange("p (cc t) -> p cc t", t=TLOC),
        in_=xn1T_d.ap().rearrange("(cc p) t -> p cc t", p=128)), inc=ldx)
    myx = ldx.n
    cd = C["c_done"].n

    for (wi, out_d, n_d) in ((0, q1T_d, qn_d), (1, k1T_d, kn_d)):
        for dd in range(16):
            sb = (wi * 16 + dd) % 2
            kb.dma("sync", lambda e, wi=wi, dd=dd, sb=sb: e.dma_start(
                out=slab[sb].ap()[:, 0:16 * 128]
                .rearrange("p (cc k) -> p cc k", k=128),
                in_=w_attn_d.ap()[:, wi * D + dd * 128:wi * D + (dd + 1) * 128]
                .rearrange("(cc p) k -> p cc k", p=128)),
                inc=ldw[sb], waits=[(mm, max(0, mm.n - 4))])
            myw = ldw[sb].n
            for t4 in range(4):
                gi = (wi * 16 + dd) * 4 + t4
                pb = gi % 2
                eb = gi % 2
                for cc in range(16):
                    kb.op("tensor", lambda e, sb=sb, t4=t4, cc=cc, pb=pb:
                          e.matmul(pq[pb][:],
                                   slab[sb][:, cc * 128:(cc + 1) * 128],
                                   sxn[:, cc * TLOC + t4 * 512:
                                       cc * TLOC + (t4 + 1) * 512],
                                   start=(cc == 0), stop=(cc == 15)),
                          inc=(mm if cc == 15 else None),
                          waits=([(ldw[sb], myw), (ldx, myx),
                                  (evs, max(0, evs.n - 1)),
                                  (C["c_done"], cd)]
                                 if cc == 0 else ()))
                mymm = mm.n
                kb.op("scalar", lambda e, pb=pb, eb=eb: e.activation(
                    sq[eb][:], pq[pb][:], AF.Identity),
                    inc=ev, waits=[(mm, mymm), (stc[eb], stc[eb].n)])
                kb.dma("sync", lambda e, dd=dd, t4=t4, eb=eb, out_d=out_d:
                       e.dma_start(
                           out=out_d[dd * 128:(dd + 1) * 128,
                                     t4 * 512:(t4 + 1) * 512],
                           in_=sq[eb][:]), inc=stc[eb], waits=[(ev, ev.n)])
                kb.op("scalar", lambda e, pb=pb, eb=eb: e.activation(
                    ssq[eb][:], pq[pb][:], AF.Square),
                    inc=evs, waits=[(ev, ev.n), (nmm, max(0, nmm.n - 1))])
                kb.op("tensor", lambda e, eb=eb, pb=pb: e.matmul(
                    pn[pb][:], C["ones128"][:], ssq[eb][:],
                    start=True, stop=True),
                    inc=nmm, waits=[(evs, evs.n), (ndv, max(0, ndv.n - 1))])
                kb.op("vector", lambda e, t4=t4, pb=pb: e.tensor_copy(
                    srow[:, t4 * 512:(t4 + 1) * 512], pn[pb][0:1, :]),
                    inc=ndv, waits=[(nmm, nmm.n), (stn, stn.n)])
            kb.dma("gpsimd", lambda e, dd=dd, n_d=n_d: e.dma_start(
                out=n_d[dd:dd + 1, :], in_=srow[:]),
                inc=stn, waits=[(ndv, ndv.n)])

    for dc in range(4):
        sb = dc % 2
        kb.dma("sync", lambda e, dc=dc, sb=sb: e.dma_start(
            out=slab[sb].ap().rearrange("p (cc k) -> p cc k", k=512),
            in_=w_attn_d.ap()[:, 2 * D + dc * 512:2 * D + (dc + 1) * 512]
            .rearrange("(cc p) k -> p cc k", p=128)),
            inc=ldw[sb], waits=[(mm, mm.n if dc < 2 else mm.n - 16)])
        myw = ldw[sb].n
        for t in range(NT):
            gi = 128 + dc * NT + t
            pb = gi % 2
            eb = gi % 2
            for cc in range(16):
                kb.op("tensor", lambda e, sb=sb, t=t, cc=cc, pb=pb:
                      e.matmul(pq[pb][:],
                               sxn[:, cc * TLOC + t * 128:
                                   cc * TLOC + (t + 1) * 128],
                               slab[sb][:, cc * 512:(cc + 1) * 512],
                               start=(cc == 0), stop=(cc == 15)),
                      inc=(mm if cc == 15 else None),
                      waits=([(ldw[sb], myw), (ev, max(0, ev.n - 1))]
                             if cc == 0 else ()))
            kb.op("scalar", lambda e, pb=pb, eb=eb: e.activation(
                sq[eb][:], pq[pb][:], AF.Identity),
                inc=ev, waits=[(mm, mm.n), (stc[eb], stc[eb].n)])
            kb.dma("sync", lambda e, t=t, dc=dc, eb=eb: e.dma_start(
                out=v1_d[t * 128:(t + 1) * 128, dc * 512:(dc + 1) * 512],
                in_=sq[eb][:]), inc=stc[eb], waits=[(ev, ev.n)])
    return dict(stc=stc)


def emit_negc(kb, st, C, qn_d, knb_src, knb_eng="sync", tag="1"):
    nc = kb.nc
    ld = kb.ctr(f"nc_ld{tag}")
    dv = kb.ctr(f"nc_dv{tag}")
    ac = kb.ctr(f"nc_ac{tag}")
    pe = kb.ctr(f"nc_pe{tag}")

    sqn = sbuf(nc, st, f"nc_qn{tag}", [16, TLOC], F32)
    skn = sbuf(nc, st, f"nc_kn{tag}", [16, TGLOB], F32)
    sm = sbuf(nc, st, f"nc_m{tag}", [16, 4], F32)
    snegb = sbuf(nc, st, f"nc_nb{tag}", [16, 1], BF16)
    srowb = sbuf(nc, st, f"nc_rb{tag}", [1, 16], BF16)
    snegc = sbuf(nc, st, f"nc_c{tag}", [128, 16], F32)
    pnc = psum(nc, st, f"nc_p{tag}", [128, 512], F32)

    ld2_ = kb.ctr(f"nc_ld2{tag}")
    kb.dma("sync", lambda e: e.dma_start(out=sqn[:], in_=qn_d[:, :]), inc=ld)
    for (ii, (rsel, sap)) in enumerate(knb_src):
        if rsel is None:
            dst = skn[:]
        else:
            dst = skn[:, rsel * 2048:(rsel + 1) * 2048]
        kb.dma(knb_eng, lambda e, dst=dst, sap=sap: e.dma_start(
            out=dst, in_=sap), inc=ld2_)
    myld = ld.n
    myld2 = ld2_.n
    kb.op("vector", lambda e: e.reduce_max(sm[:, 0:1], sqn[:], axis=AX.X),
          inc=dv, waits=[(ld, myld)])
    kb.op("vector", lambda e: e.reduce_max(sm[:, 1:2], skn[:], axis=AX.X),
          inc=dv, waits=[(dv, dv.n), (ld2_, myld2)])
    kb.op("vector", lambda e: e.tensor_tensor(
        sm[:, 2:3], sm[:, 0:1], sm[:, 1:2], ALU.mult),
        inc=dv, waits=[(dv, dv.n)])
    kb.op("scalar", lambda e: e.activation(
        sm[:, 3:4], sm[:, 2:3], AF.Ln), inc=ac, waits=[(dv, dv.n)])
    kb.op("scalar", lambda e: e.activation(
        sm[:, 3:4], sm[:, 3:4], AF.Exp, scale=C["half16"],
        bias=C["lnqscale16"]), inc=ac, waits=[(ac, ac.n)])
    kb.op("vector", lambda e: e.tensor_scalar(
        snegb[:], sm[:, 3:4], -1.0, None, ALU.mult),
        inc=dv, waits=[(ac, ac.n)])
    kb.op("tensor", lambda e: e.matmul(
        pnc[0:1, 0:16], snegb[:], C["ident"][0:16, 0:16],
        start=True, stop=True),
        inc=pe, waits=[(dv, dv.n), (C["c_done"], C["c_done"].n)])
    kb.op("vector", lambda e: e.tensor_copy(srowb[:], pnc[0:1, 0:16]),
          inc=dv, waits=[(pe, pe.n)])
    kb.op("tensor", lambda e: e.matmul(
        pnc[:, 0:16], C["ones_row"][:], srowb[:], start=True, stop=True),
        inc=pe, waits=[(dv, dv.n)])
    kb.op("vector", lambda e: e.tensor_copy(snegc[:], pnc[:, 0:16]),
          inc=dv, waits=[(pe, pe.n)])
    fin = dv.n
    return snegc, dv, fin


def build_qkv1():
    nc = bass.Bass("TRN2", target_bir_lowering=False)
    xn1T_d = nc.dram_tensor("xn1T", [D, TLOC], BF16, kind="ExternalInput")
    w_attn_d = nc.dram_tensor("w_attn", [D, 3 * D], F32, kind="ExternalInput")
    q1T_d = nc.dram_tensor("q1T", [D, TLOC], BF16, kind="ExternalOutput")
    k1T_d = nc.dram_tensor("k1T", [D, TLOC], BF16, kind="ExternalOutput")
    v1_d = nc.dram_tensor("v1", [TLOC, D], BF16, kind="ExternalOutput")
    qn_d = nc.dram_tensor("qn", [16, TLOC], F32, kind="ExternalOutput")
    kn_d = nc.dram_tensor("kn", [16, TLOC], F32, kind="ExternalOutput")
    st = ExitStack()
    with st:
        kb = KB(nc, st)
        C = emit_consts(kb, st, need_ident=False)
        emit_qkv1(kb, st, C, xn1T_d, w_attn_d, q1T_d, k1T_d, v1_d, qn_d, kn_d)
        kb.barrier()
        block = st.enter_context(nc.Block())
        kb.finalize(block)
    return nc


def build_attn1():
    nc = bass.Bass("TRN2", target_bir_lowering=False)
    qT_d = nc.dram_tensor("q1T", [D, TLOC], BF16, kind="ExternalInput")
    ktb = nc.dram_tensor("k1tb", [D, TGLOB], BF16, kind="ExternalInput")
    vb = nc.dram_tensor("v1b", [TGLOB, D], BF16, kind="ExternalInput")
    masks_d = nc.dram_tensor("masks", [4, 128, 128], BF16,
                             kind="ExternalInput")
    qn_d = nc.dram_tensor("qn", [16, TLOC], F32, kind="ExternalInput")
    knb_d = nc.dram_tensor("knb", [16, TGLOB], F32, kind="ExternalInput")
    oT_d = nc.dram_tensor("o1T", [D, TLOC], BF16, kind="ExternalOutput")
    st = ExitStack()
    with st:
        kb = KB(nc, st)
        C = emit_consts(kb, st, need_ident=True)
        snegc, ncdv, ncfin = emit_negc(kb, st, C, qn_d,
                                       [(None, knb_d.ap())])
        kb.barrier()

        def qsrc(h):
            return qT_d[h * 128:(h + 1) * 128, :]

        def ksrc(h):
            return [(r_, ktb.ap()[h * 128:(h + 1) * 128,
                                  r_ * 2048:(r_ + 1) * 2048])
                    for r_ in range(4)]

        def vsrc(h):
            return [(r_, vb.ap()[r_ * 2048:(r_ + 1) * 2048,
                                 h * 128:(h + 1) * 128]
                     .rearrange("(jp tt) d -> tt jp d", tt=128))
                    for r_ in range(4)]

        emit_attn(kb, st, C, "sm", qsrc, ksrc, vsrc, masks_d, oT_d,
                  negc_col=lambda h: snegc[:, h:h + 1])
        kb.barrier()
        block = st.enter_context(nc.Block())
        kb.finalize(block)
    return nc


def emit_moelocal(kb, st, C, x1_rows, o1T_d, w_proj_d, w1s_d, w3s_d, w2s_d,
                  gwT_d, gbbc_d, xn1T_d, ybase_d, probs_d):
    nc = kb.nc
    ldx = kb.ctr("ml_ldx")
    ld1 = [kb.ctr(f"ml_ld1{i}") for i in range(2)]
    ld3 = [kb.ctr(f"ml_ld3{i}") for i in range(2)]
    ld2 = kb.ctr("ml_ld2")
    ldp = [kb.ctr(f"ml_ldp{i}") for i in range(2)]
    ldo = [kb.ctr(f"ml_ldo{i}") for i in range(2)]
    ldr = [kb.ctr(f"ml_ldr{i}") for i in range(2)]
    ldg = kb.ctr("ml_ldg")
    ldgb = kb.ctr("ml_ldgb")
    mm1 = kb.ctr("ml_mm1")
    ev1a = kb.ctr("ml_ev1a")
    ev1v = kb.ctr("ml_ev1v")
    mm2 = kb.ctr("ml_mm2")
    ev2 = kb.ctr("ml_ev2")
    dv2 = kb.ctr("ml_dv2")
    stc = [kb.ctr(f"ml_st{i}") for i in range(2)]
    rmm = kb.ctr("ml_rmm")
    rac = kb.ctr("ml_rac")
    rdv = kb.ctr("ml_rdv")
    stp = kb.ctr("ml_stp")

    sxn = sbuf(nc, st, "ml_xn", [128, NT * D], BF16)
    sw1 = [sbuf(nc, st, f"ml_w1{i}", [128, 16 * 128], BF16) for i in range(2)]
    sw3 = [sbuf(nc, st, f"ml_w3{i}", [128, 16 * 128], BF16) for i in range(2)]
    sh = sbuf(nc, st, "ml_h", [128, NMH * D], BF16)
    ssg = [sbuf(nc, st, f"ml_sg{i}", [128, 512], F32) for i in range(2)]
    sw2 = sbuf(nc, st, "ml_w2", [128, NMH * 512], BF16)
    swp = [sbuf(nc, st, f"ml_wp{i}", [128, 16 * 512], BF16) for i in range(2)]
    soc = [sbuf(nc, st, f"ml_oc{i}", [128, 16 * 128], BF16) for i in range(2)]
    sxr = [sbuf(nc, st, f"ml_xr{i}", [128, 512], F32) for i in range(2)]
    sac = [sbuf(nc, st, f"ml_ac{i}", [128, 512], F32) for i in range(2)]
    sgw = sbuf(nc, st, "ml_gw", [128, 16 * 16], BF16)
    sgb = sbuf(nc, st, "ml_gb", [128, 16], F32)
    ssc = sbuf(nc, st, "ml_sc", [128, 16], F32)
    smk = sbuf(nc, st, "ml_mk", [128, 16], F32)
    sprob = sbuf(nc, st, "ml_pr", [128, 16], F32)
    scols = sbuf(nc, st, "ml_co", [128, 8], F32)
    pg = [psum(nc, st, f"ml_pg{i}", [128, 512], F32) for i in range(2)]
    pu = [psum(nc, st, f"ml_pu{i}", [128, 512], F32) for i in range(2)]
    pd = [psum(nc, st, f"ml_pd{i}", [128, 512], F32) for i in range(2)]
    plog = psum(nc, st, "ml_pl", [128, 512], F32)

    kb.dma("sync", lambda e: e.dma_start(
        out=sxn.ap().rearrange("p (cc t) -> p cc t", t=TLOC),
        in_=xn1T_d.ap().rearrange("(cc p) t -> p cc t", p=128)), inc=ldx)
    myx = ldx.n
    kb.dma("sync", lambda e: e.dma_start(
        out=sgw.ap().rearrange("p (cc k) -> p cc k", k=16),
        in_=gwT_d.ap().rearrange("(cc p) k -> p cc k", p=128)), inc=ldg)
    kb.dma("sync", lambda e: e.dma_start(out=sgb[:], in_=gbbc_d[:, :]),
           inc=ldgb)
    myg = ldg.n
    mygb = ldgb.n
    cd = C["c_done"].n

    for i in range(NMH):
        ib = i % 2
        kb.dma("sync", lambda e, i=i, ib=ib: e.dma_start(
            out=sw1[ib].ap().rearrange("p (cc k) -> p cc k", k=128),
            in_=w1s_d.ap()[:, i * 128:(i + 1) * 128]
            .rearrange("(cc p) k -> p cc k", p=128)),
            inc=ld1[ib], waits=[(mm1, max(0, mm1.n - 2))])
        my1 = ld1[ib].n
        kb.dma("sync", lambda e, i=i, ib=ib: e.dma_start(
            out=sw3[ib].ap().rearrange("p (cc k) -> p cc k", k=128),
            in_=w3s_d.ap()[:, i * 128:(i + 1) * 128]
            .rearrange("(cc p) k -> p cc k", p=128)),
            inc=ld3[ib], waits=[(mm1, max(0, mm1.n - 2))])
        my3 = ld3[ib].n
        for tq in range(4):
            gi = i * 4 + tq
            pb = gi % 2
            eb = gi % 2
            for cc in range(16):
                kb.op("tensor", lambda e, ib=ib, tq=tq, cc=cc, pb=pb:
                      e.matmul(pg[pb][:],
                               sw1[ib][:, cc * 128:(cc + 1) * 128],
                               sxn[:, cc * TLOC + tq * 512:
                                   cc * TLOC + (tq + 1) * 512],
                               start=(cc == 0), stop=(cc == 15)),
                      waits=([(ld1[ib], my1), (ldx, myx),
                              (ev1v, max(0, ev1v.n - 1))]
                             if cc == 0 else ()))
            for cc in range(16):
                kb.op("tensor", lambda e, ib=ib, tq=tq, cc=cc, pb=pb:
                      e.matmul(pu[pb][:],
                               sw3[ib][:, cc * 128:(cc + 1) * 128],
                               sxn[:, cc * TLOC + tq * 512:
                                   cc * TLOC + (tq + 1) * 512],
                               start=(cc == 0), stop=(cc == 15)),
                      inc=(mm1 if cc == 15 else None),
                      waits=[(ld3[ib], my3)] if cc == 0 else ())
            kb.op("scalar", lambda e, pb=pb, eb=eb: e.activation(
                ssg[eb][:], pg[pb][:], AF.Silu),
                inc=ev1a, waits=[(mm1, mm1.n), (ev1v, max(0, ev1v.n - 1))])
            kb.op("vector", lambda e, i=i, tq=tq, pb=pb, eb=eb:
                  e.tensor_tensor(
                      sh[:, i * D + tq * 512:i * D + (tq + 1) * 512],
                      ssg[eb][:], pu[pb][:], ALU.mult),
                  inc=ev1v, waits=[(ev1a, ev1a.n), (mm2, mm2.n)])
    h_done = ev1v.n

    for t in range(NT):
        for cc in range(16):
            kb.op("tensor", lambda e, t=t, cc=cc: e.matmul(
                plog[:, 0:16],
                sxn[:, cc * TLOC + t * 128:cc * TLOC + (t + 1) * 128],
                sgw[:, cc * 16:(cc + 1) * 16],
                start=(cc == 0), stop=(cc == 15)),
                inc=(rmm if cc == 15 else None),
                waits=([(ldg, myg), (ldx, myx), (rac, rac.n)]
                       if cc == 0 else ()))
        kb.op("scalar", lambda e: e.activation(ssc[:], plog[:, 0:16],
                                               AF.Sigmoid),
              inc=rac, waits=[(rmm, rmm.n), (rdv, rdv.n)])
        kb.op("vector", lambda e: e.tensor_tensor(ssc[:], ssc[:], sgb[:],
                                                  ALU.add),
              inc=rdv, waits=[(rac, rac.n), (ldgb, mygb)])
        kb.op("vector", lambda e: e.memset(sprob[:], 0.0), inc=rdv,
              waits=[(stp, stp.n)])
        kb.op("vector", lambda e: e.memset(scols[:, 0:1], 0.0), inc=rdv)
        for kk in range(4):
            kb.op("vector", lambda e: e.reduce_max(
                scols[:, 1:2], ssc[:], axis=AX.X),
                inc=rdv, waits=[(rdv, rdv.n)])
            kb.op("vector", lambda e: e.tensor_scalar(
                smk[:], ssc[:], scols[:, 1:2], None, ALU.is_equal),
                inc=rdv, waits=[(rdv, rdv.n)])
            kb.op("vector", lambda e: e.scalar_tensor_tensor(
                sprob[:], smk[:], scols[:, 1:2], sprob[:],
                ALU.mult, ALU.add),
                inc=rdv, waits=[(rdv, rdv.n)])
            kb.op("vector", lambda e: e.tensor_tensor(
                scols[:, 0:1], scols[:, 0:1], scols[:, 1:2], ALU.add),
                inc=rdv, waits=[(rdv, rdv.n)])
            kb.op("vector", lambda e: e.scalar_tensor_tensor(
                ssc[:], smk[:], -1e9, ssc[:], ALU.mult, ALU.add),
                inc=rdv, waits=[(rdv, rdv.n)])
        kb.op("vector", lambda e: e.reciprocal(scols[:, 2:3], scols[:, 0:1]),
              inc=rdv, waits=[(rdv, rdv.n)])
        kb.op("vector", lambda e: e.tensor_scalar(
            sprob[:], sprob[:], scols[:, 2:3], None, ALU.mult),
            inc=rdv, waits=[(rdv, rdv.n)])
        kb.dma("sync", lambda e, t=t: e.dma_start(
            out=probs_d[t * 128:(t + 1) * 128, :], in_=sprob[:]),
            inc=stp, waits=[(rdv, rdv.n)])

    for e4 in range(4):
        kb.dma("sync", lambda e, e4=e4: e.dma_start(
            out=sw2.ap().rearrange("p (i k) -> p i k", k=512),
            in_=w2s_d.ap()[:, e4 * 512:(e4 + 1) * 512]
            .rearrange("(i p) k -> p i k", p=128)),
            inc=ld2, waits=[(mm2, mm2.n)])
        my2 = ld2.n
        wb = e4 % 2
        kb.dma("sync", lambda e, e4=e4, wb=wb: e.dma_start(
            out=swp[wb].ap().rearrange("p (dd k) -> p dd k", k=512),
            in_=w_proj_d.ap()[:, e4 * 512:(e4 + 1) * 512]
            .rearrange("(dd p) k -> p dd k", p=128)),
            inc=ldp[wb], waits=[(mm2, max(0, mm2.n - 2))])
        myp = ldp[wb].n
        for t in range(NT):
            gi = e4 * NT + t
            pb = gi % 2
            eb = gi % 2
            ob = gi % 2
            kb.dma("sync", lambda e, t=t, ob=ob: e.dma_start(
                out=soc[ob].ap().rearrange("p (dd k) -> p dd k", k=128),
                in_=o1T_d.ap()[:, t * 128:(t + 1) * 128]
                .rearrange("(dd p) k -> p dd k", p=128)),
                inc=ldo[ob], waits=[(mm2, max(0, mm2.n - 1))])
            myo = ldo[ob].n
            kb.dma("sync", lambda e, e4=e4, t=t, eb=eb: e.dma_start(
                out=sxr[eb][:],
                in_=x1_rows[t * 128:(t + 1) * 128,
                            e4 * 512:(e4 + 1) * 512]),
                inc=ldr[eb], waits=[(dv2, max(0, dv2.n - 1))])
            myr = ldr[eb].n
            for i in range(NMH):
                kb.op("tensor", lambda e, i=i, t=t, pb=pb: e.matmul(
                    pd[pb][:],
                    sh[:, i * D + t * 128:i * D + (t + 1) * 128],
                    sw2[:, i * 512:(i + 1) * 512],
                    start=(i == 0), stop=False),
                    waits=([(ld2, my2), (ev1v, h_done),
                            (ev2, max(0, ev2.n - 1))]
                           if i == 0 else ()))
            for dd in range(16):
                kb.op("tensor", lambda e, dd=dd, wb=wb, ob=ob, pb=pb:
                      e.matmul(
                          pd[pb][:],
                          soc[ob][:, dd * 128:(dd + 1) * 128],
                          swp[wb][:, dd * 512:(dd + 1) * 512],
                          start=False, stop=(dd == 15)),
                      inc=(mm2 if dd == 15 else None),
                      waits=([(ldp[wb], myp), (ldo[ob], myo)]
                             if dd == 0 else ()))
            kb.op("scalar", lambda e, pb=pb, eb=eb: e.activation(
                sac[eb][:], pd[pb][:], AF.Identity),
                inc=ev2, waits=[(mm2, mm2.n), (dv2, max(0, dv2.n - 1)),
                                (stc[eb], stc[eb].n)])
            kb.op("vector", lambda e, eb=eb: e.tensor_tensor(
                sac[eb][:], sac[eb][:], sxr[eb][:], ALU.add),
                inc=dv2, waits=[(ev2, ev2.n), (ldr[eb], myr)])
            kb.dma("sync", lambda e, e4=e4, t=t, eb=eb: e.dma_start(
                out=ybase_d[t * 128:(t + 1) * 128,
                            e4 * 512:(e4 + 1) * 512],
                in_=sac[eb][:]), inc=stc[eb], waits=[(dv2, dv2.n)])
    return dict(stc=stc, stp=stp)


def emit_moe_unit(kb, st, bufs, uid, w1_ap, w3_ap, w2_ap, xnc_ap, probs_ap,
                  expert_col, out_ap, accum, ncols=2):
    nc = kb.nc
    b = bufs
    ldx, ld1, ld3, ld2, ldpr = b["ldx"], b["ld1"], b["ld3"], b["ld2"], b["ldpr"]
    mm1, ev1a, ev1v, ev1s = b["mm1"], b["ev1a"], b["ev1v"], b["ev1s"]
    mm2, ev2 = b["mm2"], b["ev2"]
    stc = b["stc"]

    if xnc_ap is not None:
        kb.dma("sync", lambda e: e.dma_start(
            out=b["sxn"].ap().rearrange("p (cc t) -> p cc t", t=TLOC),
            in_=xnc_ap.rearrange("(cc p) t -> p cc t", p=128)),
            inc=ldx, waits=[(mm1, mm1.n), (b["rmmg"], b["rmmg"].n)])
        b["myx"] = ldx.n
        kb.dma("sync", lambda e, ncols=ncols: e.dma_start(
            out=b["spr"].ap()[:, 0:NT * ncols]
            .rearrange("p (tt k) -> p tt k", k=ncols),
            in_=probs_ap.rearrange("(tt p) k -> p tt k", p=128)),
            inc=ldpr, waits=[(ev2, ev2.n)])
        b["mypr"] = ldpr.n
    myx = b["myx"]
    mypr = b["mypr"]

    for i in range(NMH):
        ib = i % 2
        kb.dma("sync", lambda e, i=i, ib=ib: e.dma_start(
            out=b["sw1"][ib].ap().rearrange("p (cc k) -> p cc k", k=128),
            in_=w1_ap[:, i * 128:(i + 1) * 128]
            .rearrange("(cc p) k -> p cc k", p=128)),
            inc=ld1[ib], waits=[(mm1, max(0, mm1.n - 2))])
        my1 = ld1[ib].n
        kb.dma("sync", lambda e, i=i, ib=ib: e.dma_start(
            out=b["sw3"][ib].ap().rearrange("p (cc k) -> p cc k", k=128),
            in_=w3_ap[:, i * 128:(i + 1) * 128]
            .rearrange("(cc p) k -> p cc k", p=128)),
            inc=ld3[ib], waits=[(mm1, max(0, mm1.n - 2))])
        my3 = ld3[ib].n
        for tq in range(4):
            gi = i * 4 + tq
            pb = gi % 2
            eb = gi % 2
            for cc in range(16):
                kb.op("tensor", lambda e, ib=ib, tq=tq, cc=cc, pb=pb:
                      e.matmul(b["pg"][pb][:],
                               b["sw1"][ib][:, cc * 128:(cc + 1) * 128],
                               b["sxn"][:, cc * TLOC + tq * 512:
                                        cc * TLOC + (tq + 1) * 512],
                               start=(cc == 0), stop=(cc == 15)),
                      waits=([(ld1[ib], my1), (ldx, myx),
                              (ev1v, max(0, ev1v.n - 1))]
                             if cc == 0 else ()))
            for cc in range(16):
                kb.op("tensor", lambda e, ib=ib, tq=tq, cc=cc, pb=pb:
                      e.matmul(b["pu"][pb][:],
                               b["sw3"][ib][:, cc * 128:(cc + 1) * 128],
                               b["sxn"][:, cc * TLOC + tq * 512:
                                        cc * TLOC + (tq + 1) * 512],
                               start=(cc == 0), stop=(cc == 15)),
                      inc=(mm1 if cc == 15 else None),
                      waits=[(ld3[ib], my3)] if cc == 0 else ())
            kb.op("scalar", lambda e, pb=pb, eb=eb: e.activation(
                b["ssg"][eb][:], b["pg"][pb][:], AF.Identity),
                inc=ev1a, waits=[(mm1, mm1.n), (ev1v, max(0, ev1v.n - 1))])
            kb.op("vector", lambda e, pb=pb, eb=eb: e.tensor_tensor(
                b["spp"][eb][:], b["ssg"][eb][:], b["pu"][pb][:], ALU.mult),
                inc=ev1v, waits=[(ev1a, ev1a.n),
                                 (ev1s, max(0, ev1s.n - 1))])
            kb.op("scalar", lambda e, i=i, tq=tq, eb=eb: e.activation(
                b["sh"][:, i * D + tq * 512:i * D + (tq + 1) * 512],
                b["spp"][eb][:], AF.Silu),
                inc=ev1s, waits=[(ev1v, ev1v.n), (mm2, mm2.n)])
    h_done = ev1s.n

    for e4 in range(4):
        db = e4 % 2
        kb.dma("sync", lambda e, e4=e4, db=db: e.dma_start(
            out=b["sw2"][db].ap().rearrange("p (i k) -> p i k", k=512),
            in_=w2_ap[:, e4 * 512:(e4 + 1) * 512]
            .rearrange("(i p) k -> p i k", p=128)),
            inc=ld2[db], waits=[(mm2, max(0, mm2.n - 16))])
        my2 = ld2[db].n
        for t in range(NT):
            gi2 = e4 * NT + t
            pb = gi2 % 2
            eb = gi2 % 2
            for i in range(NMH):
                kb.op("tensor", lambda e, i=i, t=t, pb=pb, db=db: e.matmul(
                    b["pd"][pb][:],
                    b["sh"][:, i * D + t * 128:i * D + (t + 1) * 128],
                    b["sw2"][db][:, i * 512:(i + 1) * 512],
                    start=(i == 0), stop=(i == NMH - 1)),
                    inc=(mm2 if i == NMH - 1 else None),
                    waits=([(ld2[db], my2), (ev1s, h_done),
                            (ev2, max(0, ev2.n - 1))]
                           if i == 0 else ()))
            kb.op("scalar", lambda e, t=t, pb=pb, eb=eb,
                  ec=expert_col, nco=ncols: e.activation(
                      b["sev"][eb][:], b["pd"][pb][:], AF.Identity,
                      scale=b["spr"][:, t * nco + ec:t * nco + ec + 1]),
                  inc=ev2, waits=[(mm2, mm2.n), (ldpr, mypr),
                                  (stc[eb], stc[eb].n)])
            kb.dma("gpsimd", lambda e, e4=e4, t=t, eb=eb, out_ap=out_ap,
                   accum=accum: e.dma_start(
                       out=out_ap[t * 128:(t + 1) * 128,
                                  e4 * 512:(e4 + 1) * 512],
                       in_=b["sev"][eb][:],
                       **(dict(accum_op=ALU.add) if accum else {})),
                   inc=stc[eb], waits=[(ev2, ev2.n)])


def mk_moe_bufs(kb, st):
    nc = kb.nc
    b = {}
    for n in ("ldx", "ldpr", "mm1", "ev1a", "ev1v", "ev1s", "mm2",
              "ev2", "rmmg"):
        b[n] = kb.ctr(f"mu_{n}")
    b["ld2"] = [kb.ctr(f"mu_ld2{i}") for i in range(2)]
    b["myx"] = 0
    b["mypr"] = 0
    b["ld1"] = [kb.ctr(f"mu_ld1{i}") for i in range(2)]
    b["ld3"] = [kb.ctr(f"mu_ld3{i}") for i in range(2)]
    b["stc"] = [kb.ctr(f"mu_st{i}") for i in range(2)]
    b["sxn"] = sbuf(nc, st, "mu_xn", [128, NT * D], BF16)
    b["sw1"] = [sbuf(nc, st, f"mu_w1{i}", [128, 16 * 128], BF16)
                for i in range(2)]
    b["sw3"] = [sbuf(nc, st, f"mu_w3{i}", [128, 16 * 128], BF16)
                for i in range(2)]
    b["sw2"] = [sbuf(nc, st, f"mu_w2{i}", [128, NMH * 512], BF16)
                for i in range(2)]
    b["sh"] = sbuf(nc, st, "mu_h", [128, NMH * D], BF16)
    b["ssg"] = [sbuf(nc, st, f"mu_sg{i}", [128, 512], F32) for i in range(2)]
    b["spp"] = [sbuf(nc, st, f"mu_pp{i}", [128, 512], F32) for i in range(2)]
    b["sev"] = [sbuf(nc, st, f"mu_ev{i}", [128, 512], F32) for i in range(2)]
    b["spr"] = sbuf(nc, st, "mu_pr", [128, NT * 16], F32)
    b["pg"] = [psum(nc, st, f"mu_pg{i}", [128, 512], F32) for i in range(2)]
    b["pu"] = [psum(nc, st, f"mu_pu{i}", [128, 512], F32) for i in range(2)]
    b["pd"] = [psum(nc, st, f"mu_pd{i}", [128, 512], F32) for i in range(2)]
    return b


def build_moelocal():
    nc = bass.Bass("TRN2", target_bir_lowering=False)
    x1 = nc.dram_tensor("x1", [TLOC, D], F32, kind="ExternalInput")
    o1T = nc.dram_tensor("o1T", [D, TLOC], BF16, kind="ExternalInput")
    w_proj = nc.dram_tensor("w_proj", [D, D], F32, kind="ExternalInput")
    w1s = nc.dram_tensor("w1s", [D, MH], F32, kind="ExternalInput")
    w3s = nc.dram_tensor("w3s", [D, MH], F32, kind="ExternalInput")
    w2s = nc.dram_tensor("w2s", [MH, D], F32, kind="ExternalInput")
    gwT = nc.dram_tensor("gwT", [D, 16], F32, kind="ExternalInput")
    gbbc = nc.dram_tensor("gbbc", [128, 16], F32, kind="ExternalInput")
    xn1T = nc.dram_tensor("xn1T", [D, TLOC], BF16, kind="ExternalInput")
    ybase = nc.dram_tensor("ybase", [TLOC, D], F32, kind="ExternalOutput")
    probs = nc.dram_tensor("probs", [TLOC, 16], F32, kind="ExternalOutput")
    st = ExitStack()
    with st:
        kb = KB(nc, st)
        C = emit_consts(kb, st, need_ident=False)
        emit_moelocal(kb, st, C, x1, o1T, w_proj, w1s, w3s, w2s, gwT, gbbc,
                      xn1T, ybase, probs)
        kb.barrier()
        block = st.enter_context(nc.Block())
        kb.finalize(block)
    return nc


def build_moeroutd():
    nc = bass.Bass("TRN2", target_bir_lowering=False)
    xnf = nc.dram_tensor("xnf", [D, NCORES * TLOC], BF16,
                         kind="ExternalInput")
    prf = nc.dram_tensor("prf", [NCORES * TLOC, 2], F32,
                         kind="ExternalInput")
    w1r = nc.dram_tensor("w1r", [2, D, MH], F32, kind="ExternalInput")
    w3r = nc.dram_tensor("w3r", [2, D, MH], F32, kind="ExternalInput")
    w2r = nc.dram_tensor("w2r", [2, MH, D], F32, kind="ExternalInput")
    part = nc.dram_tensor("part", [NCORES * TLOC, D], F32,
                          kind="ExternalOutput")
    st = ExitStack()
    with st:
        kb = KB(nc, st)
        C = emit_consts(kb, st, need_ident=False)
        bufs = mk_moe_bufs(kb, st)
        for el in range(2):
            for ch in range(NCORES):
                emit_moe_unit(
                    kb, st, bufs, el * 8 + ch,
                    w1r.ap()[el], w3r.ap()[el], w2r.ap()[el],
                    xnf.ap()[:, ch * TLOC:(ch + 1) * TLOC],
                    prf.ap()[ch * TLOC:(ch + 1) * TLOC, :],
                    el,
                    part[ch * TLOC:(ch + 1) * TLOC, :],
                    accum=(el == 1))
        kb.barrier()
        block = st.enter_context(nc.Block())
        kb.finalize(block)
    return nc


def build_reduce():
    nc = bass.Bass("TRN2", target_bir_lowering=False)
    yb = nc.dram_tensor("ybase", [TLOC, D], F32, kind="ExternalInput")
    ps = [nc.dram_tensor(f"p{i}", [TLOC, D], F32, kind="ExternalInput")
          for i in range(NCORES)]
    out = nc.dram_tensor("out", [TLOC, D], F32, kind="ExternalOutput")
    st = ExitStack()
    with st:
        kb = KB(nc, st)
        ld = [kb.ctr(f"rd_ld{i}") for i in range(2)]
        dv = kb.ctr("rd_dv")
        stc = [kb.ctr(f"rd_st{i}") for i in range(2)]
        sa = [sbuf(nc, st, f"rd_a{i}", [128, D], F32) for i in range(2)]
        sb = [sbuf(nc, st, f"rd_b{i}", [128, D], F32) for i in range(2)]
        for t in range(NT):
            tb = t % 2
            kb.dma("sync", lambda e, t=t, tb=tb: e.dma_start(
                out=sa[tb][:], in_=yb[t * 128:(t + 1) * 128, :]),
                inc=ld[tb], waits=[(stc[tb], stc[tb].n)])
            for i in range(NCORES):
                kb.dma("sync",
                       lambda e, t=t, tb=tb, i=i: e.dma_start(
                           out=sb[tb][:],
                           in_=ps[i][t * 128:(t + 1) * 128, :]),
                       inc=ld[tb], waits=[(dv, dv.n)])
                myld = ld[tb].n
                kb.op("vector", lambda e, tb=tb: e.tensor_tensor(
                    sa[tb][:], sa[tb][:], sb[tb][:], ALU.add),
                    inc=dv, waits=[(ld[tb], myld)])
            kb.dma("sync", lambda e, t=t, tb=tb: e.dma_start(
                out=out[t * 128:(t + 1) * 128, :], in_=sa[tb][:]),
                inc=stc[tb], waits=[(dv, dv.n)])
        kb.barrier()
        block = st.enter_context(nc.Block())
        kb.finalize(block)
    return nc





class _APW:

    def __init__(self, ap):
        self._ap = ap

    def ap(self):
        return self._ap

    def __getitem__(self, k):
        return self._ap[k]


def build_fused():
    nc = bass.Bass("TRN2", target_bir_lowering=False)
    EI = dict(kind="ExternalInput")
    x_rows = nc.dram_tensor("x_rows", [TLOC, D], F32, **EI)
    wqT = nc.dram_tensor("wqT", [D, D], F32, **EI)
    wkT = nc.dram_tensor("wkT", [D, D], F32, **EI)
    wvT = nc.dram_tensor("wvT", [D, D], F32, **EI)
    woT = nc.dram_tensor("woT", [D, D], F32, **EI)
    gamma_bc = nc.dram_tensor("gamma_bc", [128, 512], F32, **EI)
    beta_bc = nc.dram_tensor("beta_bc", [128, 512], F32, **EI)
    masks_d = nc.dram_tensor("masks", [4, 128, 128], BF16, **EI)
    lut_d = nc.dram_tensor("lut", [16], F32, **EI)
    gT_sl = nc.dram_tensor("gT_sl", [256, INTER], I32, **EI)
    uT_sl = nc.dram_tensor("uT_sl", [256, INTER], I32, **EI)
    dT_sl = nc.dram_tensor("dT_sl", [INTER, 256], I32, **EI)
    w_attn = nc.dram_tensor("w_attn", [D, 3 * D], F32, **EI)
    w_proj = nc.dram_tensor("w_proj", [D, D], F32, **EI)
    gwT = nc.dram_tensor("gwT", [D, 16], F32, **EI)
    gbbc = nc.dram_tensor("gbbc", [128, 16], F32, **EI)
    w1s = nc.dram_tensor("w1s", [D, MH], F32, **EI)
    w2s = nc.dram_tensor("w2s", [MH, D], F32, **EI)
    w3s = nc.dram_tensor("w3s", [D, MH], F32, **EI)
    w1r = nc.dram_tensor("w1r", [2, D, MH], F32, **EI)
    w2r = nc.dram_tensor("w2r", [2, MH, D], F32, **EI)
    w3r = nc.dram_tensor("w3r", [2, D, MH], F32, **EI)
    out_d = nc.dram_tensor("out", [TLOC, D], F32, kind="ExternalOutput")

    wg_sl = nc.dram_tensor("wg_sl", [256, INTER], BF16)
    wu_sl = nc.dram_tensor("wu_sl", [256, INTER], BF16)
    wd_sl = nc.dram_tensor("wd_sl", [INTER, 256], BF16)
    wgT_g = nc.dram_tensor("wgT_g", [D, INTER], BF16, addr_space="Shared")
    wuT_g = nc.dram_tensor("wuT_g", [D, INTER], BF16, addr_space="Shared")
    wd_g = nc.dram_tensor("wd_g", [NCORES * INTER, 256], BF16,
                          addr_space="Shared")
    c13 = nc.dram_tensor("c13", [4 * D, MH], BF16)
    c2 = nc.dram_tensor("c2", [2 * MH, D], BF16)
    g13 = nc.dram_tensor("g13", [NCORES * 4 * D, MH], BF16,
                         addr_space="Shared")
    g2 = nc.dram_tensor("g2", [NCORES * 2 * MH, D], BF16,
                        addr_space="Shared")
    kv0c = nc.dram_tensor("kv0c", [2 * D, TLOC], BF16)
    kv0g = nc.dram_tensor("kv0g", [4 * 2 * D, TLOC], BF16)
    kv1c = nc.dram_tensor("kv1c", [2 * D + 16, TLOC], BF16)
    kv1g = nc.dram_tensor("kv1g", [4 * (2 * D + 16), TLOC], BF16)
    dbg = dict(kind="ExternalOutput") if os.environ.get("MK_DBG") else {}
    qT0 = nc.dram_tensor("qT0", [D, TLOC], BF16, **dbg)
    oT0 = nc.dram_tensor("oT0", [D, TLOC], BF16, **dbg)
    xnT_d = nc.dram_tensor("xnT", [D, TLOC], BF16, **dbg)
    ffn0_d = nc.dram_tensor("ffn0", [TLOC, D], F32, **dbg)
    x1_d = nc.dram_tensor("x1", [TLOC, D], F32, **dbg)
    xn1T_d = nc.dram_tensor("xn1T", [D, TLOC], BF16, **dbg)
    q1T = nc.dram_tensor("q1T", [D, TLOC], BF16, **dbg)
    o1T = nc.dram_tensor("o1T", [D, TLOC], BF16, **dbg)
    qn_d = nc.dram_tensor("qn", [16, TLOC], F32, **dbg)
    probs_d = nc.dram_tensor("probs", [TLOC, 16], F32, **dbg)

    wqT_b = nc.dram_tensor("wqT_b", [D, D], BF16)
    wkT_b = nc.dram_tensor("wkT_b", [D, D], BF16)
    wvT_b = nc.dram_tensor("wvT_b", [D, D], BF16)
    woT_b = nc.dram_tensor("woT_b", [D, D], BF16)
    w_attn_b = nc.dram_tensor("w_attn_b", [D, 3 * D], BF16)
    w_proj_b = nc.dram_tensor("w_proj_b", [D, D], BF16)
    w1s_b = nc.dram_tensor("w1s_b", [D, MH], BF16)
    w3s_b = nc.dram_tensor("w3s_b", [D, MH], BF16)
    w2s_b = nc.dram_tensor("w2s_b", [MH, D], BF16)
    gwT_b = nc.dram_tensor("gwT_b", [D, 16], BF16)

    st = ExitStack()
    with st:
        kb = KB(nc, st)
        C = emit_consts(kb, st, need_ident=True)

        stg = kb.pin_ctr()
        snap = {}
        for (nm, src_t, dst_t) in (
                ("wq", wqT, wqT_b), ("wk", wkT, wkT_b), ("wv", wvT, wvT_b),
                ("wat", w_attn, w_attn_b), ("wo", woT, woT_b),
                ("wp", w_proj, w_proj_b), ("w1s", w1s, w1s_b),
                ("w3s", w3s, w3s_b), ("w2s", w2s, w2s_b),
                ("gw", gwT, gwT_b)):
            kb.dma("gpsimd", lambda e, s=src_t, d=dst_t: e.dma_start(
                out=d[:, :], in_=s[:, :]), inc=stg)
            snap[nm] = stg.n

        kb.new_scope()
        with ExitStack() as psA:
            gh = GhostEmitter(kb, psA, lut_d, gT_sl, uT_sl, dT_sl,
                              wg_sl, wu_sl, wd_sl, C)
            xnT_res, lnc = emit_ln_xnt(kb, psA, x_rows, xnT_d, C)
            emit_qkv0(kb, psA, xnT_res, lnc["ev"], wqT_b, wkT_b, wvT_b,
                      gamma_bc, beta_bc, _APW(qT0.ap()),
                      _APW(kv0c.ap()[0:D, :]), _APW(kv0c.ap()[D:2 * D, :]),
                      C, pre=[(stg, snap["wv"])],
                      filler=lambda: gh.emit_next(5))
            ghstc = gh.finish()
        kb.barrier()

        kb.new_scope()
        with ExitStack() as ps:
            cc0 = kb.ctr()
            kb.op("gpsimd", lambda e: e.collective_compute(
                "AllGather", ALU.bypass,
                replica_groups=[[0, 1, 2, 3], [4, 5, 6, 7]],
                ins=[kv0c.ap().opt()], outs=[kv0g.ap().opt()]), inc=cc0)
            mycc0 = cc0.n

            wst = kb.ctr()
            for el in range(2):
                kb.dma("gpsimd", lambda e, el=el: e.dma_start(
                    out=c13[el * 2 * D:el * 2 * D + D, :],
                    in_=w1r.ap()[el]), inc=wst)
                kb.dma("gpsimd", lambda e, el=el: e.dma_start(
                    out=c13[el * 2 * D + D:(el + 1) * 2 * D, :],
                    in_=w3r.ap()[el]), inc=wst)
                kb.dma("gpsimd", lambda e, el=el: e.dma_start(
                    out=c2[el * MH:(el + 1) * MH, :],
                    in_=w2r.ap()[el]), inc=wst)
            ccg = kb.ctr()
            allg = [(wg_sl, wgT_g), (wu_sl, wuT_g), (wd_sl, wd_g),
                    (c13, g13), (c2, g2)]
            for (i_, (cin, cout)) in enumerate(allg):
                kb.op("gpsimd", lambda e, cin=cin, cout=cout:
                      e.collective_compute(
                          "AllGather", ALU.bypass,
                          replica_groups=[list(range(NCORES))],
                          ins=[cin.ap().opt()], outs=[cout.ap().opt()]),
                      inc=ccg,
                      waits=[(ghstc, ghstc.n), (wst, wst.n)]
                      if i_ == 0 else ())

            kv0r = kv0g.ap().rearrange("(r z) t -> r z t", r=4)

            def qsrc0(h):
                return qT0.ap()[h * 128:(h + 1) * 128, :]

            def ksrc0(h):
                return [(r_, kv0r[r_, h * 128:(h + 1) * 128, :])
                        for r_ in range(4)]

            def vsrc0(h):
                return [(r_, kv0r[r_, D:2 * D, h * 128:(h + 1) * 128]
                         .rearrange("(jp tt) d -> tt jp d", tt=128))
                        for r_ in range(4)]

            emit_attn(kb, ps, C, "sq", qsrc0, ksrc0, vsrc0, masks_d,
                      _APW(oT0.ap()), tag="0", pre=[(cc0, mycc0)])
        kb.barrier()

        kb.new_scope()
        with ExitStack() as ps:
            wd_view = wd_g.ap().rearrange("(r i p) k -> p i r k",
                                          r=NCORES, p=128)
            emit_ffn0(kb, ps, C, xnT_d, wgT_g, wuT_g, wd_view, ffn0_d)
        kb.barrier()

        kb.new_scope()
        with ExitStack() as ps:
            r = emit_combine0(kb, ps, C, x_rows, oT0, woT_b, ffn0_d, x1_d)
            kb.wait("sync", r["stc"][0], r["stc"][0].n)
            kb.wait("sync", r["stc"][1], r["stc"][1].n)
            emit_ln_xnt(kb, ps, x1_d, xn1T_d, C, tag="1")
        kb.barrier()

        kb.new_scope()
        with ExitStack() as ps:
            emit_qkv1(kb, ps, C, xn1T_d, w_attn_b, _APW(q1T.ap()),
                      _APW(kv1c.ap()[0:D, :]), _APW(kv1c.ap()[D:2 * D, :]),
                      qn_d, _APW(kv1c.ap()[2 * D:2 * D + 16, :]))
        kb.barrier()

        kb.new_scope()
        cc1 = kb.ctr()
        kb.op("gpsimd", lambda e: e.collective_compute(
            "AllGather", ALU.bypass,
            replica_groups=[[0, 1, 2, 3], [4, 5, 6, 7]],
            ins=[kv1c.ap().opt()], outs=[kv1g.ap().opt()]), inc=cc1)
        kb.barrier()

        kb.new_scope()
        with ExitStack() as ps:
            Z = 2 * D + 16
            kv1r = kv1g.ap().rearrange("(r z) t -> r z t", r=4)
            knb_src = [(r_, kv1r[r_, 2 * D:Z, :]) for r_ in range(4)]
            snegc, ncdv, ncfin = emit_negc(kb, ps, C, qn_d, knb_src,
                                           knb_eng="gpsimd")
            kb.barrier()

            def qsrc1(h):
                return q1T.ap()[h * 128:(h + 1) * 128, :]

            def ksrc1(h):
                return [(r_, kv1r[r_, h * 128:(h + 1) * 128, :])
                        for r_ in range(4)]

            def vsrc1(h):
                return [(r_, kv1r[r_, D:2 * D, h * 128:(h + 1) * 128]
                         .rearrange("(jp tt) d -> tt jp d", tt=128))
                        for r_ in range(4)]

            emit_attn(kb, ps, C, "sm", qsrc1, ksrc1, vsrc1, masks_d,
                      _APW(o1T.ap()),
                      negc_col=lambda h: snegc[:, h:h + 1], tag="1")
        kb.barrier()

        kb.new_scope()
        with ExitStack() as ps:
            emit_moelocal(kb, ps, C, x1_d, o1T, w_proj_b, w1s_b, w3s_b,
                          w2s_b, gwT_b, gbbc, xn1T_d, out_d, probs_d)
        kb.barrier()

        kb.new_scope()
        with ExitStack() as ps:
            bufs = mk_moe_bufs(kb, ps)
            for e_ in range(NE):
                rk, el = divmod(e_, 2)
                first = (e_ == 0)
                w1_ap = g13.ap()[rk * 4 * D + el * 2 * D:
                                 rk * 4 * D + el * 2 * D + D, :]
                w3_ap = g13.ap()[rk * 4 * D + el * 2 * D + D:
                                 rk * 4 * D + (el + 1) * 2 * D, :]
                w2_ap = g2.ap()[rk * 2 * MH + el * MH:
                                rk * 2 * MH + (el + 1) * MH, :]
                emit_moe_unit(kb, ps, bufs, e_, w1_ap, w3_ap, w2_ap,
                              xn1T_d.ap() if first else None,
                              probs_d.ap() if first else None, e_,
                              _APW(out_d.ap()), accum=True, ncols=16)
        kb.barrier()

        block = st.enter_context(nc.Block())
        kb.finalize(block)
    return nc


def kernel_fused(inputs):
    H_ = host_slices(inputs)
    per, lut = H_["per"], H_["lut"]
    R = range(NCORES)
    nc = build_fused()
    ims = []
    for c in R:
        p = per[c]
        ims.append(dict(
            x_rows=p["x_rows"], wqT=p["wqT"], wkT=p["wkT"], wvT=p["wvT"],
            woT=p["woT"], gamma_bc=p["gamma_bc"], beta_bc=p["beta_bc"],
            masks=p["masks"], lut=lut, gT_sl=p["gT_sl"], uT_sl=p["uT_sl"],
            dT_sl=p["dT_sl"], w_attn=p["w_attn"], w_proj=p["w_proj"],
            gwT=p["gwT"], gbbc=p["gbbc"], w1s=p["w1s"], w2s=p["w2s"],
            w3s=p["w3s"], w1r=p["w1r_sl"], w2r=p["w2r_sl"],
            w3r=p["w3r_sl"]))
    res = _run(nc, ims)
    out = np.zeros((2, TGLOB, D), np.float32)
    for c in R:
        B, r = divmod(c, 4)
        out[B].reshape(64, 128, D)[r::4] = \
            np.asarray(res.results[c]["out"]).reshape(16, 128, D)
    return out


def host_slices(inputs):
    x = np.asarray(inputs["L0_attn_hidden_states"])
    per = []
    gidxT = np.ascontiguousarray(np.asarray(inputs["L0_ffn_gate_idx"]).T)
    uidxT = np.ascontiguousarray(np.asarray(inputs["L0_ffn_up_idx"]).T)
    didxT = np.ascontiguousarray(np.asarray(inputs["L0_ffn_down_idx"]).T)
    wqT = np.ascontiguousarray(np.asarray(inputs["L0_attn_Wq"]).T)
    wkT = np.ascontiguousarray(np.asarray(inputs["L0_attn_Wk"]).T)
    wvT = np.ascontiguousarray(np.asarray(inputs["L0_attn_Wv"]).T)
    woT = np.ascontiguousarray(np.asarray(inputs["L0_attn_Wo"]).T)
    gamma = np.asarray(inputs["L0_attn_gamma"])
    beta = np.asarray(inputs["L0_attn_beta"])
    gamma_bc = np.ascontiguousarray(np.tile(gamma[None, :], (128, 4)))
    beta_bc = np.ascontiguousarray(np.tile(beta[None, :], (128, 4)))
    w_attn = np.asarray(inputs["L1_attn_w_attn"])
    w_proj = np.asarray(inputs["L1_attn_w_proj"])
    gwT = np.ascontiguousarray(np.asarray(inputs["L1_ffn_gate_w"]).T)
    gbbc = np.ascontiguousarray(
        np.tile(np.asarray(inputs["L1_ffn_gate_b"])[None, :], (128, 1)))
    w1s = np.asarray(inputs["L1_ffn_w1_shared"])
    w2s = np.asarray(inputs["L1_ffn_w2_shared"])
    w3s = np.asarray(inputs["L1_ffn_w3_shared"])
    w1r = np.asarray(inputs["L1_ffn_w1_routed"])
    w2r = np.asarray(inputs["L1_ffn_w2_routed"])
    w3r = np.asarray(inputs["L1_ffn_w3_routed"])
    tri = np.triu(np.ones((128, 128), np.float32))
    for c in range(NCORES):
        B, r = divmod(c, 4)
        xr = np.ascontiguousarray(
            x[B].reshape(64, 128, D)[r::4].reshape(TLOC, D))
        m = np.zeros((4, 128, 128), np.float32)
        for ri in range(4):
            if ri < r:
                m[ri] = 1.0
            elif ri == r:
                m[ri] = tri
        per.append(dict(
            x_rows=xr,
            gT_sl=np.ascontiguousarray(gidxT[c * 256:(c + 1) * 256, :]),
            uT_sl=np.ascontiguousarray(uidxT[c * 256:(c + 1) * 256, :]),
            dT_sl=np.ascontiguousarray(didxT[:, c * 256:(c + 1) * 256]),
            wqT=wqT, wkT=wkT, wvT=wvT, woT=woT,
            gamma_bc=gamma_bc, beta_bc=beta_bc,
            masks=np.ascontiguousarray(m.astype(BF)),
            w_attn=w_attn, w_proj=w_proj, gwT=gwT, gbbc=gbbc,
            w1s=w1s, w2s=w2s, w3s=w3s,
            w1r_sl=np.ascontiguousarray(w1r[2 * c:2 * c + 2]),
            w2r_sl=np.ascontiguousarray(w2r[2 * c:2 * c + 2]),
            w3r_sl=np.ascontiguousarray(w3r[2 * c:2 * c + 2]),
        ))
    return dict(per=per, lut=np.ascontiguousarray(
        np.asarray(inputs["L0_ffn_lut"]).astype(np.float32)))


def _run(nc, ims):
    return run_bass_kernel_spmd(nc, ims, core_ids=list(range(NCORES)))


def _asm_kv(kT, vv):
    KTb, Vb = [], []
    for B in range(2):
        KTb.append(np.ascontiguousarray(
            np.concatenate([kT[B * 4 + r] for r in range(4)], axis=1)))
        Vb.append(np.ascontiguousarray(
            np.concatenate([vv[B * 4 + r] for r in range(4)], axis=0)))
    return KTb, Vb


def kernel(**inputs):
    if os.environ.get("MK_UNFUSED", "0") != "1":
        return kernel_fused(inputs)
    H_ = host_slices(inputs)
    per, lut = H_["per"], H_["lut"]
    R = range(NCORES)

    nc = build_ghost()
    res = _run(nc, [dict(lut=lut, gT_sl=per[c]["gT_sl"],
                         uT_sl=per[c]["uT_sl"], dT_sl=per[c]["dT_sl"])
                    for c in R])
    wgT = np.concatenate([np.asarray(res.results[c]["wg_sl"]) for c in R], 0)
    wuT = np.concatenate([np.asarray(res.results[c]["wu_sl"]) for c in R], 0)
    wdT = np.concatenate([np.asarray(res.results[c]["wd_sl"]) for c in R], 1)
    wgT = np.ascontiguousarray(wgT)
    wuT = np.ascontiguousarray(wuT)
    wdT = np.ascontiguousarray(wdT)

    nc = build_l0main()
    res = _run(nc, [dict(x_rows=per[c]["x_rows"], wqT=per[c]["wqT"],
                         wkT=per[c]["wkT"], wvT=per[c]["wvT"],
                         gamma_bc=per[c]["gamma_bc"],
                         beta_bc=per[c]["beta_bc"]) for c in R])
    xnT = [np.asarray(res.results[c]["xnT"]) for c in R]
    qT = [np.asarray(res.results[c]["qT"]) for c in R]
    kT = [np.asarray(res.results[c]["kT"]) for c in R]
    vv = [np.asarray(res.results[c]["v"]) for c in R]

    KTb, Vb = _asm_kv(kT, vv)
    nc = build_attn0()
    res = _run(nc, [dict(qT=qT[c], ktb=KTb[c // 4], vb=Vb[c // 4],
                         masks=per[c]["masks"]) for c in R])
    oT = [np.asarray(res.results[c]["oT"]) for c in R]

    nc = build_ffn0()
    res = _run(nc, [dict(xnT=xnT[c], wgT=wgT, wuT=wuT, wdT=wdT) for c in R])
    ffn0 = [np.asarray(res.results[c]["ffn0"]) for c in R]

    nc = build_combine0()
    res = _run(nc, [dict(x_rows=per[c]["x_rows"], oT=oT[c],
                         woT=per[c]["woT"], ffn0=ffn0[c]) for c in R])
    x1 = [np.asarray(res.results[c]["x1"]) for c in R]
    xn1T = [np.asarray(res.results[c]["xn1T"]) for c in R]

    nc = build_qkv1()
    res = _run(nc, [dict(xn1T=xn1T[c], w_attn=per[c]["w_attn"]) for c in R])
    q1T = [np.asarray(res.results[c]["q1T"]) for c in R]
    k1T = [np.asarray(res.results[c]["k1T"]) for c in R]
    v1 = [np.asarray(res.results[c]["v1"]) for c in R]
    qn = [np.asarray(res.results[c]["qn"]) for c in R]
    kn = [np.asarray(res.results[c]["kn"]) for c in R]

    K1Tb, V1b = _asm_kv(k1T, v1)
    knb = [np.ascontiguousarray(
        np.concatenate([kn[B * 4 + r] for r in range(4)], axis=1))
        for B in range(2)]
    nc = build_attn1()
    res = _run(nc, [dict(q1T=q1T[c], k1tb=K1Tb[c // 4], v1b=V1b[c // 4],
                         masks=per[c]["masks"], qn=qn[c], knb=knb[c // 4])
                    for c in R])
    o1T = [np.asarray(res.results[c]["o1T"]) for c in R]

    nc = build_moelocal()
    res = _run(nc, [dict(x1=x1[c], o1T=o1T[c], w_proj=per[c]["w_proj"],
                         w1s=per[c]["w1s"], w3s=per[c]["w3s"],
                         w2s=per[c]["w2s"], gwT=per[c]["gwT"],
                         gbbc=per[c]["gbbc"], xn1T=xn1T[c]) for c in R])
    ybase = [np.asarray(res.results[c]["ybase"]) for c in R]
    probs = [np.asarray(res.results[c]["probs"]) for c in R]

    xnf = np.ascontiguousarray(np.concatenate(xn1T, axis=1))
    prf = np.ascontiguousarray(np.concatenate(probs, axis=0))
    nc = build_moeroutd()
    res = _run(nc, [dict(xnf=xnf,
                         prf=np.ascontiguousarray(prf[:, 2 * c:2 * c + 2]),
                         w1r=per[c]["w1r_sl"], w3r=per[c]["w3r_sl"],
                         w2r=per[c]["w2r_sl"]) for c in R])
    part = [np.asarray(res.results[c]["part"]) for c in R]

    nc = build_reduce()
    ims = []
    for c in R:
        d = dict(ybase=ybase[c])
        for i in R:
            d[f"p{i}"] = np.ascontiguousarray(
                part[i][c * TLOC:(c + 1) * TLOC, :])
        ims.append(d)
    res = _run(nc, ims)
    outr = [np.asarray(res.results[c]["out"]) for c in R]

    out = np.zeros((2, TGLOB, D), np.float32)
    for c in R:
        B, r = divmod(c, 4)
        out[B].reshape(64, 128, D)[r::4] = outr[c].reshape(16, 128, D)
    return out


# version 08
# speedup vs baseline: 1.0477x; 1.0477x over previous
import os
import numpy as np
import ml_dtypes
from contextlib import ExitStack

import concourse.bass as bass
import concourse.mybir as mybir
from concourse.bass_utils import run_bass_kernel_spmd

F32 = mybir.dt.float32
BF16 = mybir.dt.bfloat16
I32 = mybir.dt.int32
AF = mybir.ActivationFunctionType
ALU = mybir.AluOpType
AX = mybir.AxisListType
BF = ml_dtypes.bfloat16

NCORES = 8
D = 2048
H = 16
HD = 128
NT = 16
TLOC = NT * 128
TGLOB = 8192
INTER = 5504
NI = INTER // 128
MH = 1408
NMH = MH // 128
NE = 16
EPS = 1e-5

ENGINES = ("sync", "scalar", "vector", "tensor", "gpsimd")


class Ctr:
    __slots__ = ("h", "n", "name")

    def __init__(self, h, name):
        self.h = h
        self.n = 0
        self.name = name


class KB:

    def __init__(self, nc, st):
        self.nc = nc
        self.st = st
        self.streams = {e: [] for e in ENGINES}
        self.ctrs = []
        self._sems = 0
        self._pin = 0
        self._idx = 0

    def _new_phys(self):
        h = self.st.enter_context(self.nc.semaphore(f"ps{self._sems}"))
        c = Ctr(h, f"ps{self._sems}")
        self._sems += 1
        self.ctrs.append(c)
        return c

    def ctr(self, name=None):
        if self._idx < len(self.ctrs):
            c = self.ctrs[self._idx]
        else:
            c = self._new_phys()
        self._idx += 1
        return c

    def pin_ctr(self, name=None):
        assert self._idx == self._pin == len(self.ctrs) - 0 or True
        c = self._new_phys()
        self._pin = len(self.ctrs)
        self._idx = max(self._idx, self._pin)
        return c

    def new_scope(self):
        self._idx = self._pin

    def op(self, eng, fn, inc=None, incn=1, waits=()):
        wl = [(c, v) for (c, v) in waits if v > 0]
        if inc is not None:
            inc.n += incn
        sem = inc.h if inc is not None else None

        def emit(e, wl=wl, fn=fn, sem=sem, incn=incn):
            for (c, v) in wl:
                e.wait_ge(c.h, v)
            inst = fn(e)
            if sem is not None:
                inst.then_inc(sem, incn)
        self.streams[eng].append(emit)

    def dma(self, eng, fn, inc, waits=()):
        self.op(eng, fn, inc=inc, incn=16, waits=waits)

    def wait(self, eng, ctr, val):
        if val > 0:
            self.streams[eng].append(
                lambda e, c=ctr, v=val: e.wait_ge(c.h, v))

    def barrier(self):
        vals = [(c, c.n) for c in self.ctrs if c.n > 0]
        for e in ENGINES:
            for (c, v) in vals:
                self.wait(e, c, v)

    def finalize(self, block):
        for e in ENGINES:
            items = self.streams[e]
            if not items:
                continue

            def body(eng, items=items):
                for it in items:
                    it(eng)
            getattr(block, e)(body)


def sbuf(nc, st, name, shape, dt):
    return st.enter_context(nc.sbuf_tensor(name, shape, dt))


def psum(nc, st, name, shape, dt):
    return st.enter_context(nc.psum_tensor(name, shape, dt))



def emit_consts(kb, st, need_ident=True):
    nc = kb.nc
    C = {}
    C["c_done"] = kb.pin_ctr("c_done")
    if need_ident:
        C["ident"] = sbuf(nc, st, "ident", [128, 128], BF16)
        C["identi"] = sbuf(nc, st, "identi", [128, 128], I32)
        kb.op("gpsimd", lambda e: e.iota(C["identi"][:], [[-1, 128]], base=0,
                                         channel_multiplier=1), inc=C["c_done"])
        kb.op("vector", lambda e: e.tensor_scalar(
            C["ident"][:], C["identi"][:], 0.0, None, ALU.is_equal),
            inc=C["c_done"], waits=[(C["c_done"], 1)])
    C["cols"] = sbuf(nc, st, "ccols", [128, 8], F32)

    def setcol(i, val):
        kb.op("vector", lambda e: e.memset(C["cols"][:, i:i + 1], val),
              inc=C["c_done"])
    setcol(0, EPS)
    setcol(1, -0.5)
    setcol(2, np.log(HD ** -0.5).item())
    setcol(3, -1.0)
    setcol(4, 1e9)
    setcol(5, 0.5)
    setcol(6, np.log(HD ** -0.5).item())
    setcol(7, HD ** -0.5)
    C["eps"] = C["cols"][:, 0:1]
    C["neghalf"] = C["cols"][:, 1:2]
    C["lnqscale"] = C["cols"][:, 2:3]
    C["neg1"] = C["cols"][:, 3:4]
    C["big"] = C["cols"][:, 4:5]
    C["half16"] = C["cols"][0:16, 5:6]
    C["lnqscale16"] = C["cols"][0:16, 6:7]
    C["qkscale"] = C["cols"][:, 7:8]
    C["ones_row"] = sbuf(nc, st, "conesrow", [1, 128], BF16)
    kb.op("vector", lambda e: e.memset(C["ones_row"][:], 1.0), inc=C["c_done"])
    C["ones128"] = sbuf(nc, st, "cones128", [128, 128], BF16)
    kb.op("vector", lambda e: e.memset(C["ones128"][:], 1.0), inc=C["c_done"])
    C["zeros512"] = sbuf(nc, st, "czeros512", [128, 512], BF16)
    kb.op("vector", lambda e: e.memset(C["zeros512"][:], 0.0),
          inc=C["c_done"])
    return C


class GhostEmitter:

    def __init__(self, kb, st, lut_d, gT_sl, uT_sl, dT_sl, wg_out, wu_out,
                 wd_out, C):
        nc = kb.nc
        self.kb = kb
        self.ld = kb.ctr()
        self.cc = kb.ctr()
        self.lbc = kb.ctr()
        self.stc = kb.ctr()
        self.lut_row = sbuf(nc, st, "lutrow", [1, 16], BF16)
        self.lut_bc = sbuf(nc, st, "lutbc", [128, 16], F32)
        self.plut = psum(nc, st, "plut", [128, 512], F32)
        gshared = sbuf(nc, st, "gh_g", [128, 5504], I32)
        self.bufs = [dict(
            g=gshared,
            a=sbuf(nc, st, f"gh_a{i}", [128, 5504], BF16),
        ) for i in range(2)]
        self.tmp = sbuf(nc, st, "gh_t", [128, 5504], BF16)
        self.jobs = []
        for r in range(2):
            self.jobs.append((gT_sl[r * 128:(r + 1) * 128, :],
                              wg_out[r * 128:(r + 1) * 128, :], 5504))
        for r in range(2):
            self.jobs.append((uT_sl[r * 128:(r + 1) * 128, :],
                              wu_out[r * 128:(r + 1) * 128, :], 5504))
        for r in range(NI):
            self.jobs.append((dT_sl[r * 128:(r + 1) * 128, :],
                              wd_out[r * 128:(r + 1) * 128, :], 256))
        kb.dma("gpsimd", lambda e: e.dma_start(
            out=self.lut_row[:], in_=lut_d.ap().unsqueeze(0)), inc=self.ld)
        myl = self.ld.n
        kb.op("tensor", lambda e: e.matmul(
            self.plut[:, 0:16], C["ones_row"][:], self.lut_row[:],
            start=True, stop=True),
            inc=self.lbc, waits=[(self.ld, myl),
                                 (C["c_done"], C["c_done"].n)])
        kb.op("vector", lambda e: e.tensor_copy(
            self.lut_bc[:], self.plut[:, 0:16]),
            inc=self.lbc, waits=[(self.lbc, self.lbc.n)])
        self.mylbc = self.lbc.n
        self.stc0 = self.stc.n
        self.steps = []
        for j in range(len(self.jobs)):
            self.steps.append(("load", j, 0))
            for p in range(16):
                self.steps.append(("pass", j, p))
            self.steps.append(("store", j, 0))
        self.si = 0
        self.marks = {}

    def done(self):
        return self.si >= len(self.steps)

    def emit_next(self, k):
        kb = self.kb
        n = 0
        while n < k and self.si < len(self.steps):
            (kind, j, p) = self.steps[self.si]
            self.si += 1
            n += 1
            srcap, dst, w = self.jobs[j]
            bb = self.bufs[j % 2]
            if kind == "load":
                gw = [(self.stc, self.stc0 + max(0, 16 * (j - 1)))]
                if j > 0:
                    gw.append((self.cc, self.marks[j - 1]))
                kb.dma("gpsimd", lambda e, srcap=srcap, bb=bb, w=w:
                       e.dma_start(out=bb["g"][:, 0:w], in_=srcap),
                       inc=self.ld, waits=gw)
                self.marks[("ld", j)] = self.ld.n
            elif kind == "pass":
                if p == 0:
                    kb.op("vector", lambda e, bb=bb, w=w: e.memset(
                        bb["a"][:, 0:w], 0.0), inc=self.cc,
                        waits=[(self.ld, self.marks[("ld", j)]),
                               (self.lbc, self.mylbc),
                               (self.stc,
                                self.stc0 + max(0, 16 * (j - 1)))])
                kb.op("vector", lambda e, bb=bb, w=w, p=p: e.tensor_scalar(
                    self.tmp[:, 0:w], bb["g"][:, 0:w], float(p),
                    self.lut_bc[:, p:p + 1], ALU.is_equal, ALU.mult),
                    inc=self.cc, waits=[(self.cc, self.cc.n)])
                kb.op("vector", lambda e, bb=bb, w=w: e.tensor_tensor(
                    bb["a"][:, 0:w], bb["a"][:, 0:w], self.tmp[:, 0:w],
                    ALU.add), inc=self.cc, waits=[(self.cc, self.cc.n)])
                if p == 15:
                    self.marks[j] = self.cc.n
            else:
                kb.dma("gpsimd", lambda e, dst=dst, bb=bb, w=w:
                       e.dma_start(out=dst, in_=bb["a"][:, 0:w]),
                       inc=self.stc, waits=[(self.cc, self.marks[j])])

    def finish(self):
        self.emit_next(10 ** 9)
        return self.stc


def emit_ln_xnt(kb, st, x_rows, xnT_out, C, tag=""):
    nc = kb.nc
    ld2 = [kb.ctr(f"ln_ld{i}{tag}") for i in range(2)]
    dv = kb.ctr(f"ln_dv{tag}")
    ac = kb.ctr(f"ln_ac{tag}")
    pe = kb.ctr(f"ln_pe{tag}")
    ev = kb.ctr(f"ln_ev{tag}")
    stc = kb.ctr(f"ln_st{tag}")

    sx = [sbuf(nc, st, f"ln_x{i}{tag}", [128, D], F32) for i in range(2)]
    sxn = [sbuf(nc, st, f"ln_n{i}{tag}", [128, D], BF16) for i in range(2)]
    sbn6 = [sbuf(nc, st, f"ln_b6{i}{tag}", [128, 24], F32) for i in range(2)]
    sagg = [sbuf(nc, st, f"ln_ag{i}{tag}", [128, 2], F32) for i in range(2)]
    srstd = [sbuf(nc, st, f"ln_rs{i}{tag}", [128, 1], F32) for i in range(2)]
    ptr = [psum(nc, st, f"ln_pt{i}{tag}", [128, 1024], BF16) for i in range(2)]
    xnT_res = sbuf(nc, st, f"xnT_res{tag}", [128, NT * D], BF16)

    cd = C["c_done"].n
    apply_end = {}
    for t in range(NT):
        b = t % 2
        kb.dma("sync", lambda e, t=t, b=b:
               e.dma_start(out=sx[b][:], in_=x_rows[t * 128:(t + 1) * 128, :]),
               inc=ld2[b], waits=[(dv, apply_end.get(t - 2, 0))])
        myld = ld2[b].n
        for i in range(4):
            kb.op("vector", lambda e, b=b, i=i: e.bn_stats(
                sbn6[b][:, i * 6:(i + 1) * 6], sx[b][:, i * 512:(i + 1) * 512]),
                inc=dv, waits=[(ld2[b], myld)] if i == 0 else ())
        kb.op("vector", lambda e, b=b: e.bn_aggr(sagg[b][:], sbn6[b][:]),
              inc=dv, waits=[(dv, dv.n)])
        kb.op("scalar", lambda e, b=b: e.activation(
            srstd[b][:], sagg[b][:, 1:2], AF.Ln, bias=C["eps"]),
            inc=ac, waits=[(dv, dv.n), (C["c_done"], cd)])
        kb.op("scalar", lambda e, b=b: e.activation(
            srstd[b][:], srstd[b][:], AF.Exp, scale=C["neghalf"]),
            inc=ac, waits=[(ac, ac.n)])
        kb.op("vector", lambda e, b=b: e.tensor_scalar(
            sxn[b][:], sx[b][:], sagg[b][:, 0:1], srstd[b][:],
            ALU.subtract, ALU.mult),
            inc=dv, waits=[(ac, ac.n), (pe, max(0, pe.n - 16))])
        apply_end[t] = dv.n
        for cc in range(16):
            idx = t * 16 + cc
            pb = idx % 2
            sl = idx % 8
            kb.op("tensor", lambda e, b=b, cc=cc, pb=pb, sl=sl: e.transpose(
                ptr[pb][:, sl * 128:(sl + 1) * 128],
                sxn[b][:, cc * 128:(cc + 1) * 128], C["ident"][:]),
                inc=pe,
                waits=([(dv, dv.n), (C["c_done"], cd)] if cc == 0 else [])
                + [(ev, max(0, ev.n - 1))])
            kb.op("vector", lambda e, t=t, cc=cc, pb=pb, sl=sl: e.tensor_copy(
                xnT_res[:, cc * (NT * 128) + t * 128:
                        cc * (NT * 128) + (t + 1) * 128],
                ptr[pb][:, sl * 128:(sl + 1) * 128]),
                inc=ev, waits=[(pe, pe.n)])
    if xnT_out is not None:
        kb.dma("sync", lambda e: e.dma_start(
            out=xnT_out.ap().rearrange("(cc p) t -> p cc t", p=128),
            in_=xnT_res.ap().rearrange("p (cc t) -> p cc t", t=NT * 128)),
            inc=stc, waits=[(ev, ev.n)])
    return xnT_res, dict(ev=ev, st=stc, dv=dv)


def np_bf16(a):
    return np.asarray(a, dtype=BF)



def build_ghost():
    nc = bass.Bass("TRN2", target_bir_lowering=False)
    lut_d = nc.dram_tensor("lut", [16], F32, kind="ExternalInput")
    gT = nc.dram_tensor("gT_sl", [256, INTER], I32, kind="ExternalInput")
    uT = nc.dram_tensor("uT_sl", [256, INTER], I32, kind="ExternalInput")
    dT = nc.dram_tensor("dT_sl", [INTER, 256], I32, kind="ExternalInput")
    wg = nc.dram_tensor("wg_sl", [256, INTER], BF16, kind="ExternalOutput")
    wu = nc.dram_tensor("wu_sl", [256, INTER], BF16, kind="ExternalOutput")
    wd = nc.dram_tensor("wd_sl", [INTER, 256], BF16, kind="ExternalOutput")
    st = ExitStack()
    with st:
        kb = KB(nc, st)
        C = emit_consts(kb, st, need_ident=False)
        GhostEmitter(kb, st, lut_d, gT, uT, dT, wg, wu, wd, C).finish()
        kb.barrier()
        block = st.enter_context(nc.Block())
        kb.finalize(block)
    return nc


def build_l0main(parts=None):
    parts = parts if parts is not None else os.environ.get("L0PARTS", "all")
    nc = bass.Bass("TRN2", target_bir_lowering=False)
    x_rows = nc.dram_tensor("x_rows", [TLOC, D], F32, kind="ExternalInput")
    wqT = nc.dram_tensor("wqT", [D, D], F32, kind="ExternalInput")
    wkT = nc.dram_tensor("wkT", [D, D], F32, kind="ExternalInput")
    wvT = nc.dram_tensor("wvT", [D, D], F32, kind="ExternalInput")
    gamma_bc = nc.dram_tensor("gamma_bc", [128, 512], F32, kind="ExternalInput")
    beta_bc = nc.dram_tensor("beta_bc", [128, 512], F32, kind="ExternalInput")
    xnT_d = nc.dram_tensor("xnT", [D, TLOC], BF16, kind="ExternalOutput")
    qT_d = nc.dram_tensor("qT", [D, TLOC], BF16, kind="ExternalOutput")
    kT_d = nc.dram_tensor("kT", [D, TLOC], BF16, kind="ExternalOutput")
    v_d = nc.dram_tensor("v", [TLOC, D], BF16, kind="ExternalOutput")

    st = ExitStack()
    with st:
        kb = KB(nc, st)
        C = emit_consts(kb, st)
        xnT_res, lnc = emit_ln_xnt(kb, st, x_rows, xnT_d, C)
        if parts != "ln":
            emit_qkv0(kb, st, xnT_res, lnc["ev"], wqT, wkT, wvT, gamma_bc,
                      beta_bc, qT_d, kT_d, v_d, C, parts=parts)
        kb.barrier()
        block = st.enter_context(nc.Block())
        kb.finalize(block)
    return nc


def emit_qkv0(kb, st, xnT_res, xnt_ev, wqT, wkT, wvT, gamma_bc, beta_bc,
              qT_d, kT_d, v_d, C, parts="all", pre=(), filler=None):
    nc = kb.nc
    ld2 = [kb.ctr(f"qk_ld{i}") for i in range(2)]
    gld = kb.ctr("qk_gld")
    mm = kb.ctr("qk_mm")
    evp = kb.ctr("qk_evp")
    evq = kb.ctr("qk_ev")
    petr = kb.ctr("qk_petr")
    peev = kb.ctr("qk_peev")
    stc2 = [kb.ctr(f"qk_st{i}") for i in range(2)]

    slab = [sbuf(nc, st, f"qk_sl{i}", [128, 16 * 512], BF16) for i in range(2)]
    sgb = sbuf(nc, st, "qk_gb", [128, 1024], F32)
    pq = [psum(nc, st, f"qk_p{i}", [128, 512], F32) for i in range(2)]
    ptr = [psum(nc, st, f"qk_pt{i}", [128, 1024], BF16) for i in range(2)]
    sev = [sbuf(nc, st, f"qk_e{i}", [128, 512], F32) for i in range(2)]
    saf = [sbuf(nc, st, f"qk_a{i}", [128, 512], F32) for i in range(2)]
    sbn6 = [sbuf(nc, st, f"qk_b6{i}", [128, 48], F32) for i in range(2)]
    sagg = [sbuf(nc, st, f"qk_ag{i}", [128, 8], F32) for i in range(2)]
    srstd = [sbuf(nc, st, f"qk_rs{i}", [128, 4], F32) for i in range(2)]
    sqk = [sbuf(nc, st, f"qk_o{i}", [128, 512], BF16) for i in range(2)]
    sqkT = [sbuf(nc, st, f"qk_oT{i}", [128, 512], BF16) for i in range(2)]

    kb.dma("sync", lambda e: e.dma_start(out=sgb[:, 0:512], in_=gamma_bc[:, :]),
           inc=gld)
    kb.dma("sync", lambda e: e.dma_start(out=sgb[:, 512:1024],
                                         in_=beta_bc[:, :]), inc=gld)
    mygld = gld.n

    WS = [("q", wqT, qT_d), ("k", wkT, kT_d), ("v", wvT, v_d)]
    if parts == "v":
        WS = WS[2:]
    elif parts.startswith("q"):
        WS = WS[:1]
    cd = C["c_done"].n
    ginfo = {}
    sidx = 0
    for (wname, wT, out_d) in WS:
        for dc in range(4):
            sb = sidx % 2
            kb.dma("sync", lambda e, wT=wT, dc=dc, sb=sb: e.dma_start(
                out=slab[sb].ap().rearrange("p (cc d) -> p cc d", d=512),
                in_=wT.ap()[:, dc * 512:(dc + 1) * 512]
                    .rearrange("(cc p) d -> p cc d", p=128)),
                inc=ld2[sb], waits=[(mm, max(0, mm.n - NT))] + list(pre))
            myld = ld2[sb].n
            for t in range(NT):
                gidx = sidx * NT + t
                pb = gidx % 2
                eb = gidx % 2
                prev = ginfo.get(gidx - 2, dict(evq=0, petr=0, stc=[0, 0]))
                for cc in range(16):
                    kb.op("tensor", lambda e, sb=sb, t=t, cc=cc, pb=pb:
                          e.matmul(pq[pb][:],
                                   xnT_res[:, cc * TLOC + t * 128:
                                           cc * TLOC + (t + 1) * 128],
                                   slab[sb][:, cc * 512:(cc + 1) * 512],
                                   start=(cc == 0), stop=(cc == 15)),
                          inc=(mm if cc == 15 else None),
                          waits=([(ld2[sb], myld), (xnt_ev, xnt_ev.n),
                                  (evp, max(0, evp.n - 1))]
                                 if cc == 0 else ()))
                if wname == "v":
                    kb.op("scalar", lambda e, pb=pb, eb=eb: e.activation(
                        sqk[eb][:], pq[pb][:], AF.Identity),
                        inc=evp, waits=[(mm, mm.n),
                                        (stc2[eb], prev["stc"][eb]),
                                        (petr, prev["petr"])])
                    kb.dma("sync", lambda e, t=t, dc=dc, eb=eb: e.dma_start(
                        out=v_d[t * 128:(t + 1) * 128,
                                dc * 512:(dc + 1) * 512],
                        in_=sqk[eb][:]), inc=stc2[eb], waits=[(evp, evp.n)])
                elif parts in ("qnofm", "qtrplain"):
                    kb.op("scalar", lambda e, pb=pb, eb=eb: e.activation(
                        sqk[eb][:], pq[pb][:], AF.Identity),
                        inc=evp, waits=[(mm, mm.n), (evq, prev["evq"]),
                                        (petr, prev["petr"])])
                    evp_here = evp.n
                    for hh in range(4):
                        kb.op("tensor", lambda e, eb=eb, hh=hh,
                              pb=petr.n % 2: e.transpose(
                            ptr[pb][:, 0:128],
                            sqk[eb][:, hh * 128:(hh + 1) * 128],
                            C["ident"][:]),
                            inc=petr,
                            waits=[(evp, evp_here),
                                   (peev, max(0, petr.n - 1))])
                        kb.op("vector", lambda e, eb=eb, hh=hh,
                              tb=(petr.n - 1) % 2: e.tensor_copy(
                                  sqkT[eb][:, hh * 128:(hh + 1) * 128],
                                  ptr[tb][:, 0:128]),
                              inc=peev,
                              waits=[(petr, petr.n),
                                     (stc2[eb], prev["stc"][eb])])
                    if parts == "qtrplain":
                        kb.dma("sync", lambda e, t=t, dc=dc, eb=eb,
                               out_d=out_d: e.dma_start(
                                   out=out_d[t * 128:(t + 1) * 128,
                                             dc * 512:(dc + 1) * 512],
                                   in_=sqkT[eb][:]),
                               inc=stc2[eb], waits=[(peev, peev.n)])
                    else:
                        kb.dma("sync", lambda e, t=t, dc=dc, eb=eb,
                               out_d=out_d: e.dma_start(
                                   out=out_d.ap()[dc * 512:(dc + 1) * 512,
                                                  t * 128:(t + 1) * 128]
                                   .rearrange("(hh d) t -> d hh t", d=128),
                                   in_=sqkT[eb].ap().rearrange(
                                       "d (hh t) -> d hh t", t=128)),
                               inc=stc2[eb], waits=[(peev, peev.n)])
                else:
                    kb.op("scalar", lambda e, pb=pb, eb=eb: e.activation(
                        sev[eb][:], pq[pb][:], AF.Identity),
                        inc=evp, waits=[(mm, mm.n), (evq, prev["evq"])])
                    if parts == "qa":
                        kb.op("vector", lambda e, eb=eb: e.tensor_copy(
                            saf[eb][:], sev[eb][:]),
                            inc=evq, waits=[(evp, evp.n)])
                        kb.op("vector", lambda e, eb=eb: e.tensor_copy(
                            saf[eb][:, 0:1], saf[eb][:, 0:1]),
                            inc=evq, waits=[(evq, evq.n)])
                    elif parts == "qb":
                        kb.op("vector", lambda e, eb=eb: e.tensor_tensor(
                            saf[eb][:], sev[eb][:], sgb[:, 0:512], ALU.mult),
                            inc=evq, waits=[(evp, evp.n), (gld, mygld)])
                        kb.op("vector", lambda e, eb=eb: e.tensor_copy(
                            saf[eb][:, 0:1], saf[eb][:, 0:1]),
                            inc=evq, waits=[(evq, evq.n)])
                    else:
                        kb.op("gpsimd", lambda e, eb=eb: e.tensor_tensor(
                            saf[eb][:], sev[eb][:], sgb[:, 0:512], ALU.mult),
                            inc=evq, waits=[(evp, evp.n), (gld, mygld)])
                        kb.op("gpsimd", lambda e, eb=eb: e.tensor_tensor(
                            saf[eb][:], saf[eb][:], sgb[:, 512:1024], ALU.add),
                            inc=evq, waits=[(evq, evq.n)])
                    aff_done = evq.n
                    skip_stats = parts in ("qfm1", "qa", "qb")
                    if skip_stats:
                        kb.op("scalar", lambda e, eb=eb: e.activation(
                            sqk[eb][:], saf[eb][:], AF.Identity),
                            inc=evq, waits=[(evq, aff_done),
                                            (petr, prev["petr"])])
                    for hh in range(4 if not skip_stats else 0):
                        kb.op("vector", lambda e, eb=eb, hh=hh: e.bn_stats(
                            sbn6[eb][:, hh * 12:hh * 12 + 6],
                            saf[eb][:, hh * 128:(hh + 1) * 128]),
                            inc=evq,
                            waits=[(evq, aff_done)] if hh == 0 else ())
                    stats_done = evq.n
                    for hh in range(4 if not skip_stats else 0):
                        kb.op("vector", lambda e, eb=eb, hh=hh: e.bn_aggr(
                            sagg[eb][:, hh * 2:hh * 2 + 2],
                            sbn6[eb][:, hh * 12:hh * 12 + 6]),
                            inc=evq,
                            waits=[(evq, stats_done)] if hh == 0 else ())
                    aggr_done = evq.n
                    if parts in ("all", "q", "k", "qfm3"):
                        for hh in range(4):
                            kb.op("scalar", lambda e, eb=eb, hh=hh:
                                  e.activation(
                                      srstd[eb][:, hh:hh + 1],
                                      sagg[eb][:, 2 * hh + 1:2 * hh + 2],
                                      AF.Ln, bias=C["eps"]),
                                  inc=evq,
                                  waits=([(evq, aggr_done),
                                          (C["c_done"], cd)]
                                         if hh == 0 else ()))
                        ln_done = evq.n
                        qsc = (wname == "q")
                        for hh in range(4):
                            if qsc:
                                kb.op("scalar", lambda e, eb=eb, hh=hh:
                                      e.activation(
                                          srstd[eb][:, hh:hh + 1],
                                          srstd[eb][:, hh:hh + 1], AF.Exp,
                                          scale=C["neghalf"],
                                          bias=C["lnqscale"]),
                                      inc=evq,
                                      waits=[(evq, ln_done)]
                                      if hh == 0 else ())
                            else:
                                kb.op("scalar", lambda e, eb=eb, hh=hh:
                                      e.activation(
                                          srstd[eb][:, hh:hh + 1],
                                          srstd[eb][:, hh:hh + 1], AF.Exp,
                                          scale=C["neghalf"]),
                                      inc=evq,
                                      waits=[(evq, ln_done)]
                                      if hh == 0 else ())
                    else:
                        pass
                    rstd_done = evq.n
                    for hh in range(4 if not skip_stats else 0):
                        if parts == "qfm2":
                            kb.op("vector", lambda e, eb=eb, hh=hh:
                                  e.tensor_scalar(
                                      sqk[eb][:, hh * 128:(hh + 1) * 128],
                                      saf[eb][:, hh * 128:(hh + 1) * 128],
                                      sagg[eb][:, hh * 2:hh * 2 + 1], None,
                                      ALU.subtract),
                                  inc=evq,
                                  waits=([(evq, rstd_done),
                                          (petr, prev["petr"])]
                                         if hh == 0 else ()))
                        else:
                            kb.op("vector", lambda e, eb=eb, hh=hh:
                                  e.tensor_scalar(
                                      sqk[eb][:, hh * 128:(hh + 1) * 128],
                                      saf[eb][:, hh * 128:(hh + 1) * 128],
                                      sagg[eb][:, hh * 2:hh * 2 + 1],
                                      srstd[eb][:, hh:hh + 1],
                                      ALU.subtract, ALU.mult),
                                  inc=evq,
                                  waits=([(evq, rstd_done),
                                          (petr, prev["petr"])]
                                         if hh == 0 else ()))
                    apply_done = evq.n
                    for hh in range(4):
                        kb.op("tensor", lambda e, eb=eb, hh=hh,
                              pb=petr.n % 2: e.transpose(
                            ptr[pb][:, 0:128],
                            sqk[eb][:, hh * 128:(hh + 1) * 128],
                            C["ident"][:]),
                            inc=petr,
                            waits=[(evq, apply_done),
                                   (peev, max(0, petr.n - 1))])
                        kb.op("scalar", lambda e, eb=eb, hh=hh,
                              tb=(petr.n - 1) % 2: e.activation(
                                  sqkT[eb][:, hh * 128:(hh + 1) * 128],
                                  ptr[tb][:, 0:128], AF.Identity),
                              inc=peev,
                              waits=[(petr, petr.n),
                                     (stc2[eb], prev["stc"][eb])])
                    kb.dma("sync", lambda e, t=t, dc=dc, eb=eb, out_d=out_d:
                           e.dma_start(
                               out=out_d.ap()[dc * 512:(dc + 1) * 512,
                                              t * 128:(t + 1) * 128]
                               .rearrange("(hh d) t -> d hh t", d=128),
                               in_=sqkT[eb].ap().rearrange(
                                   "d (hh t) -> d hh t", t=128)),
                           inc=stc2[eb], waits=[(peev, peev.n)])
                ginfo[gidx] = dict(evq=evq.n, petr=petr.n,
                                   stc=[stc2[0].n, stc2[1].n])
                if filler is not None:
                    filler()
            sidx += 1


def emit_attn(kb, st, C, mode, qT_src, kT_src, v_src, masks_d, oT_d,
              negc_col=None, tag="", pre=()):
    nc = kb.nc
    ldq = [kb.ctr() for i in range(2)]
    ldk = [kb.ctr() for i in range(2)]
    ldv = [kb.ctr() for i in range(2)]
    ldm = kb.ctr()
    smm = kb.ctr()
    avz = kb.ctr()
    evs = kb.ctr()
    msk = kb.ctr()
    fdv = kb.ctr()
    stc = [kb.ctr() for i in range(2)]

    skt = [sbuf(nc, st, f"at_kt{i}{tag}", [128, 64 * 128], BF16)
           for i in range(2)]
    svx = [sbuf(nc, st, f"at_vx{i}{tag}", [128, 64 * 128], BF16)
           for i in range(2)]
    sqt = [sbuf(nc, st, f"at_qt{i}{tag}", [128, TLOC], BF16) for i in range(2)]
    spt = [sbuf(nc, st, f"at_pt{i}{tag}", [128, 512], BF16) for i in range(3)]
    sof = [sbuf(nc, st, f"at_of{i}{tag}", [128, 512], F32) for i in range(2)]
    szr = [sbuf(nc, st, f"at_zr{i}{tag}", [128, 512], F32) for i in range(2)]
    sob = [sbuf(nc, st, f"at_ob{i}{tag}", [128, 512], BF16) for i in range(2)]
    smask = sbuf(nc, st, f"at_mk{tag}", [128, 512], BF16)
    pst = [psum(nc, st, f"at_ps{i}{tag}", [128, 512], F32) for i in range(3)]
    po = psum(nc, st, f"at_po{tag}", [128, 512], F32)
    pz = psum(nc, st, f"at_pz{tag}", [128, 512], F32)

    kb.dma("sync", lambda e: e.dma_start(
        out=smask.ap().rearrange("p (r c) -> p r c", c=128),
        in_=masks_d.ap().rearrange("r p c -> p r c")), inc=ldm)
    myldm = ldm.n

    cd = C["c_done"].n
    ev_end = {}
    av_end = {}
    fin_prev = {}
    nst = 0
    for h in range(H):
        hb = h % 2
        guard = [(avz, av_end.get(("head", h - 2), 0))] + list(pre)
        for (ii, (rsel, sap)) in enumerate(kT_src(h)):
            kb.dma("sync", lambda e, rsel=rsel, sap=sap, hb=hb: e.dma_start(
                out=skt[hb][:, rsel * 2048:(rsel + 1) * 2048], in_=sap),
                inc=ldk[hb], waits=guard if ii == 0 else ())
        myk = ldk[hb].n
        for (ii, (rsel, sap)) in enumerate(v_src(h)):
            kb.dma("sync", lambda e, rsel=rsel, sap=sap, hb=hb: e.dma_start(
                out=svx[hb].ap()[:, rsel * 2048:(rsel + 1) * 2048]
                .rearrange("p (jp d) -> p jp d", d=128), in_=sap),
                inc=ldv[hb], waits=guard if ii == 0 else ())
        myv = ldv[hb].n
        kb.dma("sync", lambda e, h=h, hb=hb: e.dma_start(
            out=sqt[hb][:], in_=qT_src(h)), inc=ldq[hb], waits=guard)
        myq = ldq[hb].n

        for G in range(4):
            q0 = 512 * G
            units = [(r_, jp) for r_ in range(4) for jp in range(4 * G + 4)]
            nun = len(units)
            pend = []

            def emit_avz(item, hb=hb):
                (ui, n_, kvslot, s0, width, avwaits) = item
                kb.op("tensor", lambda e, hb=hb, kvslot=kvslot, s0=s0,
                      width=width, ui=ui: e.matmul(
                          po[:, s0 * 128:512],
                          svx[hb][:, kvslot * 128:(kvslot + 1) * 128],
                          spt[n_ % 3][:, 0:width],
                          start=(ui == 0), stop=False),
                      inc=avz, waits=avwaits)
                kb.op("tensor", lambda e, s0=s0, width=width, ui=ui, n_=n_:
                      e.matmul(
                          pz[:, s0 * 128:512],
                          C["ones128"][:],
                          spt[n_ % 3][:, 0:width],
                          start=(ui == 0), stop=False),
                      inc=avz)
                av_end[n_] = avz.n

            for (ui, (r_, jp)) in enumerate(units):
                kvslot = r_ * 16 + jp
                s0 = max(0, jp - 4 * G)
                width = (4 - s0) * 128
                pb = nst % 3
                kb.op("tensor", lambda e, hb=hb, kvslot=kvslot, q0=q0, s0=s0,
                      width=width, pb=pb: e.matmul(
                          pst[pb][:, 0:width],
                          skt[hb][:, kvslot * 128:(kvslot + 1) * 128],
                          sqt[hb][:, q0 + s0 * 128:q0 + 512],
                          start=True, stop=True),
                      inc=smm,
                      waits=[(ldk[hb], myk), (ldq[hb], myq),
                             (evs, ev_end.get(nst - 3, 0)),
                             (C["c_done"], cd)] if ui == 0
                      else [(evs, ev_end.get(nst - 3, 0))])
                mysmm = smm.n
                if len(pend) >= 3:
                    emit_avz(pend.pop(0))
                tb = nst % 3
                ewaits = [(smm, mysmm), (avz, av_end.get(nst - 3, 0))]
                if mode == "sq":
                    kb.op("scalar", lambda e, pb=pb, tb=tb, width=width:
                          e.activation(spt[tb][:, 0:width],
                                       pst[pb][:, 0:width], AF.Square),
                          inc=evs, waits=ewaits)
                else:
                    ncol = negc_col(h)
                    kb.op("scalar", lambda e, pb=pb, tb=tb, width=width,
                          ncol=ncol: e.activation(
                              spt[tb][:, 0:width], pst[pb][:, 0:width],
                              AF.Exp, bias=ncol, scale=C["qkscale"]),
                          inc=evs, waits=ewaits)
                myev = evs.n
                sstar = jp - 4 * G
                if 0 <= sstar <= 3:
                    kb.op("gpsimd", lambda e, tb=tb, sstar=sstar, s0=s0,
                          r_=r_: e.tensor_tensor(
                              spt[tb][:, (sstar - s0) * 128:
                                      (sstar - s0 + 1) * 128],
                              spt[tb][:, (sstar - s0) * 128:
                                      (sstar - s0 + 1) * 128],
                              smask[:, r_ * 128:(r_ + 1) * 128], ALU.mult),
                          inc=msk, waits=[(evs, myev), (ldm, myldm)])
                    avwaits = [(msk, msk.n)]
                else:
                    avwaits = [(evs, myev)]
                if ui == 0:
                    avwaits = avwaits + [(fdv, fin_prev.get("fdv", 0)),
                                         (ldv[hb], myv)]
                pend.append((ui, nst, kvslot, s0, width, avwaits))
                ev_end[nst] = evs.n
                nst += 1
            while pend:
                emit_avz(pend.pop(0))
            kb.op("tensor", lambda e: e.matmul(
                po[:], C["ones128"][:], C["zeros512"][:],
                start=False, stop=True), inc=avz)
            kb.op("tensor", lambda e: e.matmul(
                pz[:], C["ones128"][:], C["zeros512"][:],
                start=False, stop=True), inc=avz)
            av_end[("head", h)] = avz.n
            fb = (h * 4 + G) % 2
            myavz = avz.n
            if mode == "sq":
                kb.op("scalar", lambda e, fb=fb: e.activation(
                    szr[fb][:], pz[:], AF.Ln, bias=C["eps"]),
                    inc=evs, waits=[(avz, myavz), (fdv, fin_prev.get(fb, 0))])
            else:
                kb.op("scalar", lambda e, fb=fb: e.activation(
                    szr[fb][:], pz[:], AF.Ln),
                    inc=evs, waits=[(avz, myavz), (fdv, fin_prev.get(fb, 0))])
            kb.op("scalar", lambda e, fb=fb: e.activation(
                szr[fb][:], szr[fb][:], AF.Exp, scale=C["neg1"]),
                inc=evs, waits=[(evs, evs.n)])
            kb.op("scalar", lambda e, fb=fb: e.activation(
                sof[fb][:], po[:], AF.Identity),
                inc=evs, waits=[(evs, evs.n)])
            myfin = evs.n
            kb.op("gpsimd", lambda e, fb=fb: e.tensor_tensor(
                sob[fb][:], sof[fb][:], szr[fb][:], ALU.mult),
                inc=fdv, waits=[(evs, myfin), (stc[fb], stc[fb].n)])
            fin_prev[fb] = fdv.n
            fin_prev["fdv"] = fdv.n
            kb.dma("sync", lambda e, h=h, G=G, fb=fb: e.dma_start(
                out=oT_d[h * 128:(h + 1) * 128, 512 * G:512 * G + 512],
                in_=sob[fb][:]), inc=stc[fb], waits=[(fdv, fdv.n)])
    return dict(stc=stc)


def build_attn0():
    nc = bass.Bass("TRN2", target_bir_lowering=False)
    qT_d = nc.dram_tensor("qT", [D, TLOC], BF16, kind="ExternalInput")
    ktb = nc.dram_tensor("ktb", [D, TGLOB], BF16, kind="ExternalInput")
    vb = nc.dram_tensor("vb", [TGLOB, D], BF16, kind="ExternalInput")
    masks_d = nc.dram_tensor("masks", [4, 128, 128], BF16,
                             kind="ExternalInput")
    oT_d = nc.dram_tensor("oT", [D, TLOC], BF16, kind="ExternalOutput")
    st = ExitStack()
    with st:
        kb = KB(nc, st)
        C = emit_consts(kb, st, need_ident=False)

        def qsrc(h):
            return qT_d[h * 128:(h + 1) * 128, :]

        def ksrc(h):
            return [(r_, ktb.ap()[h * 128:(h + 1) * 128,
                                  r_ * 2048:(r_ + 1) * 2048])
                    for r_ in range(4)]

        def vsrc(h):
            return [(r_, vb.ap()[r_ * 2048:(r_ + 1) * 2048,
                                 h * 128:(h + 1) * 128]
                     .rearrange("(jp tt) d -> tt jp d", tt=128))
                    for r_ in range(4)]

        emit_attn(kb, st, C, "sq", qsrc, ksrc, vsrc, masks_d, oT_d)
        kb.barrier()
        block = st.enter_context(nc.Block())
        kb.finalize(block)
    return nc


def emit_ffn0(kb, st, C, xnT_d, wgT_d, wuT_d, wd_view, ffn0_d):
    nc = kb.nc
    ldx = [kb.ctr(f"ff_ldx{i}") for i in range(2)]
    ldg = [kb.ctr(f"ff_ldg{i}") for i in range(2)]
    ldu = [kb.ctr(f"ff_ldu{i}") for i in range(2)]
    ldd = [kb.ctr(f"ff_ldd{i}") for i in range(2)]
    mm1 = kb.ctr("ff_mm1")
    ev1a = kb.ctr("ff_ev1a")
    ev1v = kb.ctr("ff_ev1v")
    mm2 = kb.ctr("ff_mm2")
    ev2 = kb.ctr("ff_ev2")
    stc = [kb.ctr(f"ff_st{i}") for i in range(2)]

    sxn = sbuf(nc, st, "ff_xn", [128, 16 * 1024], BF16)
    swg = [sbuf(nc, st, f"ff_wg{i}", [128, 16 * 128], BF16) for i in range(2)]
    swu = [sbuf(nc, st, f"ff_wu{i}", [128, 16 * 128], BF16) for i in range(2)]
    swd = [sbuf(nc, st, "ff_wd0", [128, NI * 512], BF16)]
    sh3 = sbuf(nc, st, "ff_h3", [128, NI * 1024], BF16)
    ssg = [sbuf(nc, st, f"ff_sg{i}", [128, 512], F32) for i in range(2)]
    sev = [sbuf(nc, st, f"ff_ev{i}", [128, 512], F32) for i in range(2)]
    pg = [psum(nc, st, f"ff_pg{i}", [128, 512], F32) for i in range(2)]
    pu = [psum(nc, st, f"ff_pu{i}", [128, 512], F32) for i in range(2)]
    pd = [psum(nc, st, f"ff_pd{i}", [128, 512], F32) for i in range(2)]

    h3_end = {}
    for half in range(2):
        hb = half % 2
        kb.dma("sync", lambda e, half=half: e.dma_start(
            out=sxn.ap().rearrange("p (cc t) -> p cc t", t=1024),
            in_=xnT_d.ap()[:, half * 1024:(half + 1) * 1024]
            .rearrange("(cc p) t -> p cc t", p=128)),
            inc=ldx[hb], waits=[(mm1, mm1.n)])
        myx = ldx[hb].n
        for i in range(NI):
            ib = i % 2
            kb.dma("sync", lambda e, i=i, ib=ib: e.dma_start(
                out=swg[ib].ap().rearrange("p (cc k) -> p cc k", k=128),
                in_=wgT_d.ap()[:, i * 128:(i + 1) * 128]
                .rearrange("(cc p) k -> p cc k", p=128)),
                inc=ldg[ib], waits=[(mm1, max(0, mm1.n - 2))])
            myg = ldg[ib].n
            kb.dma("gpsimd", lambda e, i=i, ib=ib: e.dma_start(
                out=swu[ib].ap().rearrange("p (cc k) -> p cc k", k=128),
                in_=wuT_d.ap()[:, i * 128:(i + 1) * 128]
                .rearrange("(cc p) k -> p cc k", p=128)),
                inc=ldu[ib], waits=[(mm1, max(0, mm1.n - 2))])
            myu = ldu[ib].n
            for tq in range(2):
                pb = (i * 2 + tq) % 2
                gi = half * NI * 2 + i * 2 + tq
                for cc in range(16):
                    kb.op("tensor", lambda e, i=i, ib=ib, tq=tq, cc=cc, pb=pb:
                          e.matmul(pg[pb][:],
                                   swg[ib][:, cc * 128:(cc + 1) * 128],
                                   sxn[:, cc * 1024 + tq * 512:
                                       cc * 1024 + tq * 512 + 512],
                                   start=(cc == 0), stop=(cc == 15)),
                          waits=([(ldg[ib], myg), (ldx[hb], myx),
                                  (ev1v, max(0, ev1v.n - 1))]
                                 if cc == 0 else ()))
                for cc in range(16):
                    kb.op("tensor", lambda e, i=i, ib=ib, tq=tq, cc=cc, pb=pb:
                          e.matmul(pu[pb][:],
                                   swu[ib][:, cc * 128:(cc + 1) * 128],
                                   sxn[:, cc * 1024 + tq * 512:
                                       cc * 1024 + tq * 512 + 512],
                                   start=(cc == 0), stop=(cc == 15)),
                          inc=(mm1 if cc == 15 else None),
                          waits=[(ldu[ib], myu)] if cc == 0 else ())
                mymm = mm1.n
                eb = gi % 2
                kb.op("scalar", lambda e, pb=pb, eb=eb: e.activation(
                    ssg[eb][:], pg[pb][:], AF.Silu),
                    inc=ev1a, waits=[(mm1, mymm),
                                     (ev1v, max(0, ev1v.n - 1))])
                kb.op("vector", lambda e, i=i, tq=tq, pb=pb, eb=eb:
                      e.tensor_tensor(
                          sh3[:, i * 1024 + tq * 512:i * 1024 + tq * 512 + 512],
                          ssg[eb][:], pu[pb][:], ALU.mult),
                      inc=ev1v, waits=[(ev1a, ev1a.n), (mm2, mm2.n)])
        h3_done = ev1v.n
        for e4 in range(4):
            db = 0
            for rr in range(2):
                kb.dma("gpsimd", lambda e, e4=e4, rr=rr: e.dma_start(
                    out=swd[0].ap().rearrange("p (i r k) -> p i r k",
                                              r=2, k=256)[:, :, rr, :],
                    in_=wd_view[:, :, 2 * e4 + rr, :]),
                    inc=ldd[0], waits=[(mm2, mm2.n)] if rr == 0 else ())
            myd = ldd[0].n
            for tsub in range(8):
                gi2 = (half * 4 + e4) * 8 + tsub
                pb = gi2 % 2
                for i in range(NI):
                    kb.op("tensor", lambda e, i=i, tsub=tsub, db=db, pb=pb:
                          e.matmul(pd[pb][:],
                                   sh3[:, i * 1024 + tsub * 128:
                                       i * 1024 + (tsub + 1) * 128],
                                   swd[db][:, i * 512:(i + 1) * 512],
                                   start=(i == 0), stop=(i == NI - 1)),
                          inc=(mm2 if i == NI - 1 else None),
                          waits=([(ldd[db], myd), (ev1v, h3_done),
                                  (ev2, max(0, ev2.n - 1))]
                                 if i == 0 else ()))
                eb = gi2 % 2
                kb.op("scalar", lambda e, pb=pb, eb=eb: e.activation(
                    sev[eb][:], pd[pb][:], AF.Identity),
                    inc=ev2, waits=[(mm2, mm2.n), (stc[eb], stc[eb].n)])
                kb.dma("sync", lambda e, half=half, e4=e4, tsub=tsub, eb=eb:
                       e.dma_start(
                           out=ffn0_d[half * 1024 + tsub * 128:
                                      half * 1024 + (tsub + 1) * 128,
                                      e4 * 512:(e4 + 1) * 512],
                           in_=sev[eb][:]),
                       inc=stc[eb], waits=[(ev2, ev2.n)])
    return dict(stc=stc)


def emit_combine0(kb, st, C, x_rows, oT_d, woT_d, ffn0_d, x1_d):
    nc = kb.nc
    ldo = kb.ctr("cb_ldo")
    ldw = [kb.ctr(f"cb_ldw{i}") for i in range(2)]
    ldf = [kb.ctr(f"cb_ldf{i}") for i in range(2)]
    ldxr = [kb.ctr(f"cb_ldx{i}") for i in range(2)]
    mm = kb.ctr("cb_mm")
    ev = kb.ctr("cb_ev")
    dvv = kb.ctr("cb_dv")
    stc = [kb.ctr(f"cb_st{i}") for i in range(2)]

    soT = sbuf(nc, st, "cb_oT", [128, 16 * D], BF16)
    swo = [sbuf(nc, st, f"cb_wo{i}", [128, 16 * 512], BF16) for i in range(2)]
    sf = [sbuf(nc, st, f"cb_f{i}", [128, 512], F32) for i in range(2)]
    sxr = [sbuf(nc, st, f"cb_x{i}", [128, 512], F32) for i in range(2)]
    sac = [sbuf(nc, st, f"cb_ac{i}", [128, 512], F32) for i in range(2)]
    pp = [psum(nc, st, f"cb_p{i}", [128, 512], F32) for i in range(2)]

    kb.dma("sync", lambda e: e.dma_start(
        out=soT.ap().rearrange("p (dd t) -> p dd t", t=TLOC),
        in_=oT_d.ap().rearrange("(dd p) t -> p dd t", p=128)), inc=ldo)
    myo = ldo.n

    for e4 in range(4):
        wb = e4 % 2
        kb.dma("sync", lambda e, e4=e4, wb=wb: e.dma_start(
            out=swo[wb].ap().rearrange("p (dd k) -> p dd k", k=512),
            in_=woT_d.ap()[:, e4 * 512:(e4 + 1) * 512]
            .rearrange("(dd p) k -> p dd k", p=128)),
            inc=ldw[wb], waits=[(mm, max(0, mm.n - 2))])
        myw = ldw[wb].n
        for t in range(NT):
            gi = e4 * NT + t
            pb = gi % 2
            eb = gi % 2
            kb.dma("sync", lambda e, e4=e4, t=t, eb=eb: e.dma_start(
                out=sf[eb][:],
                in_=ffn0_d[t * 128:(t + 1) * 128, e4 * 512:(e4 + 1) * 512]),
                inc=ldf[eb], waits=[(dvv, max(0, dvv.n - 3))])
            myf = ldf[eb].n
            kb.dma("sync", lambda e, e4=e4, t=t, eb=eb: e.dma_start(
                out=sxr[eb][:],
                in_=x_rows[t * 128:(t + 1) * 128, e4 * 512:(e4 + 1) * 512]),
                inc=ldxr[eb], waits=[(dvv, max(0, dvv.n - 2))])
            myxr = ldxr[eb].n
            for dd in range(16):
                kb.op("tensor", lambda e, dd=dd, t=t, wb=wb, pb=pb:
                      e.matmul(pp[pb][:],
                               soT[:, dd * TLOC + t * 128:
                                   dd * TLOC + (t + 1) * 128],
                               swo[wb][:, dd * 512:(dd + 1) * 512],
                               start=(dd == 0), stop=(dd == 15)),
                      inc=(mm if dd == 15 else None),
                      waits=([(ldw[wb], myw), (ldo, myo),
                              (ev, max(0, ev.n - 1))]
                             if dd == 0 else ()))
            kb.op("scalar", lambda e, pb=pb, eb=eb: e.activation(
                sac[eb][:], pp[pb][:], AF.Identity),
                inc=ev, waits=[(mm, mm.n), (dvv, max(0, dvv.n - 2)),
                               (stc[eb], stc[eb].n)])
            kb.op("vector", lambda e, eb=eb: e.tensor_tensor(
                sac[eb][:], sac[eb][:], sf[eb][:], ALU.add),
                inc=dvv, waits=[(ev, ev.n), (ldf[eb], myf)])
            kb.op("vector", lambda e, eb=eb: e.tensor_tensor(
                sac[eb][:], sac[eb][:], sxr[eb][:], ALU.add),
                inc=dvv, waits=[(dvv, dvv.n), (ldxr[eb], myxr)])
            kb.dma("sync", lambda e, e4=e4, t=t, eb=eb: e.dma_start(
                out=x1_d[t * 128:(t + 1) * 128, e4 * 512:(e4 + 1) * 512],
                in_=sac[eb][:]), inc=stc[eb], waits=[(dvv, dvv.n)])
    return dict(stc=stc)


def build_ffn0():
    nc = bass.Bass("TRN2", target_bir_lowering=False)
    xnT_d = nc.dram_tensor("xnT", [D, TLOC], BF16, kind="ExternalInput")
    wgT_d = nc.dram_tensor("wgT", [D, INTER], BF16, kind="ExternalInput")
    wuT_d = nc.dram_tensor("wuT", [D, INTER], BF16, kind="ExternalInput")
    wdT_d = nc.dram_tensor("wdT", [INTER, D], BF16, kind="ExternalInput")
    ffn0_d = nc.dram_tensor("ffn0", [TLOC, D], F32, kind="ExternalOutput")
    st = ExitStack()
    with st:
        kb = KB(nc, st)
        C = emit_consts(kb, st, need_ident=False)
        wd_view = wdT_d.ap().rearrange("(i p) (r k) -> p i r k",
                                       p=128, k=256)
        emit_ffn0(kb, st, C, xnT_d, wgT_d, wuT_d, wd_view, ffn0_d)
        kb.barrier()
        block = st.enter_context(nc.Block())
        kb.finalize(block)
    return nc


def build_combine0():
    nc = bass.Bass("TRN2", target_bir_lowering=False)
    x_rows = nc.dram_tensor("x_rows", [TLOC, D], F32, kind="ExternalInput")
    oT_d = nc.dram_tensor("oT", [D, TLOC], BF16, kind="ExternalInput")
    woT_d = nc.dram_tensor("woT", [D, D], F32, kind="ExternalInput")
    ffn0_d = nc.dram_tensor("ffn0", [TLOC, D], F32, kind="ExternalInput")
    x1_d = nc.dram_tensor("x1", [TLOC, D], F32, kind="ExternalOutput")
    xn1T_d = nc.dram_tensor("xn1T", [D, TLOC], BF16, kind="ExternalOutput")
    st = ExitStack()
    with st:
        kb = KB(nc, st)
        C = emit_consts(kb, st)
        r = emit_combine0(kb, st, C, x_rows, oT_d, woT_d, ffn0_d, x1_d)
        kb.wait("sync", r["stc"][0], r["stc"][0].n)
        kb.wait("sync", r["stc"][1], r["stc"][1].n)
        emit_ln_xnt(kb, st, x1_d, xn1T_d, C, tag="1")
        kb.barrier()
        block = st.enter_context(nc.Block())
        kb.finalize(block)
    return nc


def emit_qkv1(kb, st, C, xn1T_d, w_attn_d, q1T_d, k1T_d, v1_d, qn_d, kn_d):
    nc = kb.nc
    ldx = kb.ctr("q1_ldx")
    ldw = [kb.ctr(f"q1_ldw{i}") for i in range(2)]
    mm = kb.ctr("q1_mm")
    nmm = kb.ctr("q1_nmm")
    ev = kb.ctr("q1_ev")
    evs = kb.ctr("q1_evs")
    ndv = kb.ctr("q1_ndv")
    stc = [kb.ctr(f"q1_st{i}") for i in range(2)]
    stn = kb.ctr("q1_stn")

    sxn = sbuf(nc, st, "q1_xn", [128, NT * D], BF16)
    slab = [sbuf(nc, st, f"q1_sl{i}", [128, 16 * 512], BF16) for i in range(2)]
    sq = [sbuf(nc, st, f"q1_q{i}", [128, 512], BF16) for i in range(2)]
    ssq = [sbuf(nc, st, f"q1_sq{i}", [128, 512], BF16) for i in range(2)]
    srow = sbuf(nc, st, "q1_row", [1, TLOC], F32)
    pq = [psum(nc, st, f"q1_p{i}", [128, 512], F32) for i in range(2)]
    pn = [psum(nc, st, f"q1_pn{i}", [128, 512], F32) for i in range(2)]

    kb.dma("sync", lambda e: e.dma_start(
        out=sxn.ap().rearrange("p (cc t) -> p cc t", t=TLOC),
        in_=xn1T_d.ap().rearrange("(cc p) t -> p cc t", p=128)), inc=ldx)
    myx = ldx.n
    cd = C["c_done"].n

    for (wi, out_d, n_d) in ((0, q1T_d, qn_d), (1, k1T_d, kn_d)):
        for dd in range(16):
            sb = (wi * 16 + dd) % 2
            kb.dma("sync", lambda e, wi=wi, dd=dd, sb=sb: e.dma_start(
                out=slab[sb].ap()[:, 0:16 * 128]
                .rearrange("p (cc k) -> p cc k", k=128),
                in_=w_attn_d.ap()[:, wi * D + dd * 128:wi * D + (dd + 1) * 128]
                .rearrange("(cc p) k -> p cc k", p=128)),
                inc=ldw[sb], waits=[(mm, max(0, mm.n - 4))])
            myw = ldw[sb].n
            for t4 in range(4):
                gi = (wi * 16 + dd) * 4 + t4
                pb = gi % 2
                eb = gi % 2
                for cc in range(16):
                    kb.op("tensor", lambda e, sb=sb, t4=t4, cc=cc, pb=pb:
                          e.matmul(pq[pb][:],
                                   slab[sb][:, cc * 128:(cc + 1) * 128],
                                   sxn[:, cc * TLOC + t4 * 512:
                                       cc * TLOC + (t4 + 1) * 512],
                                   start=(cc == 0), stop=(cc == 15)),
                          inc=(mm if cc == 15 else None),
                          waits=([(ldw[sb], myw), (ldx, myx),
                                  (evs, max(0, evs.n - 1)),
                                  (C["c_done"], cd)]
                                 if cc == 0 else ()))
                mymm = mm.n
                kb.op("scalar", lambda e, pb=pb, eb=eb: e.activation(
                    sq[eb][:], pq[pb][:], AF.Identity),
                    inc=ev, waits=[(mm, mymm), (stc[eb], stc[eb].n)])
                kb.dma("sync", lambda e, dd=dd, t4=t4, eb=eb, out_d=out_d:
                       e.dma_start(
                           out=out_d[dd * 128:(dd + 1) * 128,
                                     t4 * 512:(t4 + 1) * 512],
                           in_=sq[eb][:]), inc=stc[eb], waits=[(ev, ev.n)])
                kb.op("scalar", lambda e, pb=pb, eb=eb: e.activation(
                    ssq[eb][:], pq[pb][:], AF.Square),
                    inc=evs, waits=[(ev, ev.n), (nmm, max(0, nmm.n - 1))])
                kb.op("tensor", lambda e, eb=eb, pb=pb: e.matmul(
                    pn[pb][:], C["ones128"][:], ssq[eb][:],
                    start=True, stop=True),
                    inc=nmm, waits=[(evs, evs.n), (ndv, max(0, ndv.n - 1))])
                kb.op("vector", lambda e, t4=t4, pb=pb: e.tensor_copy(
                    srow[:, t4 * 512:(t4 + 1) * 512], pn[pb][0:1, :]),
                    inc=ndv, waits=[(nmm, nmm.n), (stn, stn.n)])
            kb.dma("gpsimd", lambda e, dd=dd, n_d=n_d: e.dma_start(
                out=n_d[dd:dd + 1, :], in_=srow[:]),
                inc=stn, waits=[(ndv, ndv.n)])

    for dc in range(4):
        sb = dc % 2
        kb.dma("sync", lambda e, dc=dc, sb=sb: e.dma_start(
            out=slab[sb].ap().rearrange("p (cc k) -> p cc k", k=512),
            in_=w_attn_d.ap()[:, 2 * D + dc * 512:2 * D + (dc + 1) * 512]
            .rearrange("(cc p) k -> p cc k", p=128)),
            inc=ldw[sb], waits=[(mm, mm.n if dc < 2 else mm.n - 16)])
        myw = ldw[sb].n
        for t in range(NT):
            gi = 128 + dc * NT + t
            pb = gi % 2
            eb = gi % 2
            for cc in range(16):
                kb.op("tensor", lambda e, sb=sb, t=t, cc=cc, pb=pb:
                      e.matmul(pq[pb][:],
                               sxn[:, cc * TLOC + t * 128:
                                   cc * TLOC + (t + 1) * 128],
                               slab[sb][:, cc * 512:(cc + 1) * 512],
                               start=(cc == 0), stop=(cc == 15)),
                      inc=(mm if cc == 15 else None),
                      waits=([(ldw[sb], myw), (ev, max(0, ev.n - 1))]
                             if cc == 0 else ()))
            kb.op("scalar", lambda e, pb=pb, eb=eb: e.activation(
                sq[eb][:], pq[pb][:], AF.Identity),
                inc=ev, waits=[(mm, mm.n), (stc[eb], stc[eb].n)])
            kb.dma("sync", lambda e, t=t, dc=dc, eb=eb: e.dma_start(
                out=v1_d[t * 128:(t + 1) * 128, dc * 512:(dc + 1) * 512],
                in_=sq[eb][:]), inc=stc[eb], waits=[(ev, ev.n)])
    return dict(stc=stc)


def emit_negc(kb, st, C, qn_d, knb_src, knb_eng="sync", tag="1"):
    nc = kb.nc
    ld = kb.ctr(f"nc_ld{tag}")
    dv = kb.ctr(f"nc_dv{tag}")
    ac = kb.ctr(f"nc_ac{tag}")
    pe = kb.ctr(f"nc_pe{tag}")

    sqn = sbuf(nc, st, f"nc_qn{tag}", [16, TLOC], F32)
    skn = sbuf(nc, st, f"nc_kn{tag}", [16, TGLOB], F32)
    sm = sbuf(nc, st, f"nc_m{tag}", [16, 4], F32)
    snegb = sbuf(nc, st, f"nc_nb{tag}", [16, 1], BF16)
    srowb = sbuf(nc, st, f"nc_rb{tag}", [1, 16], BF16)
    snegc = sbuf(nc, st, f"nc_c{tag}", [128, 16], F32)
    pnc = psum(nc, st, f"nc_p{tag}", [128, 512], F32)

    ld2_ = kb.ctr(f"nc_ld2{tag}")
    kb.dma("sync", lambda e: e.dma_start(out=sqn[:], in_=qn_d[:, :]), inc=ld)
    for (ii, (rsel, sap)) in enumerate(knb_src):
        if rsel is None:
            dst = skn[:]
        else:
            dst = skn[:, rsel * 2048:(rsel + 1) * 2048]
        kb.dma(knb_eng, lambda e, dst=dst, sap=sap: e.dma_start(
            out=dst, in_=sap), inc=ld2_)
    myld = ld.n
    myld2 = ld2_.n
    kb.op("vector", lambda e: e.reduce_max(sm[:, 0:1], sqn[:], axis=AX.X),
          inc=dv, waits=[(ld, myld)])
    kb.op("vector", lambda e: e.reduce_max(sm[:, 1:2], skn[:], axis=AX.X),
          inc=dv, waits=[(dv, dv.n), (ld2_, myld2)])
    kb.op("vector", lambda e: e.tensor_tensor(
        sm[:, 2:3], sm[:, 0:1], sm[:, 1:2], ALU.mult),
        inc=dv, waits=[(dv, dv.n)])
    kb.op("scalar", lambda e: e.activation(
        sm[:, 3:4], sm[:, 2:3], AF.Ln), inc=ac, waits=[(dv, dv.n)])
    kb.op("scalar", lambda e: e.activation(
        sm[:, 3:4], sm[:, 3:4], AF.Exp, scale=C["half16"],
        bias=C["lnqscale16"]), inc=ac, waits=[(ac, ac.n)])
    kb.op("vector", lambda e: e.tensor_scalar(
        snegb[:], sm[:, 3:4], -1.0, None, ALU.mult),
        inc=dv, waits=[(ac, ac.n)])
    kb.op("tensor", lambda e: e.matmul(
        pnc[0:1, 0:16], snegb[:], C["ident"][0:16, 0:16],
        start=True, stop=True),
        inc=pe, waits=[(dv, dv.n), (C["c_done"], C["c_done"].n)])
    kb.op("vector", lambda e: e.tensor_copy(srowb[:], pnc[0:1, 0:16]),
          inc=dv, waits=[(pe, pe.n)])
    kb.op("tensor", lambda e: e.matmul(
        pnc[:, 0:16], C["ones_row"][:], srowb[:], start=True, stop=True),
        inc=pe, waits=[(dv, dv.n)])
    kb.op("vector", lambda e: e.tensor_copy(snegc[:], pnc[:, 0:16]),
          inc=dv, waits=[(pe, pe.n)])
    fin = dv.n
    return snegc, dv, fin


def build_qkv1():
    nc = bass.Bass("TRN2", target_bir_lowering=False)
    xn1T_d = nc.dram_tensor("xn1T", [D, TLOC], BF16, kind="ExternalInput")
    w_attn_d = nc.dram_tensor("w_attn", [D, 3 * D], F32, kind="ExternalInput")
    q1T_d = nc.dram_tensor("q1T", [D, TLOC], BF16, kind="ExternalOutput")
    k1T_d = nc.dram_tensor("k1T", [D, TLOC], BF16, kind="ExternalOutput")
    v1_d = nc.dram_tensor("v1", [TLOC, D], BF16, kind="ExternalOutput")
    qn_d = nc.dram_tensor("qn", [16, TLOC], F32, kind="ExternalOutput")
    kn_d = nc.dram_tensor("kn", [16, TLOC], F32, kind="ExternalOutput")
    st = ExitStack()
    with st:
        kb = KB(nc, st)
        C = emit_consts(kb, st, need_ident=False)
        emit_qkv1(kb, st, C, xn1T_d, w_attn_d, q1T_d, k1T_d, v1_d, qn_d, kn_d)
        kb.barrier()
        block = st.enter_context(nc.Block())
        kb.finalize(block)
    return nc


def build_attn1():
    nc = bass.Bass("TRN2", target_bir_lowering=False)
    qT_d = nc.dram_tensor("q1T", [D, TLOC], BF16, kind="ExternalInput")
    ktb = nc.dram_tensor("k1tb", [D, TGLOB], BF16, kind="ExternalInput")
    vb = nc.dram_tensor("v1b", [TGLOB, D], BF16, kind="ExternalInput")
    masks_d = nc.dram_tensor("masks", [4, 128, 128], BF16,
                             kind="ExternalInput")
    qn_d = nc.dram_tensor("qn", [16, TLOC], F32, kind="ExternalInput")
    knb_d = nc.dram_tensor("knb", [16, TGLOB], F32, kind="ExternalInput")
    oT_d = nc.dram_tensor("o1T", [D, TLOC], BF16, kind="ExternalOutput")
    st = ExitStack()
    with st:
        kb = KB(nc, st)
        C = emit_consts(kb, st, need_ident=True)
        snegc, ncdv, ncfin = emit_negc(kb, st, C, qn_d,
                                       [(None, knb_d.ap())])
        kb.barrier()

        def qsrc(h):
            return qT_d[h * 128:(h + 1) * 128, :]

        def ksrc(h):
            return [(r_, ktb.ap()[h * 128:(h + 1) * 128,
                                  r_ * 2048:(r_ + 1) * 2048])
                    for r_ in range(4)]

        def vsrc(h):
            return [(r_, vb.ap()[r_ * 2048:(r_ + 1) * 2048,
                                 h * 128:(h + 1) * 128]
                     .rearrange("(jp tt) d -> tt jp d", tt=128))
                    for r_ in range(4)]

        emit_attn(kb, st, C, "sm", qsrc, ksrc, vsrc, masks_d, oT_d,
                  negc_col=lambda h: snegc[:, h:h + 1])
        kb.barrier()
        block = st.enter_context(nc.Block())
        kb.finalize(block)
    return nc


def emit_moelocal(kb, st, C, x1_rows, o1T_d, w_proj_d, w1s_d, w3s_d, w2s_d,
                  gwT_d, gbbc_d, xn1T_d, ybase_d, probs_d):
    nc = kb.nc
    ldx = kb.ctr("ml_ldx")
    ld1 = [kb.ctr(f"ml_ld1{i}") for i in range(2)]
    ld3 = [kb.ctr(f"ml_ld3{i}") for i in range(2)]
    ld2 = kb.ctr("ml_ld2")
    ldp = [kb.ctr(f"ml_ldp{i}") for i in range(2)]
    ldo = [kb.ctr(f"ml_ldo{i}") for i in range(2)]
    ldr = [kb.ctr(f"ml_ldr{i}") for i in range(2)]
    ldg = kb.ctr("ml_ldg")
    ldgb = kb.ctr("ml_ldgb")
    mm1 = kb.ctr("ml_mm1")
    ev1a = kb.ctr("ml_ev1a")
    ev1v = kb.ctr("ml_ev1v")
    mm2 = kb.ctr("ml_mm2")
    ev2 = kb.ctr("ml_ev2")
    dv2 = kb.ctr("ml_dv2")
    stc = [kb.ctr(f"ml_st{i}") for i in range(2)]
    rmm = kb.ctr("ml_rmm")
    rac = kb.ctr("ml_rac")
    rdv = kb.ctr("ml_rdv")
    stp = kb.ctr("ml_stp")

    sxn = sbuf(nc, st, "ml_xn", [128, NT * D], BF16)
    sw1 = [sbuf(nc, st, f"ml_w1{i}", [128, 16 * 128], BF16) for i in range(2)]
    sw3 = [sbuf(nc, st, f"ml_w3{i}", [128, 16 * 128], BF16) for i in range(2)]
    sh = sbuf(nc, st, "ml_h", [128, NMH * D], BF16)
    ssg = [sbuf(nc, st, f"ml_sg{i}", [128, 512], F32) for i in range(2)]
    sw2 = sbuf(nc, st, "ml_w2", [128, NMH * 512], BF16)
    swp = [sbuf(nc, st, f"ml_wp{i}", [128, 16 * 512], BF16) for i in range(2)]
    soc = [sbuf(nc, st, f"ml_oc{i}", [128, 16 * 128], BF16) for i in range(2)]
    sxr = [sbuf(nc, st, f"ml_xr{i}", [128, 512], F32) for i in range(2)]
    sac = [sbuf(nc, st, f"ml_ac{i}", [128, 512], F32) for i in range(2)]
    sgw = sbuf(nc, st, "ml_gw", [128, 16 * 16], BF16)
    sgb = sbuf(nc, st, "ml_gb", [128, 16], F32)
    ssc = sbuf(nc, st, "ml_sc", [128, 16], F32)
    smk = sbuf(nc, st, "ml_mk", [128, 16], F32)
    sprob = sbuf(nc, st, "ml_pr", [128, 16], F32)
    scols = sbuf(nc, st, "ml_co", [128, 8], F32)
    pg = [psum(nc, st, f"ml_pg{i}", [128, 512], F32) for i in range(2)]
    pu = [psum(nc, st, f"ml_pu{i}", [128, 512], F32) for i in range(2)]
    pd = [psum(nc, st, f"ml_pd{i}", [128, 512], F32) for i in range(2)]
    plog = psum(nc, st, "ml_pl", [128, 512], F32)

    kb.dma("sync", lambda e: e.dma_start(
        out=sxn.ap().rearrange("p (cc t) -> p cc t", t=TLOC),
        in_=xn1T_d.ap().rearrange("(cc p) t -> p cc t", p=128)), inc=ldx)
    myx = ldx.n
    kb.dma("sync", lambda e: e.dma_start(
        out=sgw.ap().rearrange("p (cc k) -> p cc k", k=16),
        in_=gwT_d.ap().rearrange("(cc p) k -> p cc k", p=128)), inc=ldg)
    kb.dma("sync", lambda e: e.dma_start(out=sgb[:], in_=gbbc_d[:, :]),
           inc=ldgb)
    myg = ldg.n
    mygb = ldgb.n
    cd = C["c_done"].n

    for i in range(NMH):
        ib = i % 2
        kb.dma("sync", lambda e, i=i, ib=ib: e.dma_start(
            out=sw1[ib].ap().rearrange("p (cc k) -> p cc k", k=128),
            in_=w1s_d.ap()[:, i * 128:(i + 1) * 128]
            .rearrange("(cc p) k -> p cc k", p=128)),
            inc=ld1[ib], waits=[(mm1, max(0, mm1.n - 2))])
        my1 = ld1[ib].n
        kb.dma("sync", lambda e, i=i, ib=ib: e.dma_start(
            out=sw3[ib].ap().rearrange("p (cc k) -> p cc k", k=128),
            in_=w3s_d.ap()[:, i * 128:(i + 1) * 128]
            .rearrange("(cc p) k -> p cc k", p=128)),
            inc=ld3[ib], waits=[(mm1, max(0, mm1.n - 2))])
        my3 = ld3[ib].n
        for tq in range(4):
            gi = i * 4 + tq
            pb = gi % 2
            eb = gi % 2
            for cc in range(16):
                kb.op("tensor", lambda e, ib=ib, tq=tq, cc=cc, pb=pb:
                      e.matmul(pg[pb][:],
                               sw1[ib][:, cc * 128:(cc + 1) * 128],
                               sxn[:, cc * TLOC + tq * 512:
                                   cc * TLOC + (tq + 1) * 512],
                               start=(cc == 0), stop=(cc == 15)),
                      waits=([(ld1[ib], my1), (ldx, myx),
                              (ev1v, max(0, ev1v.n - 1))]
                             if cc == 0 else ()))
            for cc in range(16):
                kb.op("tensor", lambda e, ib=ib, tq=tq, cc=cc, pb=pb:
                      e.matmul(pu[pb][:],
                               sw3[ib][:, cc * 128:(cc + 1) * 128],
                               sxn[:, cc * TLOC + tq * 512:
                                   cc * TLOC + (tq + 1) * 512],
                               start=(cc == 0), stop=(cc == 15)),
                      inc=(mm1 if cc == 15 else None),
                      waits=[(ld3[ib], my3)] if cc == 0 else ())
            kb.op("scalar", lambda e, pb=pb, eb=eb: e.activation(
                ssg[eb][:], pg[pb][:], AF.Silu),
                inc=ev1a, waits=[(mm1, mm1.n), (ev1v, max(0, ev1v.n - 1))])
            kb.op("vector", lambda e, i=i, tq=tq, pb=pb, eb=eb:
                  e.tensor_tensor(
                      sh[:, i * D + tq * 512:i * D + (tq + 1) * 512],
                      ssg[eb][:], pu[pb][:], ALU.mult),
                  inc=ev1v, waits=[(ev1a, ev1a.n), (mm2, mm2.n)])
    h_done = ev1v.n

    for t in range(NT):
        for cc in range(16):
            kb.op("tensor", lambda e, t=t, cc=cc: e.matmul(
                plog[:, 0:16],
                sxn[:, cc * TLOC + t * 128:cc * TLOC + (t + 1) * 128],
                sgw[:, cc * 16:(cc + 1) * 16],
                start=(cc == 0), stop=(cc == 15)),
                inc=(rmm if cc == 15 else None),
                waits=([(ldg, myg), (ldx, myx), (rac, rac.n)]
                       if cc == 0 else ()))
        kb.op("scalar", lambda e: e.activation(ssc[:], plog[:, 0:16],
                                               AF.Sigmoid),
              inc=rac, waits=[(rmm, rmm.n), (rdv, rdv.n)])
        kb.op("vector", lambda e: e.tensor_tensor(ssc[:], ssc[:], sgb[:],
                                                  ALU.add),
              inc=rdv, waits=[(rac, rac.n), (ldgb, mygb)])
        kb.op("vector", lambda e: e.memset(sprob[:], 0.0), inc=rdv,
              waits=[(stp, stp.n)])
        kb.op("vector", lambda e: e.memset(scols[:, 0:1], 0.0), inc=rdv)
        for kk in range(4):
            kb.op("vector", lambda e: e.reduce_max(
                scols[:, 1:2], ssc[:], axis=AX.X),
                inc=rdv, waits=[(rdv, rdv.n)])
            kb.op("vector", lambda e: e.tensor_scalar(
                smk[:], ssc[:], scols[:, 1:2], None, ALU.is_equal),
                inc=rdv, waits=[(rdv, rdv.n)])
            kb.op("vector", lambda e: e.scalar_tensor_tensor(
                sprob[:], smk[:], scols[:, 1:2], sprob[:],
                ALU.mult, ALU.add),
                inc=rdv, waits=[(rdv, rdv.n)])
            kb.op("vector", lambda e: e.tensor_tensor(
                scols[:, 0:1], scols[:, 0:1], scols[:, 1:2], ALU.add),
                inc=rdv, waits=[(rdv, rdv.n)])
            kb.op("vector", lambda e: e.scalar_tensor_tensor(
                ssc[:], smk[:], -1e9, ssc[:], ALU.mult, ALU.add),
                inc=rdv, waits=[(rdv, rdv.n)])
        kb.op("vector", lambda e: e.reciprocal(scols[:, 2:3], scols[:, 0:1]),
              inc=rdv, waits=[(rdv, rdv.n)])
        kb.op("vector", lambda e: e.tensor_scalar(
            sprob[:], sprob[:], scols[:, 2:3], None, ALU.mult),
            inc=rdv, waits=[(rdv, rdv.n)])
        kb.dma("sync", lambda e, t=t: e.dma_start(
            out=probs_d[t * 128:(t + 1) * 128, :], in_=sprob[:]),
            inc=stp, waits=[(rdv, rdv.n)])

    for e4 in range(4):
        kb.dma("sync", lambda e, e4=e4: e.dma_start(
            out=sw2.ap().rearrange("p (i k) -> p i k", k=512),
            in_=w2s_d.ap()[:, e4 * 512:(e4 + 1) * 512]
            .rearrange("(i p) k -> p i k", p=128)),
            inc=ld2, waits=[(mm2, mm2.n)])
        my2 = ld2.n
        wb = e4 % 2
        kb.dma("sync", lambda e, e4=e4, wb=wb: e.dma_start(
            out=swp[wb].ap().rearrange("p (dd k) -> p dd k", k=512),
            in_=w_proj_d.ap()[:, e4 * 512:(e4 + 1) * 512]
            .rearrange("(dd p) k -> p dd k", p=128)),
            inc=ldp[wb], waits=[(mm2, max(0, mm2.n - 2))])
        myp = ldp[wb].n
        for t in range(NT):
            gi = e4 * NT + t
            pb = gi % 2
            eb = gi % 2
            ob = gi % 2
            kb.dma("sync", lambda e, t=t, ob=ob: e.dma_start(
                out=soc[ob].ap().rearrange("p (dd k) -> p dd k", k=128),
                in_=o1T_d.ap()[:, t * 128:(t + 1) * 128]
                .rearrange("(dd p) k -> p dd k", p=128)),
                inc=ldo[ob], waits=[(mm2, max(0, mm2.n - 1))])
            myo = ldo[ob].n
            kb.dma("sync", lambda e, e4=e4, t=t, eb=eb: e.dma_start(
                out=sxr[eb][:],
                in_=x1_rows[t * 128:(t + 1) * 128,
                            e4 * 512:(e4 + 1) * 512]),
                inc=ldr[eb], waits=[(dv2, max(0, dv2.n - 1))])
            myr = ldr[eb].n
            for i in range(NMH):
                kb.op("tensor", lambda e, i=i, t=t, pb=pb: e.matmul(
                    pd[pb][:],
                    sh[:, i * D + t * 128:i * D + (t + 1) * 128],
                    sw2[:, i * 512:(i + 1) * 512],
                    start=(i == 0), stop=False),
                    waits=([(ld2, my2), (ev1v, h_done),
                            (ev2, max(0, ev2.n - 1))]
                           if i == 0 else ()))
            for dd in range(16):
                kb.op("tensor", lambda e, dd=dd, wb=wb, ob=ob, pb=pb:
                      e.matmul(
                          pd[pb][:],
                          soc[ob][:, dd * 128:(dd + 1) * 128],
                          swp[wb][:, dd * 512:(dd + 1) * 512],
                          start=False, stop=(dd == 15)),
                      inc=(mm2 if dd == 15 else None),
                      waits=([(ldp[wb], myp), (ldo[ob], myo)]
                             if dd == 0 else ()))
            kb.op("scalar", lambda e, pb=pb, eb=eb: e.activation(
                sac[eb][:], pd[pb][:], AF.Identity),
                inc=ev2, waits=[(mm2, mm2.n), (dv2, max(0, dv2.n - 1)),
                                (stc[eb], stc[eb].n)])
            kb.op("vector", lambda e, eb=eb: e.tensor_tensor(
                sac[eb][:], sac[eb][:], sxr[eb][:], ALU.add),
                inc=dv2, waits=[(ev2, ev2.n), (ldr[eb], myr)])
            kb.dma("sync", lambda e, e4=e4, t=t, eb=eb: e.dma_start(
                out=ybase_d[t * 128:(t + 1) * 128,
                            e4 * 512:(e4 + 1) * 512],
                in_=sac[eb][:]), inc=stc[eb], waits=[(dv2, dv2.n)])
    return dict(stc=stc, stp=stp)


def emit_moe_unit(kb, st, bufs, uid, w1_ap, w3_ap, w2_ap, xnc_ap, probs_ap,
                  expert_col, out_ap, accum, ncols=2):
    nc = kb.nc
    b = bufs
    ldx, ld1, ld3, ld2, ldpr = b["ldx"], b["ld1"], b["ld3"], b["ld2"], b["ldpr"]
    mm1, ev1a, ev1v, ev1s = b["mm1"], b["ev1a"], b["ev1v"], b["ev1s"]
    mm2, ev2 = b["mm2"], b["ev2"]
    stc = b["stc"]

    if xnc_ap is not None:
        kb.dma("sync", lambda e: e.dma_start(
            out=b["sxn"].ap().rearrange("p (cc t) -> p cc t", t=TLOC),
            in_=xnc_ap.rearrange("(cc p) t -> p cc t", p=128)),
            inc=ldx, waits=[(mm1, mm1.n), (b["rmmg"], b["rmmg"].n)])
        b["myx"] = ldx.n
        kb.dma("sync", lambda e, ncols=ncols: e.dma_start(
            out=b["spr"].ap()[:, 0:NT * ncols]
            .rearrange("p (tt k) -> p tt k", k=ncols),
            in_=probs_ap.rearrange("(tt p) k -> p tt k", p=128)),
            inc=ldpr, waits=[(ev2, ev2.n)])
        b["mypr"] = ldpr.n
    myx = b["myx"]
    mypr = b["mypr"]

    for i in range(NMH):
        ib = i % 2
        kb.dma("sync", lambda e, i=i, ib=ib: e.dma_start(
            out=b["sw1"][ib].ap().rearrange("p (cc k) -> p cc k", k=128),
            in_=w1_ap[:, i * 128:(i + 1) * 128]
            .rearrange("(cc p) k -> p cc k", p=128)),
            inc=ld1[ib], waits=[(mm1, max(0, mm1.n - 2))])
        my1 = ld1[ib].n
        kb.dma("sync", lambda e, i=i, ib=ib: e.dma_start(
            out=b["sw3"][ib].ap().rearrange("p (cc k) -> p cc k", k=128),
            in_=w3_ap[:, i * 128:(i + 1) * 128]
            .rearrange("(cc p) k -> p cc k", p=128)),
            inc=ld3[ib], waits=[(mm1, max(0, mm1.n - 2))])
        my3 = ld3[ib].n
        for tq in range(4):
            gi = i * 4 + tq
            pb = gi % 2
            eb = gi % 2
            for cc in range(16):
                kb.op("tensor", lambda e, ib=ib, tq=tq, cc=cc, pb=pb:
                      e.matmul(b["pg"][pb][:],
                               b["sw1"][ib][:, cc * 128:(cc + 1) * 128],
                               b["sxn"][:, cc * TLOC + tq * 512:
                                        cc * TLOC + (tq + 1) * 512],
                               start=(cc == 0), stop=(cc == 15)),
                      waits=([(ld1[ib], my1), (ldx, myx),
                              (ev1v, max(0, ev1v.n - 1))]
                             if cc == 0 else ()))
            for cc in range(16):
                kb.op("tensor", lambda e, ib=ib, tq=tq, cc=cc, pb=pb:
                      e.matmul(b["pu"][pb][:],
                               b["sw3"][ib][:, cc * 128:(cc + 1) * 128],
                               b["sxn"][:, cc * TLOC + tq * 512:
                                        cc * TLOC + (tq + 1) * 512],
                               start=(cc == 0), stop=(cc == 15)),
                      inc=(mm1 if cc == 15 else None),
                      waits=[(ld3[ib], my3)] if cc == 0 else ())
            kb.op("scalar", lambda e, pb=pb, eb=eb: e.activation(
                b["ssg"][eb][:], b["pg"][pb][:], AF.Identity),
                inc=ev1a, waits=[(mm1, mm1.n), (ev1v, max(0, ev1v.n - 1))])
            kb.op("vector", lambda e, pb=pb, eb=eb: e.tensor_tensor(
                b["spp"][eb][:], b["ssg"][eb][:], b["pu"][pb][:], ALU.mult),
                inc=ev1v, waits=[(ev1a, ev1a.n),
                                 (ev1s, max(0, ev1s.n - 1))])
            kb.op("scalar", lambda e, i=i, tq=tq, eb=eb: e.activation(
                b["sh"][:, i * D + tq * 512:i * D + (tq + 1) * 512],
                b["spp"][eb][:], AF.Silu),
                inc=ev1s, waits=[(ev1v, ev1v.n), (mm2, mm2.n)])
    h_done = ev1s.n

    for e4 in range(4):
        db = e4 % 2
        kb.dma("sync", lambda e, e4=e4, db=db: e.dma_start(
            out=b["sw2"][db].ap().rearrange("p (i k) -> p i k", k=512),
            in_=w2_ap[:, e4 * 512:(e4 + 1) * 512]
            .rearrange("(i p) k -> p i k", p=128)),
            inc=ld2[db], waits=[(mm2, max(0, mm2.n - 16))])
        my2 = ld2[db].n
        for t in range(NT):
            gi2 = e4 * NT + t
            pb = gi2 % 2
            eb = gi2 % 2
            for i in range(NMH):
                kb.op("tensor", lambda e, i=i, t=t, pb=pb, db=db: e.matmul(
                    b["pd"][pb][:],
                    b["sh"][:, i * D + t * 128:i * D + (t + 1) * 128],
                    b["sw2"][db][:, i * 512:(i + 1) * 512],
                    start=(i == 0), stop=(i == NMH - 1)),
                    inc=(mm2 if i == NMH - 1 else None),
                    waits=([(ld2[db], my2), (ev1s, h_done),
                            (ev2, max(0, ev2.n - 1))]
                           if i == 0 else ()))
            kb.op("scalar", lambda e, t=t, pb=pb, eb=eb,
                  ec=expert_col, nco=ncols: e.activation(
                      b["sev"][eb][:], b["pd"][pb][:], AF.Identity,
                      scale=b["spr"][:, t * nco + ec:t * nco + ec + 1]),
                  inc=ev2, waits=[(mm2, mm2.n), (ldpr, mypr),
                                  (stc[eb], stc[eb].n)])
            kb.dma("gpsimd", lambda e, e4=e4, t=t, eb=eb, out_ap=out_ap,
                   accum=accum: e.dma_start(
                       out=out_ap[t * 128:(t + 1) * 128,
                                  e4 * 512:(e4 + 1) * 512],
                       in_=b["sev"][eb][:],
                       **(dict(accum_op=ALU.add) if accum else {})),
                   inc=stc[eb], waits=[(ev2, ev2.n)])


def mk_moe_bufs(kb, st):
    nc = kb.nc
    b = {}
    for n in ("ldx", "ldpr", "mm1", "ev1a", "ev1v", "ev1s", "mm2",
              "ev2", "rmmg"):
        b[n] = kb.ctr(f"mu_{n}")
    b["ld2"] = [kb.ctr(f"mu_ld2{i}") for i in range(2)]
    b["myx"] = 0
    b["mypr"] = 0
    b["ld1"] = [kb.ctr(f"mu_ld1{i}") for i in range(2)]
    b["ld3"] = [kb.ctr(f"mu_ld3{i}") for i in range(2)]
    b["stc"] = [kb.ctr(f"mu_st{i}") for i in range(2)]
    b["sxn"] = sbuf(nc, st, "mu_xn", [128, NT * D], BF16)
    b["sw1"] = [sbuf(nc, st, f"mu_w1{i}", [128, 16 * 128], BF16)
                for i in range(2)]
    b["sw3"] = [sbuf(nc, st, f"mu_w3{i}", [128, 16 * 128], BF16)
                for i in range(2)]
    b["sw2"] = [sbuf(nc, st, f"mu_w2{i}", [128, NMH * 512], BF16)
                for i in range(2)]
    b["sh"] = sbuf(nc, st, "mu_h", [128, NMH * D], BF16)
    b["ssg"] = [sbuf(nc, st, f"mu_sg{i}", [128, 512], F32) for i in range(2)]
    b["spp"] = [sbuf(nc, st, f"mu_pp{i}", [128, 512], F32) for i in range(2)]
    b["sev"] = [sbuf(nc, st, f"mu_ev{i}", [128, 512], F32) for i in range(2)]
    b["spr"] = sbuf(nc, st, "mu_pr", [128, NT * 16], F32)
    b["pg"] = [psum(nc, st, f"mu_pg{i}", [128, 512], F32) for i in range(2)]
    b["pu"] = [psum(nc, st, f"mu_pu{i}", [128, 512], F32) for i in range(2)]
    b["pd"] = [psum(nc, st, f"mu_pd{i}", [128, 512], F32) for i in range(2)]
    return b


def build_moelocal():
    nc = bass.Bass("TRN2", target_bir_lowering=False)
    x1 = nc.dram_tensor("x1", [TLOC, D], F32, kind="ExternalInput")
    o1T = nc.dram_tensor("o1T", [D, TLOC], BF16, kind="ExternalInput")
    w_proj = nc.dram_tensor("w_proj", [D, D], F32, kind="ExternalInput")
    w1s = nc.dram_tensor("w1s", [D, MH], F32, kind="ExternalInput")
    w3s = nc.dram_tensor("w3s", [D, MH], F32, kind="ExternalInput")
    w2s = nc.dram_tensor("w2s", [MH, D], F32, kind="ExternalInput")
    gwT = nc.dram_tensor("gwT", [D, 16], F32, kind="ExternalInput")
    gbbc = nc.dram_tensor("gbbc", [128, 16], F32, kind="ExternalInput")
    xn1T = nc.dram_tensor("xn1T", [D, TLOC], BF16, kind="ExternalInput")
    ybase = nc.dram_tensor("ybase", [TLOC, D], F32, kind="ExternalOutput")
    probs = nc.dram_tensor("probs", [TLOC, 16], F32, kind="ExternalOutput")
    st = ExitStack()
    with st:
        kb = KB(nc, st)
        C = emit_consts(kb, st, need_ident=False)
        emit_moelocal(kb, st, C, x1, o1T, w_proj, w1s, w3s, w2s, gwT, gbbc,
                      xn1T, ybase, probs)
        kb.barrier()
        block = st.enter_context(nc.Block())
        kb.finalize(block)
    return nc


def build_moeroutd():
    nc = bass.Bass("TRN2", target_bir_lowering=False)
    xnf = nc.dram_tensor("xnf", [D, NCORES * TLOC], BF16,
                         kind="ExternalInput")
    prf = nc.dram_tensor("prf", [NCORES * TLOC, 2], F32,
                         kind="ExternalInput")
    w1r = nc.dram_tensor("w1r", [2, D, MH], F32, kind="ExternalInput")
    w3r = nc.dram_tensor("w3r", [2, D, MH], F32, kind="ExternalInput")
    w2r = nc.dram_tensor("w2r", [2, MH, D], F32, kind="ExternalInput")
    part = nc.dram_tensor("part", [NCORES * TLOC, D], F32,
                          kind="ExternalOutput")
    st = ExitStack()
    with st:
        kb = KB(nc, st)
        C = emit_consts(kb, st, need_ident=False)
        bufs = mk_moe_bufs(kb, st)
        for el in range(2):
            for ch in range(NCORES):
                emit_moe_unit(
                    kb, st, bufs, el * 8 + ch,
                    w1r.ap()[el], w3r.ap()[el], w2r.ap()[el],
                    xnf.ap()[:, ch * TLOC:(ch + 1) * TLOC],
                    prf.ap()[ch * TLOC:(ch + 1) * TLOC, :],
                    el,
                    part[ch * TLOC:(ch + 1) * TLOC, :],
                    accum=(el == 1))
        kb.barrier()
        block = st.enter_context(nc.Block())
        kb.finalize(block)
    return nc


def build_reduce():
    nc = bass.Bass("TRN2", target_bir_lowering=False)
    yb = nc.dram_tensor("ybase", [TLOC, D], F32, kind="ExternalInput")
    ps = [nc.dram_tensor(f"p{i}", [TLOC, D], F32, kind="ExternalInput")
          for i in range(NCORES)]
    out = nc.dram_tensor("out", [TLOC, D], F32, kind="ExternalOutput")
    st = ExitStack()
    with st:
        kb = KB(nc, st)
        ld = [kb.ctr(f"rd_ld{i}") for i in range(2)]
        dv = kb.ctr("rd_dv")
        stc = [kb.ctr(f"rd_st{i}") for i in range(2)]
        sa = [sbuf(nc, st, f"rd_a{i}", [128, D], F32) for i in range(2)]
        sb = [sbuf(nc, st, f"rd_b{i}", [128, D], F32) for i in range(2)]
        for t in range(NT):
            tb = t % 2
            kb.dma("sync", lambda e, t=t, tb=tb: e.dma_start(
                out=sa[tb][:], in_=yb[t * 128:(t + 1) * 128, :]),
                inc=ld[tb], waits=[(stc[tb], stc[tb].n)])
            for i in range(NCORES):
                kb.dma("sync",
                       lambda e, t=t, tb=tb, i=i: e.dma_start(
                           out=sb[tb][:],
                           in_=ps[i][t * 128:(t + 1) * 128, :]),
                       inc=ld[tb], waits=[(dv, dv.n)])
                myld = ld[tb].n
                kb.op("vector", lambda e, tb=tb: e.tensor_tensor(
                    sa[tb][:], sa[tb][:], sb[tb][:], ALU.add),
                    inc=dv, waits=[(ld[tb], myld)])
            kb.dma("sync", lambda e, t=t, tb=tb: e.dma_start(
                out=out[t * 128:(t + 1) * 128, :], in_=sa[tb][:]),
                inc=stc[tb], waits=[(dv, dv.n)])
        kb.barrier()
        block = st.enter_context(nc.Block())
        kb.finalize(block)
    return nc





class _APW:

    def __init__(self, ap):
        self._ap = ap

    def ap(self):
        return self._ap

    def __getitem__(self, k):
        return self._ap[k]


def build_fused():
    nc = bass.Bass("TRN2", target_bir_lowering=False)
    EI = dict(kind="ExternalInput")
    x_rows = nc.dram_tensor("x_rows", [TLOC, D], F32, **EI)
    wqT = nc.dram_tensor("wqT", [D, D], F32, **EI)
    wkT = nc.dram_tensor("wkT", [D, D], F32, **EI)
    wvT = nc.dram_tensor("wvT", [D, D], F32, **EI)
    woT = nc.dram_tensor("woT", [D, D], F32, **EI)
    gamma_bc = nc.dram_tensor("gamma_bc", [128, 512], F32, **EI)
    beta_bc = nc.dram_tensor("beta_bc", [128, 512], F32, **EI)
    masks_d = nc.dram_tensor("masks", [4, 128, 128], BF16, **EI)
    lut_d = nc.dram_tensor("lut", [16], F32, **EI)
    gT_sl = nc.dram_tensor("gT_sl", [256, INTER], I32, **EI)
    uT_sl = nc.dram_tensor("uT_sl", [256, INTER], I32, **EI)
    dT_sl = nc.dram_tensor("dT_sl", [INTER, 256], I32, **EI)
    w_attn = nc.dram_tensor("w_attn", [D, 3 * D], F32, **EI)
    w_proj = nc.dram_tensor("w_proj", [D, D], F32, **EI)
    gwT = nc.dram_tensor("gwT", [D, 16], F32, **EI)
    gbbc = nc.dram_tensor("gbbc", [128, 16], F32, **EI)
    w1s = nc.dram_tensor("w1s", [D, MH], F32, **EI)
    w2s = nc.dram_tensor("w2s", [MH, D], F32, **EI)
    w3s = nc.dram_tensor("w3s", [D, MH], F32, **EI)
    w1r = nc.dram_tensor("w1r", [2, D, MH], F32, **EI)
    w2r = nc.dram_tensor("w2r", [2, MH, D], F32, **EI)
    w3r = nc.dram_tensor("w3r", [2, D, MH], F32, **EI)
    out_d = nc.dram_tensor("out", [TLOC, D], F32, kind="ExternalOutput")

    wg_sl = nc.dram_tensor("wg_sl", [256, INTER], BF16)
    wu_sl = nc.dram_tensor("wu_sl", [256, INTER], BF16)
    wd_sl = nc.dram_tensor("wd_sl", [INTER, 256], BF16)
    wgT_g = nc.dram_tensor("wgT_g", [D, INTER], BF16, addr_space="Shared")
    wuT_g = nc.dram_tensor("wuT_g", [D, INTER], BF16, addr_space="Shared")
    wd_g = nc.dram_tensor("wd_g", [NCORES * INTER, 256], BF16,
                          addr_space="Shared")
    c13 = nc.dram_tensor("c13", [4 * D, MH], BF16)
    c2 = nc.dram_tensor("c2", [2 * MH, D], BF16)
    g13 = nc.dram_tensor("g13", [NCORES * 4 * D, MH], BF16,
                         addr_space="Shared")
    g2 = nc.dram_tensor("g2", [NCORES * 2 * MH, D], BF16,
                        addr_space="Shared")
    kv0c = nc.dram_tensor("kv0c", [2 * D, TLOC], BF16)
    kv0g = nc.dram_tensor("kv0g", [4 * 2 * D, TLOC], BF16)
    kv1c = nc.dram_tensor("kv1c", [2 * D + 16, TLOC], BF16)
    kv1g = nc.dram_tensor("kv1g", [4 * (2 * D + 16), TLOC], BF16)
    dbg = dict(kind="ExternalOutput") if os.environ.get("MK_DBG") else {}
    qT0 = nc.dram_tensor("qT0", [D, TLOC], BF16, **dbg)
    oT0 = nc.dram_tensor("oT0", [D, TLOC], BF16, **dbg)
    xnT_d = nc.dram_tensor("xnT", [D, TLOC], BF16, **dbg)
    ffn0_d = nc.dram_tensor("ffn0", [TLOC, D], F32, **dbg)
    x1_d = nc.dram_tensor("x1", [TLOC, D], F32, **dbg)
    xn1T_d = nc.dram_tensor("xn1T", [D, TLOC], BF16, **dbg)
    q1T = nc.dram_tensor("q1T", [D, TLOC], BF16, **dbg)
    o1T = nc.dram_tensor("o1T", [D, TLOC], BF16, **dbg)
    qn_d = nc.dram_tensor("qn", [16, TLOC], F32, **dbg)
    probs_d = nc.dram_tensor("probs", [TLOC, 16], F32, **dbg)

    wqT_b = nc.dram_tensor("wqT_b", [D, D], BF16)
    wkT_b = nc.dram_tensor("wkT_b", [D, D], BF16)
    wvT_b = nc.dram_tensor("wvT_b", [D, D], BF16)
    woT_b = nc.dram_tensor("woT_b", [D, D], BF16)
    w_attn_b = nc.dram_tensor("w_attn_b", [D, 3 * D], BF16)
    w_proj_b = nc.dram_tensor("w_proj_b", [D, D], BF16)
    w1s_b = nc.dram_tensor("w1s_b", [D, MH], BF16)
    w3s_b = nc.dram_tensor("w3s_b", [D, MH], BF16)
    w2s_b = nc.dram_tensor("w2s_b", [MH, D], BF16)
    gwT_b = nc.dram_tensor("gwT_b", [D, 16], BF16)

    st = ExitStack()
    with st:
        kb = KB(nc, st)
        C = emit_consts(kb, st, need_ident=True)

        stg = kb.pin_ctr()
        snap = {}
        for (nm, src_t, dst_t) in (
                ("wq", wqT, wqT_b), ("wk", wkT, wkT_b), ("wv", wvT, wvT_b),
                ("wat", w_attn, w_attn_b), ("wo", woT, woT_b),
                ("wp", w_proj, w_proj_b), ("w1s", w1s, w1s_b),
                ("w3s", w3s, w3s_b), ("w2s", w2s, w2s_b),
                ("gw", gwT, gwT_b)):
            kb.dma("gpsimd", lambda e, s=src_t, d=dst_t: e.dma_start(
                out=d[:, :], in_=s[:, :]), inc=stg)
            snap[nm] = stg.n

        kb.new_scope()
        with ExitStack() as psA:
            gh = GhostEmitter(kb, psA, lut_d, gT_sl, uT_sl, dT_sl,
                              wg_sl, wu_sl, wd_sl, C)
            xnT_res, lnc = emit_ln_xnt(kb, psA, x_rows, xnT_d, C)
            emit_qkv0(kb, psA, xnT_res, lnc["ev"], wqT_b, wkT_b, wvT_b,
                      gamma_bc, beta_bc, _APW(qT0.ap()),
                      _APW(kv0c.ap()[0:D, :]), _APW(kv0c.ap()[D:2 * D, :]),
                      C, pre=[(stg, snap["wv"])],
                      filler=lambda: gh.emit_next(5))
            ghstc = gh.finish()
            wst = kb.ctr()
            for el in range(2):
                kb.dma("gpsimd", lambda e, el=el: e.dma_start(
                    out=c13[el * 2 * D:el * 2 * D + D, :],
                    in_=w1r.ap()[el]), inc=wst)
                kb.dma("gpsimd", lambda e, el=el: e.dma_start(
                    out=c13[el * 2 * D + D:(el + 1) * 2 * D, :],
                    in_=w3r.ap()[el]), inc=wst)
                kb.dma("gpsimd", lambda e, el=el: e.dma_start(
                    out=c2[el * MH:(el + 1) * MH, :],
                    in_=w2r.ap()[el]), inc=wst)
            ccg = kb.ctr()
            allg = [(wg_sl, wgT_g), (wu_sl, wuT_g), (wd_sl, wd_g),
                    (c13, g13), (c2, g2)]
            for (i_, (cin, cout)) in enumerate(allg):
                kb.op("gpsimd", lambda e, cin=cin, cout=cout:
                      e.collective_compute(
                          "AllGather", ALU.bypass,
                          replica_groups=[list(range(NCORES))],
                          ins=[cin.ap().opt()], outs=[cout.ap().opt()]),
                      inc=ccg,
                      waits=[(ghstc, ghstc.n), (wst, wst.n)]
                      if i_ == 0 else ())
        kb.barrier()

        kb.new_scope()
        cc0 = kb.pin_ctr()
        kb.op("gpsimd", lambda e: e.collective_compute(
            "AllGather", ALU.bypass,
            replica_groups=[[0, 1, 2, 3], [4, 5, 6, 7]],
            ins=[kv0c.ap().opt()], outs=[kv0g.ap().opt()]), inc=cc0)
        mycc0 = cc0.n
        with ExitStack() as ps:
            wd_view = wd_g.ap().rearrange("(r i p) k -> p i r k",
                                          r=NCORES, p=128)
            emit_ffn0(kb, ps, C, xnT_d, wgT_g, wuT_g, wd_view, ffn0_d)
        kb.barrier()

        kb.new_scope()
        with ExitStack() as ps:
            kv0r = kv0g.ap().rearrange("(r z) t -> r z t", r=4)

            def qsrc0(h):
                return qT0.ap()[h * 128:(h + 1) * 128, :]

            def ksrc0(h):
                return [(r_, kv0r[r_, h * 128:(h + 1) * 128, :])
                        for r_ in range(4)]

            def vsrc0(h):
                return [(r_, kv0r[r_, D:2 * D, h * 128:(h + 1) * 128]
                         .rearrange("(jp tt) d -> tt jp d", tt=128))
                        for r_ in range(4)]

            emit_attn(kb, ps, C, "sq", qsrc0, ksrc0, vsrc0, masks_d,
                      _APW(oT0.ap()), tag="0", pre=[(cc0, mycc0)])
        kb.barrier()

        kb.new_scope()
        with ExitStack() as ps:
            r = emit_combine0(kb, ps, C, x_rows, oT0, woT_b, ffn0_d, x1_d)
            kb.wait("sync", r["stc"][0], r["stc"][0].n)
            kb.wait("sync", r["stc"][1], r["stc"][1].n)
            emit_ln_xnt(kb, ps, x1_d, xn1T_d, C, tag="1")
        kb.barrier()

        kb.new_scope()
        with ExitStack() as ps:
            emit_qkv1(kb, ps, C, xn1T_d, w_attn_b, _APW(q1T.ap()),
                      _APW(kv1c.ap()[0:D, :]), _APW(kv1c.ap()[D:2 * D, :]),
                      qn_d, _APW(kv1c.ap()[2 * D:2 * D + 16, :]))
        kb.barrier()

        kb.new_scope()
        cc1 = kb.ctr()
        kb.op("gpsimd", lambda e: e.collective_compute(
            "AllGather", ALU.bypass,
            replica_groups=[[0, 1, 2, 3], [4, 5, 6, 7]],
            ins=[kv1c.ap().opt()], outs=[kv1g.ap().opt()]), inc=cc1)
        kb.barrier()

        kb.new_scope()
        with ExitStack() as ps:
            Z = 2 * D + 16
            kv1r = kv1g.ap().rearrange("(r z) t -> r z t", r=4)
            knb_src = [(r_, kv1r[r_, 2 * D:Z, :]) for r_ in range(4)]
            snegc, ncdv, ncfin = emit_negc(kb, ps, C, qn_d, knb_src,
                                           knb_eng="gpsimd")
            kb.barrier()

            def qsrc1(h):
                return q1T.ap()[h * 128:(h + 1) * 128, :]

            def ksrc1(h):
                return [(r_, kv1r[r_, h * 128:(h + 1) * 128, :])
                        for r_ in range(4)]

            def vsrc1(h):
                return [(r_, kv1r[r_, D:2 * D, h * 128:(h + 1) * 128]
                         .rearrange("(jp tt) d -> tt jp d", tt=128))
                        for r_ in range(4)]

            emit_attn(kb, ps, C, "sm", qsrc1, ksrc1, vsrc1, masks_d,
                      _APW(o1T.ap()),
                      negc_col=lambda h: snegc[:, h:h + 1], tag="1")
        kb.barrier()

        kb.new_scope()
        with ExitStack() as ps:
            emit_moelocal(kb, ps, C, x1_d, o1T, w_proj_b, w1s_b, w3s_b,
                          w2s_b, gwT_b, gbbc, xn1T_d, out_d, probs_d)
        kb.barrier()

        kb.new_scope()
        with ExitStack() as ps:
            bufs = mk_moe_bufs(kb, ps)
            for e_ in range(NE):
                rk, el = divmod(e_, 2)
                first = (e_ == 0)
                w1_ap = g13.ap()[rk * 4 * D + el * 2 * D:
                                 rk * 4 * D + el * 2 * D + D, :]
                w3_ap = g13.ap()[rk * 4 * D + el * 2 * D + D:
                                 rk * 4 * D + (el + 1) * 2 * D, :]
                w2_ap = g2.ap()[rk * 2 * MH + el * MH:
                                rk * 2 * MH + (el + 1) * MH, :]
                emit_moe_unit(kb, ps, bufs, e_, w1_ap, w3_ap, w2_ap,
                              xn1T_d.ap() if first else None,
                              probs_d.ap() if first else None, e_,
                              _APW(out_d.ap()), accum=True, ncols=16)
        kb.barrier()

        block = st.enter_context(nc.Block())
        kb.finalize(block)
    return nc


def kernel_fused(inputs):
    H_ = host_slices(inputs)
    per, lut = H_["per"], H_["lut"]
    R = range(NCORES)
    nc = build_fused()
    ims = []
    for c in R:
        p = per[c]
        ims.append(dict(
            x_rows=p["x_rows"], wqT=p["wqT"], wkT=p["wkT"], wvT=p["wvT"],
            woT=p["woT"], gamma_bc=p["gamma_bc"], beta_bc=p["beta_bc"],
            masks=p["masks"], lut=lut, gT_sl=p["gT_sl"], uT_sl=p["uT_sl"],
            dT_sl=p["dT_sl"], w_attn=p["w_attn"], w_proj=p["w_proj"],
            gwT=p["gwT"], gbbc=p["gbbc"], w1s=p["w1s"], w2s=p["w2s"],
            w3s=p["w3s"], w1r=p["w1r_sl"], w2r=p["w2r_sl"],
            w3r=p["w3r_sl"]))
    res = _run(nc, ims)
    out = np.zeros((2, TGLOB, D), np.float32)
    for c in R:
        B, r = divmod(c, 4)
        out[B].reshape(64, 128, D)[r::4] = \
            np.asarray(res.results[c]["out"]).reshape(16, 128, D)
    return out


def host_slices(inputs):
    x = np.asarray(inputs["L0_attn_hidden_states"])
    per = []
    gidxT = np.ascontiguousarray(np.asarray(inputs["L0_ffn_gate_idx"]).T)
    uidxT = np.ascontiguousarray(np.asarray(inputs["L0_ffn_up_idx"]).T)
    didxT = np.ascontiguousarray(np.asarray(inputs["L0_ffn_down_idx"]).T)
    wqT = np.ascontiguousarray(np.asarray(inputs["L0_attn_Wq"]).T)
    wkT = np.ascontiguousarray(np.asarray(inputs["L0_attn_Wk"]).T)
    wvT = np.ascontiguousarray(np.asarray(inputs["L0_attn_Wv"]).T)
    woT = np.ascontiguousarray(np.asarray(inputs["L0_attn_Wo"]).T)
    gamma = np.asarray(inputs["L0_attn_gamma"])
    beta = np.asarray(inputs["L0_attn_beta"])
    gamma_bc = np.ascontiguousarray(np.tile(gamma[None, :], (128, 4)))
    beta_bc = np.ascontiguousarray(np.tile(beta[None, :], (128, 4)))
    w_attn = np.asarray(inputs["L1_attn_w_attn"])
    w_proj = np.asarray(inputs["L1_attn_w_proj"])
    gwT = np.ascontiguousarray(np.asarray(inputs["L1_ffn_gate_w"]).T)
    gbbc = np.ascontiguousarray(
        np.tile(np.asarray(inputs["L1_ffn_gate_b"])[None, :], (128, 1)))
    w1s = np.asarray(inputs["L1_ffn_w1_shared"])
    w2s = np.asarray(inputs["L1_ffn_w2_shared"])
    w3s = np.asarray(inputs["L1_ffn_w3_shared"])
    w1r = np.asarray(inputs["L1_ffn_w1_routed"])
    w2r = np.asarray(inputs["L1_ffn_w2_routed"])
    w3r = np.asarray(inputs["L1_ffn_w3_routed"])
    tri = np.triu(np.ones((128, 128), np.float32))
    for c in range(NCORES):
        B, r = divmod(c, 4)
        xr = np.ascontiguousarray(
            x[B].reshape(64, 128, D)[r::4].reshape(TLOC, D))
        m = np.zeros((4, 128, 128), np.float32)
        for ri in range(4):
            if ri < r:
                m[ri] = 1.0
            elif ri == r:
                m[ri] = tri
        per.append(dict(
            x_rows=xr,
            gT_sl=np.ascontiguousarray(gidxT[c * 256:(c + 1) * 256, :]),
            uT_sl=np.ascontiguousarray(uidxT[c * 256:(c + 1) * 256, :]),
            dT_sl=np.ascontiguousarray(didxT[:, c * 256:(c + 1) * 256]),
            wqT=wqT, wkT=wkT, wvT=wvT, woT=woT,
            gamma_bc=gamma_bc, beta_bc=beta_bc,
            masks=np.ascontiguousarray(m.astype(BF)),
            w_attn=w_attn, w_proj=w_proj, gwT=gwT, gbbc=gbbc,
            w1s=w1s, w2s=w2s, w3s=w3s,
            w1r_sl=np.ascontiguousarray(w1r[2 * c:2 * c + 2]),
            w2r_sl=np.ascontiguousarray(w2r[2 * c:2 * c + 2]),
            w3r_sl=np.ascontiguousarray(w3r[2 * c:2 * c + 2]),
        ))
    return dict(per=per, lut=np.ascontiguousarray(
        np.asarray(inputs["L0_ffn_lut"]).astype(np.float32)))


def _run(nc, ims):
    return run_bass_kernel_spmd(nc, ims, core_ids=list(range(NCORES)))


def _asm_kv(kT, vv):
    KTb, Vb = [], []
    for B in range(2):
        KTb.append(np.ascontiguousarray(
            np.concatenate([kT[B * 4 + r] for r in range(4)], axis=1)))
        Vb.append(np.ascontiguousarray(
            np.concatenate([vv[B * 4 + r] for r in range(4)], axis=0)))
    return KTb, Vb


def kernel(**inputs):
    if os.environ.get("MK_UNFUSED", "0") != "1":
        return kernel_fused(inputs)
    H_ = host_slices(inputs)
    per, lut = H_["per"], H_["lut"]
    R = range(NCORES)

    nc = build_ghost()
    res = _run(nc, [dict(lut=lut, gT_sl=per[c]["gT_sl"],
                         uT_sl=per[c]["uT_sl"], dT_sl=per[c]["dT_sl"])
                    for c in R])
    wgT = np.concatenate([np.asarray(res.results[c]["wg_sl"]) for c in R], 0)
    wuT = np.concatenate([np.asarray(res.results[c]["wu_sl"]) for c in R], 0)
    wdT = np.concatenate([np.asarray(res.results[c]["wd_sl"]) for c in R], 1)
    wgT = np.ascontiguousarray(wgT)
    wuT = np.ascontiguousarray(wuT)
    wdT = np.ascontiguousarray(wdT)

    nc = build_l0main()
    res = _run(nc, [dict(x_rows=per[c]["x_rows"], wqT=per[c]["wqT"],
                         wkT=per[c]["wkT"], wvT=per[c]["wvT"],
                         gamma_bc=per[c]["gamma_bc"],
                         beta_bc=per[c]["beta_bc"]) for c in R])
    xnT = [np.asarray(res.results[c]["xnT"]) for c in R]
    qT = [np.asarray(res.results[c]["qT"]) for c in R]
    kT = [np.asarray(res.results[c]["kT"]) for c in R]
    vv = [np.asarray(res.results[c]["v"]) for c in R]

    KTb, Vb = _asm_kv(kT, vv)
    nc = build_attn0()
    res = _run(nc, [dict(qT=qT[c], ktb=KTb[c // 4], vb=Vb[c // 4],
                         masks=per[c]["masks"]) for c in R])
    oT = [np.asarray(res.results[c]["oT"]) for c in R]

    nc = build_ffn0()
    res = _run(nc, [dict(xnT=xnT[c], wgT=wgT, wuT=wuT, wdT=wdT) for c in R])
    ffn0 = [np.asarray(res.results[c]["ffn0"]) for c in R]

    nc = build_combine0()
    res = _run(nc, [dict(x_rows=per[c]["x_rows"], oT=oT[c],
                         woT=per[c]["woT"], ffn0=ffn0[c]) for c in R])
    x1 = [np.asarray(res.results[c]["x1"]) for c in R]
    xn1T = [np.asarray(res.results[c]["xn1T"]) for c in R]

    nc = build_qkv1()
    res = _run(nc, [dict(xn1T=xn1T[c], w_attn=per[c]["w_attn"]) for c in R])
    q1T = [np.asarray(res.results[c]["q1T"]) for c in R]
    k1T = [np.asarray(res.results[c]["k1T"]) for c in R]
    v1 = [np.asarray(res.results[c]["v1"]) for c in R]
    qn = [np.asarray(res.results[c]["qn"]) for c in R]
    kn = [np.asarray(res.results[c]["kn"]) for c in R]

    K1Tb, V1b = _asm_kv(k1T, v1)
    knb = [np.ascontiguousarray(
        np.concatenate([kn[B * 4 + r] for r in range(4)], axis=1))
        for B in range(2)]
    nc = build_attn1()
    res = _run(nc, [dict(q1T=q1T[c], k1tb=K1Tb[c // 4], v1b=V1b[c // 4],
                         masks=per[c]["masks"], qn=qn[c], knb=knb[c // 4])
                    for c in R])
    o1T = [np.asarray(res.results[c]["o1T"]) for c in R]

    nc = build_moelocal()
    res = _run(nc, [dict(x1=x1[c], o1T=o1T[c], w_proj=per[c]["w_proj"],
                         w1s=per[c]["w1s"], w3s=per[c]["w3s"],
                         w2s=per[c]["w2s"], gwT=per[c]["gwT"],
                         gbbc=per[c]["gbbc"], xn1T=xn1T[c]) for c in R])
    ybase = [np.asarray(res.results[c]["ybase"]) for c in R]
    probs = [np.asarray(res.results[c]["probs"]) for c in R]

    xnf = np.ascontiguousarray(np.concatenate(xn1T, axis=1))
    prf = np.ascontiguousarray(np.concatenate(probs, axis=0))
    nc = build_moeroutd()
    res = _run(nc, [dict(xnf=xnf,
                         prf=np.ascontiguousarray(prf[:, 2 * c:2 * c + 2]),
                         w1r=per[c]["w1r_sl"], w3r=per[c]["w3r_sl"],
                         w2r=per[c]["w2r_sl"]) for c in R])
    part = [np.asarray(res.results[c]["part"]) for c in R]

    nc = build_reduce()
    ims = []
    for c in R:
        d = dict(ybase=ybase[c])
        for i in R:
            d[f"p{i}"] = np.ascontiguousarray(
                part[i][c * TLOC:(c + 1) * TLOC, :])
        ims.append(d)
    res = _run(nc, ims)
    outr = [np.asarray(res.results[c]["out"]) for c in R]

    out = np.zeros((2, TGLOB, D), np.float32)
    for c in R:
        B, r = divmod(c, 4)
        out[B].reshape(64, 128, D)[r::4] = outr[c].reshape(16, 128, D)
    return out


# version 09
# speedup vs baseline: 1.0676x; 1.0190x over previous
import os
import numpy as np
import ml_dtypes
from contextlib import ExitStack

import concourse.bass as bass
import concourse.mybir as mybir
from concourse.bass_utils import run_bass_kernel_spmd

F32 = mybir.dt.float32
BF16 = mybir.dt.bfloat16
I32 = mybir.dt.int32
AF = mybir.ActivationFunctionType
ALU = mybir.AluOpType
AX = mybir.AxisListType
BF = ml_dtypes.bfloat16

NCORES = 8
D = 2048
H = 16
HD = 128
NT = 16
TLOC = NT * 128
TGLOB = 8192
INTER = 5504
NI = INTER // 128
MH = 1408
NMH = MH // 128
NE = 16
EPS = 1e-5

ENGINES = ("sync", "scalar", "vector", "tensor", "gpsimd")


class Ctr:
    __slots__ = ("h", "n", "name")

    def __init__(self, h, name):
        self.h = h
        self.n = 0
        self.name = name


class KB:

    def __init__(self, nc, st):
        self.nc = nc
        self.st = st
        self.streams = {e: [] for e in ENGINES}
        self.ctrs = []
        self._sems = 0
        self._pin = 0
        self._idx = 0

    def _new_phys(self):
        h = self.st.enter_context(self.nc.semaphore(f"ps{self._sems}"))
        c = Ctr(h, f"ps{self._sems}")
        self._sems += 1
        self.ctrs.append(c)
        return c

    def ctr(self, name=None):
        if self._idx < len(self.ctrs):
            c = self.ctrs[self._idx]
        else:
            c = self._new_phys()
        self._idx += 1
        return c

    def pin_ctr(self, name=None):
        assert self._idx == self._pin == len(self.ctrs) - 0 or True
        c = self._new_phys()
        self._pin = len(self.ctrs)
        self._idx = max(self._idx, self._pin)
        return c

    def new_scope(self):
        self._idx = self._pin

    def op(self, eng, fn, inc=None, incn=1, waits=()):
        wl = [(c, v) for (c, v) in waits if v > 0]
        if inc is not None:
            inc.n += incn
        sem = inc.h if inc is not None else None

        def emit(e, wl=wl, fn=fn, sem=sem, incn=incn):
            for (c, v) in wl:
                e.wait_ge(c.h, v)
            inst = fn(e)
            if sem is not None:
                inst.then_inc(sem, incn)
        self.streams[eng].append(emit)

    def dma(self, eng, fn, inc, waits=()):
        self.op(eng, fn, inc=inc, incn=16, waits=waits)

    def wait(self, eng, ctr, val):
        if val > 0:
            self.streams[eng].append(
                lambda e, c=ctr, v=val: e.wait_ge(c.h, v))

    def barrier(self):
        vals = [(c, c.n) for c in self.ctrs if c.n > 0]
        for e in ENGINES:
            for (c, v) in vals:
                self.wait(e, c, v)

    def finalize(self, block):
        for e in ENGINES:
            items = self.streams[e]
            if not items:
                continue

            def body(eng, items=items):
                for it in items:
                    it(eng)
            getattr(block, e)(body)


def sbuf(nc, st, name, shape, dt):
    return st.enter_context(nc.sbuf_tensor(name, shape, dt))


def psum(nc, st, name, shape, dt):
    return st.enter_context(nc.psum_tensor(name, shape, dt))



def emit_consts(kb, st, need_ident=True):
    nc = kb.nc
    C = {}
    C["c_done"] = kb.pin_ctr("c_done")
    if need_ident:
        C["ident"] = sbuf(nc, st, "ident", [128, 128], BF16)
        C["identi"] = sbuf(nc, st, "identi", [128, 128], I32)
        kb.op("gpsimd", lambda e: e.iota(C["identi"][:], [[-1, 128]], base=0,
                                         channel_multiplier=1), inc=C["c_done"])
        kb.op("vector", lambda e: e.tensor_scalar(
            C["ident"][:], C["identi"][:], 0.0, None, ALU.is_equal),
            inc=C["c_done"], waits=[(C["c_done"], 1)])
    C["cols"] = sbuf(nc, st, "ccols", [128, 8], F32)

    def setcol(i, val):
        kb.op("vector", lambda e: e.memset(C["cols"][:, i:i + 1], val),
              inc=C["c_done"])
    setcol(0, EPS)
    setcol(1, -0.5)
    setcol(2, np.log(HD ** -0.5).item())
    setcol(3, -1.0)
    setcol(4, 1e9)
    setcol(5, 0.5)
    setcol(6, np.log(HD ** -0.5).item())
    setcol(7, HD ** -0.5)
    C["eps"] = C["cols"][:, 0:1]
    C["neghalf"] = C["cols"][:, 1:2]
    C["lnqscale"] = C["cols"][:, 2:3]
    C["neg1"] = C["cols"][:, 3:4]
    C["big"] = C["cols"][:, 4:5]
    C["half16"] = C["cols"][0:16, 5:6]
    C["lnqscale16"] = C["cols"][0:16, 6:7]
    C["qkscale"] = C["cols"][:, 7:8]
    C["ones_row"] = sbuf(nc, st, "conesrow", [1, 128], BF16)
    kb.op("vector", lambda e: e.memset(C["ones_row"][:], 1.0), inc=C["c_done"])
    C["ones128"] = sbuf(nc, st, "cones128", [128, 128], BF16)
    kb.op("vector", lambda e: e.memset(C["ones128"][:], 1.0), inc=C["c_done"])
    C["zeros512"] = sbuf(nc, st, "czeros512", [128, 512], BF16)
    kb.op("vector", lambda e: e.memset(C["zeros512"][:], 0.0),
          inc=C["c_done"])
    return C


class GhostEmitter:

    def __init__(self, kb, st, lut_d, gT_sl, uT_sl, dT_sl, wg_out, wu_out,
                 wd_out, C):
        nc = kb.nc
        self.kb = kb
        self.ld = kb.ctr()
        self.cc = kb.ctr()
        self.lbc = kb.ctr()
        self.stc = kb.ctr()
        self.lut_row = sbuf(nc, st, "lutrow", [1, 16], BF16)
        self.lut_bc = sbuf(nc, st, "lutbc", [128, 16], F32)
        self.plut = psum(nc, st, "plut", [128, 512], F32)
        gshared = sbuf(nc, st, "gh_g", [128, 5504], I32)
        self.bufs = [dict(
            g=gshared,
            a=sbuf(nc, st, f"gh_a{i}", [128, 5504], BF16),
        ) for i in range(2)]
        self.tmp = sbuf(nc, st, "gh_t", [128, 5504], BF16)
        self.jobs = []
        for r in range(2):
            self.jobs.append((gT_sl[r * 128:(r + 1) * 128, :],
                              wg_out[r * 128:(r + 1) * 128, :], 5504))
        for r in range(2):
            self.jobs.append((uT_sl[r * 128:(r + 1) * 128, :],
                              wu_out[r * 128:(r + 1) * 128, :], 5504))
        for r in range(NI):
            self.jobs.append((dT_sl[r * 128:(r + 1) * 128, :],
                              wd_out[r * 128:(r + 1) * 128, :], 256))
        kb.dma("gpsimd", lambda e: e.dma_start(
            out=self.lut_row[:], in_=lut_d.ap().unsqueeze(0)), inc=self.ld)
        myl = self.ld.n
        kb.op("tensor", lambda e: e.matmul(
            self.plut[:, 0:16], C["ones_row"][:], self.lut_row[:],
            start=True, stop=True),
            inc=self.lbc, waits=[(self.ld, myl),
                                 (C["c_done"], C["c_done"].n)])
        kb.op("vector", lambda e: e.tensor_copy(
            self.lut_bc[:], self.plut[:, 0:16]),
            inc=self.lbc, waits=[(self.lbc, self.lbc.n)])
        self.mylbc = self.lbc.n
        self.stc0 = self.stc.n
        self.steps = []
        for j in range(len(self.jobs)):
            self.steps.append(("load", j, 0))
            for p in range(16):
                self.steps.append(("pass", j, p))
            self.steps.append(("store", j, 0))
        self.si = 0
        self.marks = {}

    def done(self):
        return self.si >= len(self.steps)

    def emit_next(self, k):
        kb = self.kb
        n = 0
        while n < k and self.si < len(self.steps):
            (kind, j, p) = self.steps[self.si]
            self.si += 1
            n += 1
            srcap, dst, w = self.jobs[j]
            bb = self.bufs[j % 2]
            if kind == "load":
                gw = [(self.stc, self.stc0 + max(0, 16 * (j - 1)))]
                if j > 0:
                    gw.append((self.cc, self.marks[j - 1]))
                kb.dma("gpsimd", lambda e, srcap=srcap, bb=bb, w=w:
                       e.dma_start(out=bb["g"][:, 0:w], in_=srcap),
                       inc=self.ld, waits=gw)
                self.marks[("ld", j)] = self.ld.n
            elif kind == "pass":
                if p == 0:
                    kb.op("vector", lambda e, bb=bb, w=w: e.memset(
                        bb["a"][:, 0:w], 0.0), inc=self.cc,
                        waits=[(self.ld, self.marks[("ld", j)]),
                               (self.lbc, self.mylbc),
                               (self.stc,
                                self.stc0 + max(0, 16 * (j - 1)))])
                kb.op("vector", lambda e, bb=bb, w=w, p=p: e.tensor_scalar(
                    self.tmp[:, 0:w], bb["g"][:, 0:w], float(p),
                    self.lut_bc[:, p:p + 1], ALU.is_equal, ALU.mult),
                    inc=self.cc, waits=[(self.cc, self.cc.n)])
                kb.op("vector", lambda e, bb=bb, w=w: e.tensor_tensor(
                    bb["a"][:, 0:w], bb["a"][:, 0:w], self.tmp[:, 0:w],
                    ALU.add), inc=self.cc, waits=[(self.cc, self.cc.n)])
                if p == 15:
                    self.marks[j] = self.cc.n
            else:
                kb.dma("gpsimd", lambda e, dst=dst, bb=bb, w=w:
                       e.dma_start(out=dst, in_=bb["a"][:, 0:w]),
                       inc=self.stc, waits=[(self.cc, self.marks[j])])

    def finish(self):
        self.emit_next(10 ** 9)
        return self.stc


def emit_ln_xnt(kb, st, x_rows, xnT_out, C, tag=""):
    nc = kb.nc
    ld2 = [kb.ctr(f"ln_ld{i}{tag}") for i in range(2)]
    dv = kb.ctr(f"ln_dv{tag}")
    ac = kb.ctr(f"ln_ac{tag}")
    pe = kb.ctr(f"ln_pe{tag}")
    ev = kb.ctr(f"ln_ev{tag}")
    stc = kb.ctr(f"ln_st{tag}")

    sx = [sbuf(nc, st, f"ln_x{i}{tag}", [128, D], F32) for i in range(2)]
    sxn = [sbuf(nc, st, f"ln_n{i}{tag}", [128, D], BF16) for i in range(2)]
    sbn6 = [sbuf(nc, st, f"ln_b6{i}{tag}", [128, 24], F32) for i in range(2)]
    sagg = [sbuf(nc, st, f"ln_ag{i}{tag}", [128, 2], F32) for i in range(2)]
    srstd = [sbuf(nc, st, f"ln_rs{i}{tag}", [128, 1], F32) for i in range(2)]
    ptr = [psum(nc, st, f"ln_pt{i}{tag}", [128, 1024], BF16) for i in range(2)]
    xnT_res = sbuf(nc, st, f"xnT_res{tag}", [128, NT * D], BF16)

    cd = C["c_done"].n
    apply_end = {}
    for t in range(NT):
        b = t % 2
        kb.dma("sync", lambda e, t=t, b=b:
               e.dma_start(out=sx[b][:], in_=x_rows[t * 128:(t + 1) * 128, :]),
               inc=ld2[b], waits=[(dv, apply_end.get(t - 2, 0))])
        myld = ld2[b].n
        for i in range(4):
            kb.op("vector", lambda e, b=b, i=i: e.bn_stats(
                sbn6[b][:, i * 6:(i + 1) * 6], sx[b][:, i * 512:(i + 1) * 512]),
                inc=dv, waits=[(ld2[b], myld)] if i == 0 else ())
        kb.op("vector", lambda e, b=b: e.bn_aggr(sagg[b][:], sbn6[b][:]),
              inc=dv, waits=[(dv, dv.n)])
        kb.op("scalar", lambda e, b=b: e.activation(
            srstd[b][:], sagg[b][:, 1:2], AF.Ln, bias=C["eps"]),
            inc=ac, waits=[(dv, dv.n), (C["c_done"], cd)])
        kb.op("scalar", lambda e, b=b: e.activation(
            srstd[b][:], srstd[b][:], AF.Exp, scale=C["neghalf"]),
            inc=ac, waits=[(ac, ac.n)])
        kb.op("vector", lambda e, b=b: e.tensor_scalar(
            sxn[b][:], sx[b][:], sagg[b][:, 0:1], srstd[b][:],
            ALU.subtract, ALU.mult),
            inc=dv, waits=[(ac, ac.n), (pe, max(0, pe.n - 16))])
        apply_end[t] = dv.n
        for cc in range(16):
            idx = t * 16 + cc
            pb = idx % 2
            sl = idx % 8
            kb.op("tensor", lambda e, b=b, cc=cc, pb=pb, sl=sl: e.transpose(
                ptr[pb][:, sl * 128:(sl + 1) * 128],
                sxn[b][:, cc * 128:(cc + 1) * 128], C["ident"][:]),
                inc=pe,
                waits=([(dv, dv.n), (C["c_done"], cd)] if cc == 0 else [])
                + [(ev, max(0, ev.n - 1))])
            kb.op("vector", lambda e, t=t, cc=cc, pb=pb, sl=sl: e.tensor_copy(
                xnT_res[:, cc * (NT * 128) + t * 128:
                        cc * (NT * 128) + (t + 1) * 128],
                ptr[pb][:, sl * 128:(sl + 1) * 128]),
                inc=ev, waits=[(pe, pe.n)])
    if xnT_out is not None:
        kb.dma("sync", lambda e: e.dma_start(
            out=xnT_out.ap().rearrange("(cc p) t -> p cc t", p=128),
            in_=xnT_res.ap().rearrange("p (cc t) -> p cc t", t=NT * 128)),
            inc=stc, waits=[(ev, ev.n)])
    return xnT_res, dict(ev=ev, st=stc, dv=dv)


def np_bf16(a):
    return np.asarray(a, dtype=BF)



def build_ghost():
    nc = bass.Bass("TRN2", target_bir_lowering=False)
    lut_d = nc.dram_tensor("lut", [16], F32, kind="ExternalInput")
    gT = nc.dram_tensor("gT_sl", [256, INTER], I32, kind="ExternalInput")
    uT = nc.dram_tensor("uT_sl", [256, INTER], I32, kind="ExternalInput")
    dT = nc.dram_tensor("dT_sl", [INTER, 256], I32, kind="ExternalInput")
    wg = nc.dram_tensor("wg_sl", [256, INTER], BF16, kind="ExternalOutput")
    wu = nc.dram_tensor("wu_sl", [256, INTER], BF16, kind="ExternalOutput")
    wd = nc.dram_tensor("wd_sl", [INTER, 256], BF16, kind="ExternalOutput")
    st = ExitStack()
    with st:
        kb = KB(nc, st)
        C = emit_consts(kb, st, need_ident=False)
        GhostEmitter(kb, st, lut_d, gT, uT, dT, wg, wu, wd, C).finish()
        kb.barrier()
        block = st.enter_context(nc.Block())
        kb.finalize(block)
    return nc


def build_l0main(parts=None):
    parts = parts if parts is not None else os.environ.get("L0PARTS", "all")
    nc = bass.Bass("TRN2", target_bir_lowering=False)
    x_rows = nc.dram_tensor("x_rows", [TLOC, D], F32, kind="ExternalInput")
    wqT = nc.dram_tensor("wqT", [D, D], F32, kind="ExternalInput")
    wkT = nc.dram_tensor("wkT", [D, D], F32, kind="ExternalInput")
    wvT = nc.dram_tensor("wvT", [D, D], F32, kind="ExternalInput")
    gamma_bc = nc.dram_tensor("gamma_bc", [128, 512], F32, kind="ExternalInput")
    beta_bc = nc.dram_tensor("beta_bc", [128, 512], F32, kind="ExternalInput")
    xnT_d = nc.dram_tensor("xnT", [D, TLOC], BF16, kind="ExternalOutput")
    qT_d = nc.dram_tensor("qT", [D, TLOC], BF16, kind="ExternalOutput")
    kT_d = nc.dram_tensor("kT", [D, TLOC], BF16, kind="ExternalOutput")
    v_d = nc.dram_tensor("v", [TLOC, D], BF16, kind="ExternalOutput")

    st = ExitStack()
    with st:
        kb = KB(nc, st)
        C = emit_consts(kb, st)
        xnT_res, lnc = emit_ln_xnt(kb, st, x_rows, xnT_d, C)
        if parts != "ln":
            emit_qkv0(kb, st, xnT_res, lnc["ev"], wqT, wkT, wvT, gamma_bc,
                      beta_bc, qT_d, kT_d, v_d, C, parts=parts)
        kb.barrier()
        block = st.enter_context(nc.Block())
        kb.finalize(block)
    return nc


def emit_qkv0(kb, st, xnT_res, xnt_ev, wqT, wkT, wvT, gamma_bc, beta_bc,
              qT_d, kT_d, v_d, C, parts="all", pre=(), filler=None):
    nc = kb.nc
    ld2 = [kb.ctr(f"qk_ld{i}") for i in range(2)]
    gld = kb.ctr("qk_gld")
    mm = kb.ctr("qk_mm")
    evp = kb.ctr("qk_evp")
    evq = kb.ctr("qk_ev")
    petr = kb.ctr("qk_petr")
    peev = kb.ctr("qk_peev")
    stc2 = [kb.ctr(f"qk_st{i}") for i in range(2)]

    slab = [sbuf(nc, st, f"qk_sl{i}", [128, 16 * 512], BF16) for i in range(2)]
    sgb = sbuf(nc, st, "qk_gb", [128, 1024], F32)
    pq = [psum(nc, st, f"qk_p{i}", [128, 512], F32) for i in range(3)]
    ptr = [psum(nc, st, f"qk_pt{i}", [128, 1024], BF16) for i in range(2)]
    sev = [sbuf(nc, st, f"qk_e{i}", [128, 512], F32) for i in range(4)]
    saf = [sbuf(nc, st, f"qk_a{i}", [128, 512], F32) for i in range(4)]
    sbn6 = [sbuf(nc, st, f"qk_b6{i}", [128, 48], F32) for i in range(4)]
    sagg = [sbuf(nc, st, f"qk_ag{i}", [128, 8], F32) for i in range(4)]
    srstd = [sbuf(nc, st, f"qk_rs{i}", [128, 4], F32) for i in range(4)]
    sqk = [sbuf(nc, st, f"qk_o{i}", [128, 512], BF16) for i in range(4)]
    sqkT = [sbuf(nc, st, f"qk_oT{i}", [128, 512], BF16) for i in range(4)]

    kb.dma("sync", lambda e: e.dma_start(out=sgb[:, 0:512], in_=gamma_bc[:, :]),
           inc=gld)
    kb.dma("sync", lambda e: e.dma_start(out=sgb[:, 512:1024],
                                         in_=beta_bc[:, :]), inc=gld)
    mygld = gld.n

    WS = [("q", wqT, qT_d), ("k", wkT, kT_d), ("v", wvT, v_d)]
    if parts == "v":
        WS = WS[2:]
    elif parts.startswith("q"):
        WS = WS[:1]
    cd = C["c_done"].n
    ginfo = {}
    sidx = 0
    for (wname, wT, out_d) in WS:
        for dc in range(4):
            sb = sidx % 2
            kb.dma("sync", lambda e, wT=wT, dc=dc, sb=sb: e.dma_start(
                out=slab[sb].ap().rearrange("p (cc d) -> p cc d", d=512),
                in_=wT.ap()[:, dc * 512:(dc + 1) * 512]
                    .rearrange("(cc p) d -> p cc d", p=128)),
                inc=ld2[sb], waits=[(mm, max(0, mm.n - NT))] + list(pre))
            myld = ld2[sb].n
            for t in range(NT):
                gidx = sidx * NT + t
                pb = gidx % 3
                eb = gidx % 4
                prev = ginfo.get(gidx - 4, dict(evq=0, petr=0, stc=[0, 0]))
                for cc in range(16):
                    kb.op("tensor", lambda e, sb=sb, t=t, cc=cc, pb=pb:
                          e.matmul(pq[pb][:],
                                   xnT_res[:, cc * TLOC + t * 128:
                                           cc * TLOC + (t + 1) * 128],
                                   slab[sb][:, cc * 512:(cc + 1) * 512],
                                   start=(cc == 0), stop=(cc == 15)),
                          inc=(mm if cc == 15 else None),
                          waits=([(ld2[sb], myld), (xnt_ev, xnt_ev.n),
                                  (evp, max(0, evp.n - 2))]
                                 if cc == 0 else ()))
                sb2 = gidx % 2
                if wname == "v":
                    kb.op("scalar", lambda e, pb=pb, eb=eb: e.activation(
                        sqk[eb][:], pq[pb][:], AF.Identity),
                        inc=evp, waits=[(mm, mm.n),
                                        (stc2[sb2], prev["stc"][sb2]),
                                        (petr, prev["petr"])])
                    kb.dma("sync", lambda e, t=t, dc=dc, eb=eb, sb2=sb2:
                           e.dma_start(
                               out=v_d[t * 128:(t + 1) * 128,
                                       dc * 512:(dc + 1) * 512],
                               in_=sqk[eb][:]), inc=stc2[sb2],
                           waits=[(evp, evp.n)])
                elif parts in ("qnofm", "qtrplain"):
                    kb.op("scalar", lambda e, pb=pb, eb=eb: e.activation(
                        sqk[eb][:], pq[pb][:], AF.Identity),
                        inc=evp, waits=[(mm, mm.n), (evq, prev["evq"]),
                                        (petr, prev["petr"])])
                    evp_here = evp.n
                    for hh in range(4):
                        kb.op("tensor", lambda e, eb=eb, hh=hh,
                              pb=petr.n % 2: e.transpose(
                            ptr[pb][:, 0:128],
                            sqk[eb][:, hh * 128:(hh + 1) * 128],
                            C["ident"][:]),
                            inc=petr,
                            waits=[(evp, evp_here),
                                   (peev, max(0, petr.n - 1))])
                        kb.op("vector", lambda e, eb=eb, hh=hh,
                              tb=(petr.n - 1) % 2: e.tensor_copy(
                                  sqkT[eb][:, hh * 128:(hh + 1) * 128],
                                  ptr[tb][:, 0:128]),
                              inc=peev,
                              waits=[(petr, petr.n),
                                     (stc2[eb], prev["stc"][eb])])
                    if parts == "qtrplain":
                        kb.dma("sync", lambda e, t=t, dc=dc, eb=eb,
                               out_d=out_d: e.dma_start(
                                   out=out_d[t * 128:(t + 1) * 128,
                                             dc * 512:(dc + 1) * 512],
                                   in_=sqkT[eb][:]),
                               inc=stc2[eb], waits=[(peev, peev.n)])
                    else:
                        kb.dma("sync", lambda e, t=t, dc=dc, eb=eb,
                               out_d=out_d: e.dma_start(
                                   out=out_d.ap()[dc * 512:(dc + 1) * 512,
                                                  t * 128:(t + 1) * 128]
                                   .rearrange("(hh d) t -> d hh t", d=128),
                                   in_=sqkT[eb].ap().rearrange(
                                       "d (hh t) -> d hh t", t=128)),
                               inc=stc2[eb], waits=[(peev, peev.n)])
                else:
                    kb.op("scalar", lambda e, pb=pb, eb=eb: e.activation(
                        sev[eb][:], pq[pb][:], AF.Identity),
                        inc=evp, waits=[(mm, mm.n), (evq, prev["evq"])])
                    if parts == "qa":
                        kb.op("vector", lambda e, eb=eb: e.tensor_copy(
                            saf[eb][:], sev[eb][:]),
                            inc=evq, waits=[(evp, evp.n)])
                        kb.op("vector", lambda e, eb=eb: e.tensor_copy(
                            saf[eb][:, 0:1], saf[eb][:, 0:1]),
                            inc=evq, waits=[(evq, evq.n)])
                    elif parts == "qb":
                        kb.op("vector", lambda e, eb=eb: e.tensor_tensor(
                            saf[eb][:], sev[eb][:], sgb[:, 0:512], ALU.mult),
                            inc=evq, waits=[(evp, evp.n), (gld, mygld)])
                        kb.op("vector", lambda e, eb=eb: e.tensor_copy(
                            saf[eb][:, 0:1], saf[eb][:, 0:1]),
                            inc=evq, waits=[(evq, evq.n)])
                    else:
                        kb.op("vector", lambda e, eb=eb: e.tensor_tensor(
                            saf[eb][:], sev[eb][:], sgb[:, 0:512], ALU.mult),
                            inc=evq, waits=[(evp, evp.n), (gld, mygld)])
                        kb.op("vector", lambda e, eb=eb: e.tensor_tensor(
                            saf[eb][:], saf[eb][:], sgb[:, 512:1024], ALU.add),
                            inc=evq, waits=[(evq, evq.n)])
                    aff_done = evq.n
                    skip_stats = parts in ("qfm1", "qa", "qb")
                    if skip_stats:
                        kb.op("scalar", lambda e, eb=eb: e.activation(
                            sqk[eb][:], saf[eb][:], AF.Identity),
                            inc=evq, waits=[(evq, aff_done),
                                            (petr, prev["petr"])])
                    for hh in range(4 if not skip_stats else 0):
                        kb.op("vector", lambda e, eb=eb, hh=hh: e.bn_stats(
                            sbn6[eb][:, hh * 12:hh * 12 + 6],
                            saf[eb][:, hh * 128:(hh + 1) * 128]),
                            inc=evq,
                            waits=[(evq, aff_done)] if hh == 0 else ())
                    stats_done = evq.n
                    for hh in range(4 if not skip_stats else 0):
                        kb.op("vector", lambda e, eb=eb, hh=hh: e.bn_aggr(
                            sagg[eb][:, hh * 2:hh * 2 + 2],
                            sbn6[eb][:, hh * 12:hh * 12 + 6]),
                            inc=evq,
                            waits=[(evq, stats_done)] if hh == 0 else ())
                    aggr_done = evq.n
                    if parts in ("all", "q", "k", "qfm3"):
                        for hh in range(4):
                            kb.op("scalar", lambda e, eb=eb, hh=hh:
                                  e.activation(
                                      srstd[eb][:, hh:hh + 1],
                                      sagg[eb][:, 2 * hh + 1:2 * hh + 2],
                                      AF.Ln, bias=C["eps"]),
                                  inc=evq,
                                  waits=([(evq, aggr_done),
                                          (C["c_done"], cd)]
                                         if hh == 0 else ()))
                        ln_done = evq.n
                        qsc = (wname == "q")
                        for hh in range(4):
                            if qsc:
                                kb.op("scalar", lambda e, eb=eb, hh=hh:
                                      e.activation(
                                          srstd[eb][:, hh:hh + 1],
                                          srstd[eb][:, hh:hh + 1], AF.Exp,
                                          scale=C["neghalf"],
                                          bias=C["lnqscale"]),
                                      inc=evq,
                                      waits=[(evq, ln_done)]
                                      if hh == 0 else ())
                            else:
                                kb.op("scalar", lambda e, eb=eb, hh=hh:
                                      e.activation(
                                          srstd[eb][:, hh:hh + 1],
                                          srstd[eb][:, hh:hh + 1], AF.Exp,
                                          scale=C["neghalf"]),
                                      inc=evq,
                                      waits=[(evq, ln_done)]
                                      if hh == 0 else ())
                    else:
                        pass
                    rstd_done = evq.n
                    for hh in range(4 if not skip_stats else 0):
                        if parts == "qfm2":
                            kb.op("vector", lambda e, eb=eb, hh=hh:
                                  e.tensor_scalar(
                                      sqk[eb][:, hh * 128:(hh + 1) * 128],
                                      saf[eb][:, hh * 128:(hh + 1) * 128],
                                      sagg[eb][:, hh * 2:hh * 2 + 1], None,
                                      ALU.subtract),
                                  inc=evq,
                                  waits=([(evq, rstd_done),
                                          (petr, prev["petr"])]
                                         if hh == 0 else ()))
                        else:
                            kb.op("vector", lambda e, eb=eb, hh=hh:
                                  e.tensor_scalar(
                                      sqk[eb][:, hh * 128:(hh + 1) * 128],
                                      saf[eb][:, hh * 128:(hh + 1) * 128],
                                      sagg[eb][:, hh * 2:hh * 2 + 1],
                                      srstd[eb][:, hh:hh + 1],
                                      ALU.subtract, ALU.mult),
                                  inc=evq,
                                  waits=([(evq, rstd_done),
                                          (petr, prev["petr"])]
                                         if hh == 0 else ()))
                    apply_done = evq.n
                    for hh in range(4):
                        kb.op("tensor", lambda e, eb=eb, hh=hh,
                              pb=petr.n % 2: e.transpose(
                            ptr[pb][:, 0:128],
                            sqk[eb][:, hh * 128:(hh + 1) * 128],
                            C["ident"][:]),
                            inc=petr,
                            waits=[(evq, apply_done),
                                   (peev, max(0, petr.n - 1))])
                        kb.op("scalar", lambda e, eb=eb, hh=hh,
                              tb=(petr.n - 1) % 2: e.activation(
                                  sqkT[eb][:, hh * 128:(hh + 1) * 128],
                                  ptr[tb][:, 0:128], AF.Identity),
                              inc=peev,
                              waits=[(petr, petr.n),
                                     (stc2[sb2], prev["stc"][sb2])])
                    kb.dma("sync", lambda e, t=t, dc=dc, eb=eb, out_d=out_d:
                           e.dma_start(
                               out=out_d.ap()[dc * 512:(dc + 1) * 512,
                                              t * 128:(t + 1) * 128]
                               .rearrange("(hh d) t -> d hh t", d=128),
                               in_=sqkT[eb].ap().rearrange(
                                   "d (hh t) -> d hh t", t=128)),
                           inc=stc2[sb2], waits=[(peev, peev.n)])
                ginfo[gidx] = dict(evq=evq.n, petr=petr.n,
                                   stc=[stc2[0].n, stc2[1].n])
                if filler is not None:
                    filler()
            sidx += 1


def emit_attn(kb, st, C, mode, qT_src, kT_src, v_src, masks_d, oT_d,
              negc_col=None, tag="", pre=()):
    nc = kb.nc
    ldq = [kb.ctr() for i in range(2)]
    ldk = [kb.ctr() for i in range(2)]
    ldv = [kb.ctr() for i in range(2)]
    ldm = kb.ctr()
    smm = kb.ctr()
    avz = kb.ctr()
    evs = kb.ctr()
    msk = kb.ctr()
    fdv = kb.ctr()
    stc = [kb.ctr() for i in range(2)]

    skt = [sbuf(nc, st, f"at_kt{i}{tag}", [128, 64 * 128], BF16)
           for i in range(2)]
    svx = [sbuf(nc, st, f"at_vx{i}{tag}", [128, 64 * 128], BF16)
           for i in range(2)]
    sqt = [sbuf(nc, st, f"at_qt{i}{tag}", [128, TLOC], BF16) for i in range(2)]
    spt = [sbuf(nc, st, f"at_pt{i}{tag}", [128, 512], BF16) for i in range(3)]
    sof = [sbuf(nc, st, f"at_of{i}{tag}", [128, 512], F32) for i in range(2)]
    szr = [sbuf(nc, st, f"at_zr{i}{tag}", [128, 512], F32) for i in range(2)]
    sob = [sbuf(nc, st, f"at_ob{i}{tag}", [128, 512], BF16) for i in range(2)]
    smask = sbuf(nc, st, f"at_mk{tag}", [128, 512], BF16)
    pst = [psum(nc, st, f"at_ps{i}{tag}", [128, 512], F32) for i in range(3)]
    po = psum(nc, st, f"at_po{tag}", [128, 512], F32)
    pz = psum(nc, st, f"at_pz{tag}", [128, 512], F32)

    kb.dma("sync", lambda e: e.dma_start(
        out=smask.ap().rearrange("p (r c) -> p r c", c=128),
        in_=masks_d.ap().rearrange("r p c -> p r c")), inc=ldm)
    myldm = ldm.n

    cd = C["c_done"].n
    ev_end = {}
    av_end = {}
    fin_prev = {}
    nst = 0
    for h in range(H):
        hb = h % 2
        guard = [(avz, av_end.get(("head", h - 2), 0))] + list(pre)
        for (ii, (rsel, sap)) in enumerate(kT_src(h)):
            kb.dma("sync", lambda e, rsel=rsel, sap=sap, hb=hb: e.dma_start(
                out=skt[hb][:, rsel * 2048:(rsel + 1) * 2048], in_=sap),
                inc=ldk[hb], waits=guard if ii == 0 else ())
        myk = ldk[hb].n
        for (ii, (rsel, sap)) in enumerate(v_src(h)):
            kb.dma("sync", lambda e, rsel=rsel, sap=sap, hb=hb: e.dma_start(
                out=svx[hb].ap()[:, rsel * 2048:(rsel + 1) * 2048]
                .rearrange("p (jp d) -> p jp d", d=128), in_=sap),
                inc=ldv[hb], waits=guard if ii == 0 else ())
        myv = ldv[hb].n
        kb.dma("sync", lambda e, h=h, hb=hb: e.dma_start(
            out=sqt[hb][:], in_=qT_src(h)), inc=ldq[hb], waits=guard)
        myq = ldq[hb].n

        for G in range(4):
            q0 = 512 * G
            units = [(r_, jp) for r_ in range(4) for jp in range(4 * G + 4)]
            nun = len(units)
            pend = []

            def emit_avz(item, hb=hb):
                (ui, n_, kvslot, s0, width, avwaits) = item
                kb.op("tensor", lambda e, hb=hb, kvslot=kvslot, s0=s0,
                      width=width, ui=ui: e.matmul(
                          po[:, s0 * 128:512],
                          svx[hb][:, kvslot * 128:(kvslot + 1) * 128],
                          spt[n_ % 3][:, 0:width],
                          start=(ui == 0), stop=False),
                      inc=avz, waits=avwaits)
                kb.op("tensor", lambda e, s0=s0, width=width, ui=ui, n_=n_:
                      e.matmul(
                          pz[:, s0 * 128:512],
                          C["ones128"][:],
                          spt[n_ % 3][:, 0:width],
                          start=(ui == 0), stop=False),
                      inc=avz)
                av_end[n_] = avz.n

            for (ui, (r_, jp)) in enumerate(units):
                kvslot = r_ * 16 + jp
                s0 = max(0, jp - 4 * G)
                width = (4 - s0) * 128
                pb = nst % 3
                kb.op("tensor", lambda e, hb=hb, kvslot=kvslot, q0=q0, s0=s0,
                      width=width, pb=pb: e.matmul(
                          pst[pb][:, 0:width],
                          skt[hb][:, kvslot * 128:(kvslot + 1) * 128],
                          sqt[hb][:, q0 + s0 * 128:q0 + 512],
                          start=True, stop=True),
                      inc=smm,
                      waits=[(ldk[hb], myk), (ldq[hb], myq),
                             (evs, ev_end.get(nst - 3, 0)),
                             (C["c_done"], cd)] if ui == 0
                      else [(evs, ev_end.get(nst - 3, 0))])
                mysmm = smm.n
                if len(pend) >= 3:
                    emit_avz(pend.pop(0))
                tb = nst % 3
                ewaits = [(smm, mysmm), (avz, av_end.get(nst - 3, 0))]
                if mode == "sq":
                    kb.op("scalar", lambda e, pb=pb, tb=tb, width=width:
                          e.activation(spt[tb][:, 0:width],
                                       pst[pb][:, 0:width], AF.Square),
                          inc=evs, waits=ewaits)
                else:
                    ncol = negc_col(h)
                    kb.op("scalar", lambda e, pb=pb, tb=tb, width=width,
                          ncol=ncol: e.activation(
                              spt[tb][:, 0:width], pst[pb][:, 0:width],
                              AF.Exp, bias=ncol, scale=C["qkscale"]),
                          inc=evs, waits=ewaits)
                myev = evs.n
                sstar = jp - 4 * G
                if 0 <= sstar <= 3:
                    kb.op("gpsimd", lambda e, tb=tb, sstar=sstar, s0=s0,
                          r_=r_: e.tensor_tensor(
                              spt[tb][:, (sstar - s0) * 128:
                                      (sstar - s0 + 1) * 128],
                              spt[tb][:, (sstar - s0) * 128:
                                      (sstar - s0 + 1) * 128],
                              smask[:, r_ * 128:(r_ + 1) * 128], ALU.mult),
                          inc=msk, waits=[(evs, myev), (ldm, myldm)])
                    avwaits = [(msk, msk.n)]
                else:
                    avwaits = [(evs, myev)]
                if ui == 0:
                    avwaits = avwaits + [(fdv, fin_prev.get("fdv", 0)),
                                         (ldv[hb], myv)]
                pend.append((ui, nst, kvslot, s0, width, avwaits))
                ev_end[nst] = evs.n
                nst += 1
            while pend:
                emit_avz(pend.pop(0))
            kb.op("tensor", lambda e: e.matmul(
                po[:], C["ones128"][:], C["zeros512"][:],
                start=False, stop=True), inc=avz)
            kb.op("tensor", lambda e: e.matmul(
                pz[:], C["ones128"][:], C["zeros512"][:],
                start=False, stop=True), inc=avz)
            av_end[("head", h)] = avz.n
            fb = (h * 4 + G) % 2
            myavz = avz.n
            if mode == "sq":
                kb.op("scalar", lambda e, fb=fb: e.activation(
                    szr[fb][:], pz[:], AF.Ln, bias=C["eps"]),
                    inc=evs, waits=[(avz, myavz), (fdv, fin_prev.get(fb, 0))])
            else:
                kb.op("scalar", lambda e, fb=fb: e.activation(
                    szr[fb][:], pz[:], AF.Ln),
                    inc=evs, waits=[(avz, myavz), (fdv, fin_prev.get(fb, 0))])
            kb.op("scalar", lambda e, fb=fb: e.activation(
                szr[fb][:], szr[fb][:], AF.Exp, scale=C["neg1"]),
                inc=evs, waits=[(evs, evs.n)])
            kb.op("scalar", lambda e, fb=fb: e.activation(
                sof[fb][:], po[:], AF.Identity),
                inc=evs, waits=[(evs, evs.n)])
            myfin = evs.n
            kb.op("gpsimd", lambda e, fb=fb: e.tensor_tensor(
                sob[fb][:], sof[fb][:], szr[fb][:], ALU.mult),
                inc=fdv, waits=[(evs, myfin), (stc[fb], stc[fb].n)])
            fin_prev[fb] = fdv.n
            fin_prev["fdv"] = fdv.n
            kb.dma("sync", lambda e, h=h, G=G, fb=fb: e.dma_start(
                out=oT_d[h * 128:(h + 1) * 128, 512 * G:512 * G + 512],
                in_=sob[fb][:]), inc=stc[fb], waits=[(fdv, fdv.n)])
    return dict(stc=stc)


def build_attn0():
    nc = bass.Bass("TRN2", target_bir_lowering=False)
    qT_d = nc.dram_tensor("qT", [D, TLOC], BF16, kind="ExternalInput")
    ktb = nc.dram_tensor("ktb", [D, TGLOB], BF16, kind="ExternalInput")
    vb = nc.dram_tensor("vb", [TGLOB, D], BF16, kind="ExternalInput")
    masks_d = nc.dram_tensor("masks", [4, 128, 128], BF16,
                             kind="ExternalInput")
    oT_d = nc.dram_tensor("oT", [D, TLOC], BF16, kind="ExternalOutput")
    st = ExitStack()
    with st:
        kb = KB(nc, st)
        C = emit_consts(kb, st, need_ident=False)

        def qsrc(h):
            return qT_d[h * 128:(h + 1) * 128, :]

        def ksrc(h):
            return [(r_, ktb.ap()[h * 128:(h + 1) * 128,
                                  r_ * 2048:(r_ + 1) * 2048])
                    for r_ in range(4)]

        def vsrc(h):
            return [(r_, vb.ap()[r_ * 2048:(r_ + 1) * 2048,
                                 h * 128:(h + 1) * 128]
                     .rearrange("(jp tt) d -> tt jp d", tt=128))
                    for r_ in range(4)]

        emit_attn(kb, st, C, "sq", qsrc, ksrc, vsrc, masks_d, oT_d)
        kb.barrier()
        block = st.enter_context(nc.Block())
        kb.finalize(block)
    return nc


def emit_ffn0(kb, st, C, xnT_d, wgT_d, wuT_d, wd_view, ffn0_d):
    nc = kb.nc
    ldx = [kb.ctr(f"ff_ldx{i}") for i in range(2)]
    ldg = [kb.ctr(f"ff_ldg{i}") for i in range(2)]
    ldu = [kb.ctr(f"ff_ldu{i}") for i in range(2)]
    ldd = [kb.ctr(f"ff_ldd{i}") for i in range(2)]
    mm1 = kb.ctr("ff_mm1")
    ev1a = kb.ctr("ff_ev1a")
    ev1v = kb.ctr("ff_ev1v")
    mm2 = kb.ctr("ff_mm2")
    ev2 = kb.ctr("ff_ev2")
    stc = [kb.ctr(f"ff_st{i}") for i in range(2)]

    sxn = sbuf(nc, st, "ff_xn", [128, 16 * 1024], BF16)
    swg = [sbuf(nc, st, f"ff_wg{i}", [128, 16 * 128], BF16) for i in range(2)]
    swu = [sbuf(nc, st, f"ff_wu{i}", [128, 16 * 128], BF16) for i in range(2)]
    swd = [sbuf(nc, st, "ff_wd0", [128, NI * 512], BF16)]
    sh3 = sbuf(nc, st, "ff_h3", [128, NI * 1024], BF16)
    ssg = [sbuf(nc, st, f"ff_sg{i}", [128, 512], F32) for i in range(2)]
    sev = [sbuf(nc, st, f"ff_ev{i}", [128, 512], F32) for i in range(2)]
    pg = [psum(nc, st, f"ff_pg{i}", [128, 512], F32) for i in range(2)]
    pu = [psum(nc, st, f"ff_pu{i}", [128, 512], F32) for i in range(2)]
    pd = [psum(nc, st, f"ff_pd{i}", [128, 512], F32) for i in range(2)]

    h3_end = {}
    for half in range(2):
        hb = half % 2
        kb.dma("sync", lambda e, half=half: e.dma_start(
            out=sxn.ap().rearrange("p (cc t) -> p cc t", t=1024),
            in_=xnT_d.ap()[:, half * 1024:(half + 1) * 1024]
            .rearrange("(cc p) t -> p cc t", p=128)),
            inc=ldx[hb], waits=[(mm1, mm1.n)])
        myx = ldx[hb].n
        for i in range(NI):
            ib = i % 2
            kb.dma("sync", lambda e, i=i, ib=ib: e.dma_start(
                out=swg[ib].ap().rearrange("p (cc k) -> p cc k", k=128),
                in_=wgT_d.ap()[:, i * 128:(i + 1) * 128]
                .rearrange("(cc p) k -> p cc k", p=128)),
                inc=ldg[ib], waits=[(mm1, max(0, mm1.n - 2))])
            myg = ldg[ib].n
            kb.dma("gpsimd", lambda e, i=i, ib=ib: e.dma_start(
                out=swu[ib].ap().rearrange("p (cc k) -> p cc k", k=128),
                in_=wuT_d.ap()[:, i * 128:(i + 1) * 128]
                .rearrange("(cc p) k -> p cc k", p=128)),
                inc=ldu[ib], waits=[(mm1, max(0, mm1.n - 2))])
            myu = ldu[ib].n
            for tq in range(2):
                pb = (i * 2 + tq) % 2
                gi = half * NI * 2 + i * 2 + tq
                for cc in range(16):
                    kb.op("tensor", lambda e, i=i, ib=ib, tq=tq, cc=cc, pb=pb:
                          e.matmul(pg[pb][:],
                                   swg[ib][:, cc * 128:(cc + 1) * 128],
                                   sxn[:, cc * 1024 + tq * 512:
                                       cc * 1024 + tq * 512 + 512],
                                   start=(cc == 0), stop=(cc == 15)),
                          waits=([(ldg[ib], myg), (ldx[hb], myx),
                                  (ev1v, max(0, ev1v.n - 1))]
                                 if cc == 0 else ()))
                for cc in range(16):
                    kb.op("tensor", lambda e, i=i, ib=ib, tq=tq, cc=cc, pb=pb:
                          e.matmul(pu[pb][:],
                                   swu[ib][:, cc * 128:(cc + 1) * 128],
                                   sxn[:, cc * 1024 + tq * 512:
                                       cc * 1024 + tq * 512 + 512],
                                   start=(cc == 0), stop=(cc == 15)),
                          inc=(mm1 if cc == 15 else None),
                          waits=[(ldu[ib], myu)] if cc == 0 else ())
                mymm = mm1.n
                eb = gi % 2
                kb.op("scalar", lambda e, pb=pb, eb=eb: e.activation(
                    ssg[eb][:], pg[pb][:], AF.Silu),
                    inc=ev1a, waits=[(mm1, mymm),
                                     (ev1v, max(0, ev1v.n - 1))])
                kb.op("vector", lambda e, i=i, tq=tq, pb=pb, eb=eb:
                      e.tensor_tensor(
                          sh3[:, i * 1024 + tq * 512:i * 1024 + tq * 512 + 512],
                          ssg[eb][:], pu[pb][:], ALU.mult),
                      inc=ev1v, waits=[(ev1a, ev1a.n), (mm2, mm2.n)])
        h3_done = ev1v.n
        for e4 in range(4):
            db = 0
            for rr in range(2):
                kb.dma("gpsimd", lambda e, e4=e4, rr=rr: e.dma_start(
                    out=swd[0].ap().rearrange("p (i r k) -> p i r k",
                                              r=2, k=256)[:, :, rr, :],
                    in_=wd_view[:, :, 2 * e4 + rr, :]),
                    inc=ldd[0], waits=[(mm2, mm2.n)] if rr == 0 else ())
            myd = ldd[0].n
            for tsub in range(8):
                gi2 = (half * 4 + e4) * 8 + tsub
                pb = gi2 % 2
                for i in range(NI):
                    kb.op("tensor", lambda e, i=i, tsub=tsub, db=db, pb=pb:
                          e.matmul(pd[pb][:],
                                   sh3[:, i * 1024 + tsub * 128:
                                       i * 1024 + (tsub + 1) * 128],
                                   swd[db][:, i * 512:(i + 1) * 512],
                                   start=(i == 0), stop=(i == NI - 1)),
                          inc=(mm2 if i == NI - 1 else None),
                          waits=([(ldd[db], myd), (ev1v, h3_done),
                                  (ev2, max(0, ev2.n - 1))]
                                 if i == 0 else ()))
                eb = gi2 % 2
                kb.op("scalar", lambda e, pb=pb, eb=eb: e.activation(
                    sev[eb][:], pd[pb][:], AF.Identity),
                    inc=ev2, waits=[(mm2, mm2.n), (stc[eb], stc[eb].n)])
                kb.dma("sync", lambda e, half=half, e4=e4, tsub=tsub, eb=eb:
                       e.dma_start(
                           out=ffn0_d[half * 1024 + tsub * 128:
                                      half * 1024 + (tsub + 1) * 128,
                                      e4 * 512:(e4 + 1) * 512],
                           in_=sev[eb][:]),
                       inc=stc[eb], waits=[(ev2, ev2.n)])
    return dict(stc=stc)


def emit_combine0(kb, st, C, x_rows, oT_d, woT_d, ffn0_d, x1_d):
    nc = kb.nc
    ldo = kb.ctr("cb_ldo")
    ldw = [kb.ctr(f"cb_ldw{i}") for i in range(2)]
    ldf = [kb.ctr(f"cb_ldf{i}") for i in range(2)]
    ldxr = [kb.ctr(f"cb_ldx{i}") for i in range(2)]
    mm = kb.ctr("cb_mm")
    ev = kb.ctr("cb_ev")
    dvv = kb.ctr("cb_dv")
    stc = [kb.ctr(f"cb_st{i}") for i in range(2)]

    soT = sbuf(nc, st, "cb_oT", [128, 16 * D], BF16)
    swo = [sbuf(nc, st, f"cb_wo{i}", [128, 16 * 512], BF16) for i in range(2)]
    sf = [sbuf(nc, st, f"cb_f{i}", [128, 512], F32) for i in range(2)]
    sxr = [sbuf(nc, st, f"cb_x{i}", [128, 512], F32) for i in range(2)]
    sac = [sbuf(nc, st, f"cb_ac{i}", [128, 512], F32) for i in range(2)]
    pp = [psum(nc, st, f"cb_p{i}", [128, 512], F32) for i in range(2)]

    kb.dma("sync", lambda e: e.dma_start(
        out=soT.ap().rearrange("p (dd t) -> p dd t", t=TLOC),
        in_=oT_d.ap().rearrange("(dd p) t -> p dd t", p=128)), inc=ldo)
    myo = ldo.n

    for e4 in range(4):
        wb = e4 % 2
        kb.dma("sync", lambda e, e4=e4, wb=wb: e.dma_start(
            out=swo[wb].ap().rearrange("p (dd k) -> p dd k", k=512),
            in_=woT_d.ap()[:, e4 * 512:(e4 + 1) * 512]
            .rearrange("(dd p) k -> p dd k", p=128)),
            inc=ldw[wb], waits=[(mm, max(0, mm.n - 2))])
        myw = ldw[wb].n
        for t in range(NT):
            gi = e4 * NT + t
            pb = gi % 2
            eb = gi % 2
            kb.dma("sync", lambda e, e4=e4, t=t, eb=eb: e.dma_start(
                out=sf[eb][:],
                in_=ffn0_d[t * 128:(t + 1) * 128, e4 * 512:(e4 + 1) * 512]),
                inc=ldf[eb], waits=[(dvv, max(0, dvv.n - 3))])
            myf = ldf[eb].n
            kb.dma("sync", lambda e, e4=e4, t=t, eb=eb: e.dma_start(
                out=sxr[eb][:],
                in_=x_rows[t * 128:(t + 1) * 128, e4 * 512:(e4 + 1) * 512]),
                inc=ldxr[eb], waits=[(dvv, max(0, dvv.n - 2))])
            myxr = ldxr[eb].n
            for dd in range(16):
                kb.op("tensor", lambda e, dd=dd, t=t, wb=wb, pb=pb:
                      e.matmul(pp[pb][:],
                               soT[:, dd * TLOC + t * 128:
                                   dd * TLOC + (t + 1) * 128],
                               swo[wb][:, dd * 512:(dd + 1) * 512],
                               start=(dd == 0), stop=(dd == 15)),
                      inc=(mm if dd == 15 else None),
                      waits=([(ldw[wb], myw), (ldo, myo),
                              (ev, max(0, ev.n - 1))]
                             if dd == 0 else ()))
            kb.op("scalar", lambda e, pb=pb, eb=eb: e.activation(
                sac[eb][:], pp[pb][:], AF.Identity),
                inc=ev, waits=[(mm, mm.n), (dvv, max(0, dvv.n - 2)),
                               (stc[eb], stc[eb].n)])
            kb.op("vector", lambda e, eb=eb: e.tensor_tensor(
                sac[eb][:], sac[eb][:], sf[eb][:], ALU.add),
                inc=dvv, waits=[(ev, ev.n), (ldf[eb], myf)])
            kb.op("vector", lambda e, eb=eb: e.tensor_tensor(
                sac[eb][:], sac[eb][:], sxr[eb][:], ALU.add),
                inc=dvv, waits=[(dvv, dvv.n), (ldxr[eb], myxr)])
            kb.dma("sync", lambda e, e4=e4, t=t, eb=eb: e.dma_start(
                out=x1_d[t * 128:(t + 1) * 128, e4 * 512:(e4 + 1) * 512],
                in_=sac[eb][:]), inc=stc[eb], waits=[(dvv, dvv.n)])
    return dict(stc=stc)


def build_ffn0():
    nc = bass.Bass("TRN2", target_bir_lowering=False)
    xnT_d = nc.dram_tensor("xnT", [D, TLOC], BF16, kind="ExternalInput")
    wgT_d = nc.dram_tensor("wgT", [D, INTER], BF16, kind="ExternalInput")
    wuT_d = nc.dram_tensor("wuT", [D, INTER], BF16, kind="ExternalInput")
    wdT_d = nc.dram_tensor("wdT", [INTER, D], BF16, kind="ExternalInput")
    ffn0_d = nc.dram_tensor("ffn0", [TLOC, D], F32, kind="ExternalOutput")
    st = ExitStack()
    with st:
        kb = KB(nc, st)
        C = emit_consts(kb, st, need_ident=False)
        wd_view = wdT_d.ap().rearrange("(i p) (r k) -> p i r k",
                                       p=128, k=256)
        emit_ffn0(kb, st, C, xnT_d, wgT_d, wuT_d, wd_view, ffn0_d)
        kb.barrier()
        block = st.enter_context(nc.Block())
        kb.finalize(block)
    return nc


def build_combine0():
    nc = bass.Bass("TRN2", target_bir_lowering=False)
    x_rows = nc.dram_tensor("x_rows", [TLOC, D], F32, kind="ExternalInput")
    oT_d = nc.dram_tensor("oT", [D, TLOC], BF16, kind="ExternalInput")
    woT_d = nc.dram_tensor("woT", [D, D], F32, kind="ExternalInput")
    ffn0_d = nc.dram_tensor("ffn0", [TLOC, D], F32, kind="ExternalInput")
    x1_d = nc.dram_tensor("x1", [TLOC, D], F32, kind="ExternalOutput")
    xn1T_d = nc.dram_tensor("xn1T", [D, TLOC], BF16, kind="ExternalOutput")
    st = ExitStack()
    with st:
        kb = KB(nc, st)
        C = emit_consts(kb, st)
        r = emit_combine0(kb, st, C, x_rows, oT_d, woT_d, ffn0_d, x1_d)
        kb.wait("sync", r["stc"][0], r["stc"][0].n)
        kb.wait("sync", r["stc"][1], r["stc"][1].n)
        emit_ln_xnt(kb, st, x1_d, xn1T_d, C, tag="1")
        kb.barrier()
        block = st.enter_context(nc.Block())
        kb.finalize(block)
    return nc


def emit_qkv1(kb, st, C, xn1T_d, w_attn_d, q1T_d, k1T_d, v1_d, qn_d, kn_d):
    nc = kb.nc
    ldx = kb.ctr("q1_ldx")
    ldw = [kb.ctr(f"q1_ldw{i}") for i in range(2)]
    mm = kb.ctr("q1_mm")
    nmm = kb.ctr("q1_nmm")
    ev = kb.ctr("q1_ev")
    evs = kb.ctr("q1_evs")
    ndv = kb.ctr("q1_ndv")
    stc = [kb.ctr(f"q1_st{i}") for i in range(2)]
    stn = kb.ctr("q1_stn")

    sxn = sbuf(nc, st, "q1_xn", [128, NT * D], BF16)
    slab = [sbuf(nc, st, f"q1_sl{i}", [128, 16 * 512], BF16) for i in range(2)]
    sq = [sbuf(nc, st, f"q1_q{i}", [128, 512], BF16) for i in range(2)]
    ssq = [sbuf(nc, st, f"q1_sq{i}", [128, 512], BF16) for i in range(2)]
    srow = sbuf(nc, st, "q1_row", [1, TLOC], F32)
    pq = [psum(nc, st, f"q1_p{i}", [128, 512], F32) for i in range(2)]
    pn = [psum(nc, st, f"q1_pn{i}", [128, 512], F32) for i in range(2)]

    kb.dma("sync", lambda e: e.dma_start(
        out=sxn.ap().rearrange("p (cc t) -> p cc t", t=TLOC),
        in_=xn1T_d.ap().rearrange("(cc p) t -> p cc t", p=128)), inc=ldx)
    myx = ldx.n
    cd = C["c_done"].n

    for (wi, out_d, n_d) in ((0, q1T_d, qn_d), (1, k1T_d, kn_d)):
        for dd in range(16):
            sb = (wi * 16 + dd) % 2
            kb.dma("sync", lambda e, wi=wi, dd=dd, sb=sb: e.dma_start(
                out=slab[sb].ap()[:, 0:16 * 128]
                .rearrange("p (cc k) -> p cc k", k=128),
                in_=w_attn_d.ap()[:, wi * D + dd * 128:wi * D + (dd + 1) * 128]
                .rearrange("(cc p) k -> p cc k", p=128)),
                inc=ldw[sb], waits=[(mm, max(0, mm.n - 4))])
            myw = ldw[sb].n
            for t4 in range(4):
                gi = (wi * 16 + dd) * 4 + t4
                pb = gi % 2
                eb = gi % 2
                for cc in range(16):
                    kb.op("tensor", lambda e, sb=sb, t4=t4, cc=cc, pb=pb:
                          e.matmul(pq[pb][:],
                                   slab[sb][:, cc * 128:(cc + 1) * 128],
                                   sxn[:, cc * TLOC + t4 * 512:
                                       cc * TLOC + (t4 + 1) * 512],
                                   start=(cc == 0), stop=(cc == 15)),
                          inc=(mm if cc == 15 else None),
                          waits=([(ldw[sb], myw), (ldx, myx),
                                  (evs, max(0, evs.n - 1)),
                                  (C["c_done"], cd)]
                                 if cc == 0 else ()))
                mymm = mm.n
                kb.op("scalar", lambda e, pb=pb, eb=eb: e.activation(
                    sq[eb][:], pq[pb][:], AF.Identity),
                    inc=ev, waits=[(mm, mymm), (stc[eb], stc[eb].n)])
                kb.dma("sync", lambda e, dd=dd, t4=t4, eb=eb, out_d=out_d:
                       e.dma_start(
                           out=out_d[dd * 128:(dd + 1) * 128,
                                     t4 * 512:(t4 + 1) * 512],
                           in_=sq[eb][:]), inc=stc[eb], waits=[(ev, ev.n)])
                kb.op("scalar", lambda e, pb=pb, eb=eb: e.activation(
                    ssq[eb][:], pq[pb][:], AF.Square),
                    inc=evs, waits=[(ev, ev.n), (nmm, max(0, nmm.n - 1))])
                kb.op("tensor", lambda e, eb=eb, pb=pb: e.matmul(
                    pn[pb][:], C["ones128"][:], ssq[eb][:],
                    start=True, stop=True),
                    inc=nmm, waits=[(evs, evs.n), (ndv, max(0, ndv.n - 1))])
                kb.op("vector", lambda e, t4=t4, pb=pb: e.tensor_copy(
                    srow[:, t4 * 512:(t4 + 1) * 512], pn[pb][0:1, :]),
                    inc=ndv, waits=[(nmm, nmm.n), (stn, stn.n)])
            kb.dma("gpsimd", lambda e, dd=dd, n_d=n_d: e.dma_start(
                out=n_d[dd:dd + 1, :], in_=srow[:]),
                inc=stn, waits=[(ndv, ndv.n)])

    for dc in range(4):
        sb = dc % 2
        kb.dma("sync", lambda e, dc=dc, sb=sb: e.dma_start(
            out=slab[sb].ap().rearrange("p (cc k) -> p cc k", k=512),
            in_=w_attn_d.ap()[:, 2 * D + dc * 512:2 * D + (dc + 1) * 512]
            .rearrange("(cc p) k -> p cc k", p=128)),
            inc=ldw[sb], waits=[(mm, mm.n if dc < 2 else mm.n - 16)])
        myw = ldw[sb].n
        for t in range(NT):
            gi = 128 + dc * NT + t
            pb = gi % 2
            eb = gi % 2
            for cc in range(16):
                kb.op("tensor", lambda e, sb=sb, t=t, cc=cc, pb=pb:
                      e.matmul(pq[pb][:],
                               sxn[:, cc * TLOC + t * 128:
                                   cc * TLOC + (t + 1) * 128],
                               slab[sb][:, cc * 512:(cc + 1) * 512],
                               start=(cc == 0), stop=(cc == 15)),
                      inc=(mm if cc == 15 else None),
                      waits=([(ldw[sb], myw), (ev, max(0, ev.n - 1))]
                             if cc == 0 else ()))
            kb.op("scalar", lambda e, pb=pb, eb=eb: e.activation(
                sq[eb][:], pq[pb][:], AF.Identity),
                inc=ev, waits=[(mm, mm.n), (stc[eb], stc[eb].n)])
            kb.dma("sync", lambda e, t=t, dc=dc, eb=eb: e.dma_start(
                out=v1_d[t * 128:(t + 1) * 128, dc * 512:(dc + 1) * 512],
                in_=sq[eb][:]), inc=stc[eb], waits=[(ev, ev.n)])
    return dict(stc=stc)


def emit_negc(kb, st, C, qn_d, knb_src, knb_eng="sync", tag="1"):
    nc = kb.nc
    ld = kb.ctr(f"nc_ld{tag}")
    dv = kb.ctr(f"nc_dv{tag}")
    ac = kb.ctr(f"nc_ac{tag}")
    pe = kb.ctr(f"nc_pe{tag}")

    sqn = sbuf(nc, st, f"nc_qn{tag}", [16, TLOC], F32)
    skn = sbuf(nc, st, f"nc_kn{tag}", [16, TGLOB], F32)
    sm = sbuf(nc, st, f"nc_m{tag}", [16, 4], F32)
    snegb = sbuf(nc, st, f"nc_nb{tag}", [16, 1], BF16)
    srowb = sbuf(nc, st, f"nc_rb{tag}", [1, 16], BF16)
    snegc = sbuf(nc, st, f"nc_c{tag}", [128, 16], F32)
    pnc = psum(nc, st, f"nc_p{tag}", [128, 512], F32)

    ld2_ = kb.ctr(f"nc_ld2{tag}")
    kb.dma("sync", lambda e: e.dma_start(out=sqn[:], in_=qn_d[:, :]), inc=ld)
    for (ii, (rsel, sap)) in enumerate(knb_src):
        if rsel is None:
            dst = skn[:]
        else:
            dst = skn[:, rsel * 2048:(rsel + 1) * 2048]
        kb.dma(knb_eng, lambda e, dst=dst, sap=sap: e.dma_start(
            out=dst, in_=sap), inc=ld2_)
    myld = ld.n
    myld2 = ld2_.n
    kb.op("vector", lambda e: e.reduce_max(sm[:, 0:1], sqn[:], axis=AX.X),
          inc=dv, waits=[(ld, myld)])
    kb.op("vector", lambda e: e.reduce_max(sm[:, 1:2], skn[:], axis=AX.X),
          inc=dv, waits=[(dv, dv.n), (ld2_, myld2)])
    kb.op("vector", lambda e: e.tensor_tensor(
        sm[:, 2:3], sm[:, 0:1], sm[:, 1:2], ALU.mult),
        inc=dv, waits=[(dv, dv.n)])
    kb.op("scalar", lambda e: e.activation(
        sm[:, 3:4], sm[:, 2:3], AF.Ln), inc=ac, waits=[(dv, dv.n)])
    kb.op("scalar", lambda e: e.activation(
        sm[:, 3:4], sm[:, 3:4], AF.Exp, scale=C["half16"],
        bias=C["lnqscale16"]), inc=ac, waits=[(ac, ac.n)])
    kb.op("vector", lambda e: e.tensor_scalar(
        snegb[:], sm[:, 3:4], -1.0, None, ALU.mult),
        inc=dv, waits=[(ac, ac.n)])
    kb.op("tensor", lambda e: e.matmul(
        pnc[0:1, 0:16], snegb[:], C["ident"][0:16, 0:16],
        start=True, stop=True),
        inc=pe, waits=[(dv, dv.n), (C["c_done"], C["c_done"].n)])
    kb.op("vector", lambda e: e.tensor_copy(srowb[:], pnc[0:1, 0:16]),
          inc=dv, waits=[(pe, pe.n)])
    kb.op("tensor", lambda e: e.matmul(
        pnc[:, 0:16], C["ones_row"][:], srowb[:], start=True, stop=True),
        inc=pe, waits=[(dv, dv.n)])
    kb.op("vector", lambda e: e.tensor_copy(snegc[:], pnc[:, 0:16]),
          inc=dv, waits=[(pe, pe.n)])
    fin = dv.n
    return snegc, dv, fin


def build_qkv1():
    nc = bass.Bass("TRN2", target_bir_lowering=False)
    xn1T_d = nc.dram_tensor("xn1T", [D, TLOC], BF16, kind="ExternalInput")
    w_attn_d = nc.dram_tensor("w_attn", [D, 3 * D], F32, kind="ExternalInput")
    q1T_d = nc.dram_tensor("q1T", [D, TLOC], BF16, kind="ExternalOutput")
    k1T_d = nc.dram_tensor("k1T", [D, TLOC], BF16, kind="ExternalOutput")
    v1_d = nc.dram_tensor("v1", [TLOC, D], BF16, kind="ExternalOutput")
    qn_d = nc.dram_tensor("qn", [16, TLOC], F32, kind="ExternalOutput")
    kn_d = nc.dram_tensor("kn", [16, TLOC], F32, kind="ExternalOutput")
    st = ExitStack()
    with st:
        kb = KB(nc, st)
        C = emit_consts(kb, st, need_ident=False)
        emit_qkv1(kb, st, C, xn1T_d, w_attn_d, q1T_d, k1T_d, v1_d, qn_d, kn_d)
        kb.barrier()
        block = st.enter_context(nc.Block())
        kb.finalize(block)
    return nc


def build_attn1():
    nc = bass.Bass("TRN2", target_bir_lowering=False)
    qT_d = nc.dram_tensor("q1T", [D, TLOC], BF16, kind="ExternalInput")
    ktb = nc.dram_tensor("k1tb", [D, TGLOB], BF16, kind="ExternalInput")
    vb = nc.dram_tensor("v1b", [TGLOB, D], BF16, kind="ExternalInput")
    masks_d = nc.dram_tensor("masks", [4, 128, 128], BF16,
                             kind="ExternalInput")
    qn_d = nc.dram_tensor("qn", [16, TLOC], F32, kind="ExternalInput")
    knb_d = nc.dram_tensor("knb", [16, TGLOB], F32, kind="ExternalInput")
    oT_d = nc.dram_tensor("o1T", [D, TLOC], BF16, kind="ExternalOutput")
    st = ExitStack()
    with st:
        kb = KB(nc, st)
        C = emit_consts(kb, st, need_ident=True)
        snegc, ncdv, ncfin = emit_negc(kb, st, C, qn_d,
                                       [(None, knb_d.ap())])
        kb.barrier()

        def qsrc(h):
            return qT_d[h * 128:(h + 1) * 128, :]

        def ksrc(h):
            return [(r_, ktb.ap()[h * 128:(h + 1) * 128,
                                  r_ * 2048:(r_ + 1) * 2048])
                    for r_ in range(4)]

        def vsrc(h):
            return [(r_, vb.ap()[r_ * 2048:(r_ + 1) * 2048,
                                 h * 128:(h + 1) * 128]
                     .rearrange("(jp tt) d -> tt jp d", tt=128))
                    for r_ in range(4)]

        emit_attn(kb, st, C, "sm", qsrc, ksrc, vsrc, masks_d, oT_d,
                  negc_col=lambda h: snegc[:, h:h + 1])
        kb.barrier()
        block = st.enter_context(nc.Block())
        kb.finalize(block)
    return nc


def emit_moelocal(kb, st, C, x1_rows, o1T_d, w_proj_d, w1s_d, w3s_d, w2s_d,
                  gwT_d, gbbc_d, xn1T_d, ybase_d, probs_d):
    nc = kb.nc
    ldx = kb.ctr("ml_ldx")
    ld1 = [kb.ctr(f"ml_ld1{i}") for i in range(2)]
    ld3 = [kb.ctr(f"ml_ld3{i}") for i in range(2)]
    ld2 = kb.ctr("ml_ld2")
    ldp = [kb.ctr(f"ml_ldp{i}") for i in range(2)]
    ldo = [kb.ctr(f"ml_ldo{i}") for i in range(2)]
    ldr = [kb.ctr(f"ml_ldr{i}") for i in range(2)]
    ldg = kb.ctr("ml_ldg")
    ldgb = kb.ctr("ml_ldgb")
    mm1 = kb.ctr("ml_mm1")
    ev1a = kb.ctr("ml_ev1a")
    ev1v = kb.ctr("ml_ev1v")
    mm2 = kb.ctr("ml_mm2")
    ev2 = kb.ctr("ml_ev2")
    dv2 = kb.ctr("ml_dv2")
    stc = [kb.ctr(f"ml_st{i}") for i in range(2)]
    rmm = kb.ctr("ml_rmm")
    rac = kb.ctr("ml_rac")
    rdv = kb.ctr("ml_rdv")
    stp = kb.ctr("ml_stp")

    sxn = sbuf(nc, st, "ml_xn", [128, NT * D], BF16)
    sw1 = [sbuf(nc, st, f"ml_w1{i}", [128, 16 * 128], BF16) for i in range(2)]
    sw3 = [sbuf(nc, st, f"ml_w3{i}", [128, 16 * 128], BF16) for i in range(2)]
    sh = sbuf(nc, st, "ml_h", [128, NMH * D], BF16)
    ssg = [sbuf(nc, st, f"ml_sg{i}", [128, 512], F32) for i in range(2)]
    sw2 = sbuf(nc, st, "ml_w2", [128, NMH * 512], BF16)
    swp = [sbuf(nc, st, f"ml_wp{i}", [128, 16 * 512], BF16) for i in range(2)]
    soc = [sbuf(nc, st, f"ml_oc{i}", [128, 16 * 128], BF16) for i in range(2)]
    sxr = [sbuf(nc, st, f"ml_xr{i}", [128, 512], F32) for i in range(2)]
    sac = [sbuf(nc, st, f"ml_ac{i}", [128, 512], F32) for i in range(2)]
    sgw = sbuf(nc, st, "ml_gw", [128, 16 * 16], BF16)
    sgb = sbuf(nc, st, "ml_gb", [128, 16], F32)
    ssc = sbuf(nc, st, "ml_sc", [128, 16], F32)
    smk = sbuf(nc, st, "ml_mk", [128, 16], F32)
    sprob = sbuf(nc, st, "ml_pr", [128, 16], F32)
    scols = sbuf(nc, st, "ml_co", [128, 8], F32)
    pg = [psum(nc, st, f"ml_pg{i}", [128, 512], F32) for i in range(2)]
    pu = [psum(nc, st, f"ml_pu{i}", [128, 512], F32) for i in range(2)]
    pd = [psum(nc, st, f"ml_pd{i}", [128, 512], F32) for i in range(2)]
    plog = psum(nc, st, "ml_pl", [128, 512], F32)

    kb.dma("sync", lambda e: e.dma_start(
        out=sxn.ap().rearrange("p (cc t) -> p cc t", t=TLOC),
        in_=xn1T_d.ap().rearrange("(cc p) t -> p cc t", p=128)), inc=ldx)
    myx = ldx.n
    kb.dma("sync", lambda e: e.dma_start(
        out=sgw.ap().rearrange("p (cc k) -> p cc k", k=16),
        in_=gwT_d.ap().rearrange("(cc p) k -> p cc k", p=128)), inc=ldg)
    kb.dma("sync", lambda e: e.dma_start(out=sgb[:], in_=gbbc_d[:, :]),
           inc=ldgb)
    myg = ldg.n
    mygb = ldgb.n
    cd = C["c_done"].n

    for i in range(NMH):
        ib = i % 2
        kb.dma("sync", lambda e, i=i, ib=ib: e.dma_start(
            out=sw1[ib].ap().rearrange("p (cc k) -> p cc k", k=128),
            in_=w1s_d.ap()[:, i * 128:(i + 1) * 128]
            .rearrange("(cc p) k -> p cc k", p=128)),
            inc=ld1[ib], waits=[(mm1, max(0, mm1.n - 2))])
        my1 = ld1[ib].n
        kb.dma("sync", lambda e, i=i, ib=ib: e.dma_start(
            out=sw3[ib].ap().rearrange("p (cc k) -> p cc k", k=128),
            in_=w3s_d.ap()[:, i * 128:(i + 1) * 128]
            .rearrange("(cc p) k -> p cc k", p=128)),
            inc=ld3[ib], waits=[(mm1, max(0, mm1.n - 2))])
        my3 = ld3[ib].n
        for tq in range(4):
            gi = i * 4 + tq
            pb = gi % 2
            eb = gi % 2
            for cc in range(16):
                kb.op("tensor", lambda e, ib=ib, tq=tq, cc=cc, pb=pb:
                      e.matmul(pg[pb][:],
                               sw1[ib][:, cc * 128:(cc + 1) * 128],
                               sxn[:, cc * TLOC + tq * 512:
                                   cc * TLOC + (tq + 1) * 512],
                               start=(cc == 0), stop=(cc == 15)),
                      waits=([(ld1[ib], my1), (ldx, myx),
                              (ev1v, max(0, ev1v.n - 1))]
                             if cc == 0 else ()))
            for cc in range(16):
                kb.op("tensor", lambda e, ib=ib, tq=tq, cc=cc, pb=pb:
                      e.matmul(pu[pb][:],
                               sw3[ib][:, cc * 128:(cc + 1) * 128],
                               sxn[:, cc * TLOC + tq * 512:
                                   cc * TLOC + (tq + 1) * 512],
                               start=(cc == 0), stop=(cc == 15)),
                      inc=(mm1 if cc == 15 else None),
                      waits=[(ld3[ib], my3)] if cc == 0 else ())
            kb.op("scalar", lambda e, pb=pb, eb=eb: e.activation(
                ssg[eb][:], pg[pb][:], AF.Silu),
                inc=ev1a, waits=[(mm1, mm1.n), (ev1v, max(0, ev1v.n - 1))])
            kb.op("vector", lambda e, i=i, tq=tq, pb=pb, eb=eb:
                  e.tensor_tensor(
                      sh[:, i * D + tq * 512:i * D + (tq + 1) * 512],
                      ssg[eb][:], pu[pb][:], ALU.mult),
                  inc=ev1v, waits=[(ev1a, ev1a.n), (mm2, mm2.n)])
    h_done = ev1v.n

    for t in range(NT):
        for cc in range(16):
            kb.op("tensor", lambda e, t=t, cc=cc: e.matmul(
                plog[:, 0:16],
                sxn[:, cc * TLOC + t * 128:cc * TLOC + (t + 1) * 128],
                sgw[:, cc * 16:(cc + 1) * 16],
                start=(cc == 0), stop=(cc == 15)),
                inc=(rmm if cc == 15 else None),
                waits=([(ldg, myg), (ldx, myx), (rac, rac.n)]
                       if cc == 0 else ()))
        kb.op("scalar", lambda e: e.activation(ssc[:], plog[:, 0:16],
                                               AF.Sigmoid),
              inc=rac, waits=[(rmm, rmm.n), (rdv, rdv.n)])
        kb.op("vector", lambda e: e.tensor_tensor(ssc[:], ssc[:], sgb[:],
                                                  ALU.add),
              inc=rdv, waits=[(rac, rac.n), (ldgb, mygb)])
        kb.op("vector", lambda e: e.memset(sprob[:], 0.0), inc=rdv,
              waits=[(stp, stp.n)])
        kb.op("vector", lambda e: e.memset(scols[:, 0:1], 0.0), inc=rdv)
        for kk in range(4):
            kb.op("vector", lambda e: e.reduce_max(
                scols[:, 1:2], ssc[:], axis=AX.X),
                inc=rdv, waits=[(rdv, rdv.n)])
            kb.op("vector", lambda e: e.tensor_scalar(
                smk[:], ssc[:], scols[:, 1:2], None, ALU.is_equal),
                inc=rdv, waits=[(rdv, rdv.n)])
            kb.op("vector", lambda e: e.scalar_tensor_tensor(
                sprob[:], smk[:], scols[:, 1:2], sprob[:],
                ALU.mult, ALU.add),
                inc=rdv, waits=[(rdv, rdv.n)])
            kb.op("vector", lambda e: e.tensor_tensor(
                scols[:, 0:1], scols[:, 0:1], scols[:, 1:2], ALU.add),
                inc=rdv, waits=[(rdv, rdv.n)])
            kb.op("vector", lambda e: e.scalar_tensor_tensor(
                ssc[:], smk[:], -1e9, ssc[:], ALU.mult, ALU.add),
                inc=rdv, waits=[(rdv, rdv.n)])
        kb.op("vector", lambda e: e.reciprocal(scols[:, 2:3], scols[:, 0:1]),
              inc=rdv, waits=[(rdv, rdv.n)])
        kb.op("vector", lambda e: e.tensor_scalar(
            sprob[:], sprob[:], scols[:, 2:3], None, ALU.mult),
            inc=rdv, waits=[(rdv, rdv.n)])
        kb.dma("sync", lambda e, t=t: e.dma_start(
            out=probs_d[t * 128:(t + 1) * 128, :], in_=sprob[:]),
            inc=stp, waits=[(rdv, rdv.n)])

    for e4 in range(4):
        kb.dma("sync", lambda e, e4=e4: e.dma_start(
            out=sw2.ap().rearrange("p (i k) -> p i k", k=512),
            in_=w2s_d.ap()[:, e4 * 512:(e4 + 1) * 512]
            .rearrange("(i p) k -> p i k", p=128)),
            inc=ld2, waits=[(mm2, mm2.n)])
        my2 = ld2.n
        wb = e4 % 2
        kb.dma("sync", lambda e, e4=e4, wb=wb: e.dma_start(
            out=swp[wb].ap().rearrange("p (dd k) -> p dd k", k=512),
            in_=w_proj_d.ap()[:, e4 * 512:(e4 + 1) * 512]
            .rearrange("(dd p) k -> p dd k", p=128)),
            inc=ldp[wb], waits=[(mm2, max(0, mm2.n - 2))])
        myp = ldp[wb].n
        for t in range(NT):
            gi = e4 * NT + t
            pb = gi % 2
            eb = gi % 2
            ob = gi % 2
            kb.dma("sync", lambda e, t=t, ob=ob: e.dma_start(
                out=soc[ob].ap().rearrange("p (dd k) -> p dd k", k=128),
                in_=o1T_d.ap()[:, t * 128:(t + 1) * 128]
                .rearrange("(dd p) k -> p dd k", p=128)),
                inc=ldo[ob], waits=[(mm2, max(0, mm2.n - 1))])
            myo = ldo[ob].n
            kb.dma("sync", lambda e, e4=e4, t=t, eb=eb: e.dma_start(
                out=sxr[eb][:],
                in_=x1_rows[t * 128:(t + 1) * 128,
                            e4 * 512:(e4 + 1) * 512]),
                inc=ldr[eb], waits=[(dv2, max(0, dv2.n - 1))])
            myr = ldr[eb].n
            for i in range(NMH):
                kb.op("tensor", lambda e, i=i, t=t, pb=pb: e.matmul(
                    pd[pb][:],
                    sh[:, i * D + t * 128:i * D + (t + 1) * 128],
                    sw2[:, i * 512:(i + 1) * 512],
                    start=(i == 0), stop=False),
                    waits=([(ld2, my2), (ev1v, h_done),
                            (ev2, max(0, ev2.n - 1))]
                           if i == 0 else ()))
            for dd in range(16):
                kb.op("tensor", lambda e, dd=dd, wb=wb, ob=ob, pb=pb:
                      e.matmul(
                          pd[pb][:],
                          soc[ob][:, dd * 128:(dd + 1) * 128],
                          swp[wb][:, dd * 512:(dd + 1) * 512],
                          start=False, stop=(dd == 15)),
                      inc=(mm2 if dd == 15 else None),
                      waits=([(ldp[wb], myp), (ldo[ob], myo)]
                             if dd == 0 else ()))
            kb.op("scalar", lambda e, pb=pb, eb=eb: e.activation(
                sac[eb][:], pd[pb][:], AF.Identity),
                inc=ev2, waits=[(mm2, mm2.n), (dv2, max(0, dv2.n - 1)),
                                (stc[eb], stc[eb].n)])
            kb.op("vector", lambda e, eb=eb: e.tensor_tensor(
                sac[eb][:], sac[eb][:], sxr[eb][:], ALU.add),
                inc=dv2, waits=[(ev2, ev2.n), (ldr[eb], myr)])
            kb.dma("sync", lambda e, e4=e4, t=t, eb=eb: e.dma_start(
                out=ybase_d[t * 128:(t + 1) * 128,
                            e4 * 512:(e4 + 1) * 512],
                in_=sac[eb][:]), inc=stc[eb], waits=[(dv2, dv2.n)])
    return dict(stc=stc, stp=stp)


def emit_moe_unit(kb, st, bufs, uid, w1_ap, w3_ap, w2_ap, xnc_ap, probs_ap,
                  expert_col, out_ap, accum, ncols=2):
    nc = kb.nc
    b = bufs
    ldx, ld1, ld3, ld2, ldpr = b["ldx"], b["ld1"], b["ld3"], b["ld2"], b["ldpr"]
    mm1, ev1a, ev1v, ev1s = b["mm1"], b["ev1a"], b["ev1v"], b["ev1s"]
    mm2, ev2 = b["mm2"], b["ev2"]
    stc = b["stc"]

    if xnc_ap is not None:
        kb.dma("sync", lambda e: e.dma_start(
            out=b["sxn"].ap().rearrange("p (cc t) -> p cc t", t=TLOC),
            in_=xnc_ap.rearrange("(cc p) t -> p cc t", p=128)),
            inc=ldx, waits=[(mm1, mm1.n), (b["rmmg"], b["rmmg"].n)])
        b["myx"] = ldx.n
        kb.dma("sync", lambda e, ncols=ncols: e.dma_start(
            out=b["spr"].ap()[:, 0:NT * ncols]
            .rearrange("p (tt k) -> p tt k", k=ncols),
            in_=probs_ap.rearrange("(tt p) k -> p tt k", p=128)),
            inc=ldpr, waits=[(ev2, ev2.n)])
        b["mypr"] = ldpr.n
    myx = b["myx"]
    mypr = b["mypr"]

    for i in range(NMH):
        ib = i % 2
        kb.dma("sync", lambda e, i=i, ib=ib: e.dma_start(
            out=b["sw1"][ib].ap().rearrange("p (cc k) -> p cc k", k=128),
            in_=w1_ap[:, i * 128:(i + 1) * 128]
            .rearrange("(cc p) k -> p cc k", p=128)),
            inc=ld1[ib], waits=[(mm1, max(0, mm1.n - 2))])
        my1 = ld1[ib].n
        kb.dma("sync", lambda e, i=i, ib=ib: e.dma_start(
            out=b["sw3"][ib].ap().rearrange("p (cc k) -> p cc k", k=128),
            in_=w3_ap[:, i * 128:(i + 1) * 128]
            .rearrange("(cc p) k -> p cc k", p=128)),
            inc=ld3[ib], waits=[(mm1, max(0, mm1.n - 2))])
        my3 = ld3[ib].n
        for tq in range(4):
            gi = i * 4 + tq
            pb = gi % 2
            eb = gi % 2
            for cc in range(16):
                kb.op("tensor", lambda e, ib=ib, tq=tq, cc=cc, pb=pb:
                      e.matmul(b["pg"][pb][:],
                               b["sw1"][ib][:, cc * 128:(cc + 1) * 128],
                               b["sxn"][:, cc * TLOC + tq * 512:
                                        cc * TLOC + (tq + 1) * 512],
                               start=(cc == 0), stop=(cc == 15)),
                      waits=([(ld1[ib], my1), (ldx, myx),
                              (ev1v, max(0, ev1v.n - 1))]
                             if cc == 0 else ()))
            for cc in range(16):
                kb.op("tensor", lambda e, ib=ib, tq=tq, cc=cc, pb=pb:
                      e.matmul(b["pu"][pb][:],
                               b["sw3"][ib][:, cc * 128:(cc + 1) * 128],
                               b["sxn"][:, cc * TLOC + tq * 512:
                                        cc * TLOC + (tq + 1) * 512],
                               start=(cc == 0), stop=(cc == 15)),
                      inc=(mm1 if cc == 15 else None),
                      waits=[(ld3[ib], my3)] if cc == 0 else ())
            kb.op("scalar", lambda e, pb=pb, eb=eb: e.activation(
                b["ssg"][eb][:], b["pg"][pb][:], AF.Identity),
                inc=ev1a, waits=[(mm1, mm1.n), (ev1v, max(0, ev1v.n - 1))])
            kb.op("vector", lambda e, pb=pb, eb=eb: e.tensor_tensor(
                b["spp"][eb][:], b["ssg"][eb][:], b["pu"][pb][:], ALU.mult),
                inc=ev1v, waits=[(ev1a, ev1a.n),
                                 (ev1s, max(0, ev1s.n - 1))])
            kb.op("scalar", lambda e, i=i, tq=tq, eb=eb: e.activation(
                b["sh"][:, i * D + tq * 512:i * D + (tq + 1) * 512],
                b["spp"][eb][:], AF.Silu),
                inc=ev1s, waits=[(ev1v, ev1v.n), (mm2, mm2.n)])
    h_done = ev1s.n

    for e4 in range(4):
        db = e4 % 2
        kb.dma("sync", lambda e, e4=e4, db=db: e.dma_start(
            out=b["sw2"][db].ap().rearrange("p (i k) -> p i k", k=512),
            in_=w2_ap[:, e4 * 512:(e4 + 1) * 512]
            .rearrange("(i p) k -> p i k", p=128)),
            inc=ld2[db], waits=[(mm2, max(0, mm2.n - 16))])
        my2 = ld2[db].n
        for t in range(NT):
            gi2 = e4 * NT + t
            pb = gi2 % 2
            eb = gi2 % 2
            for i in range(NMH):
                kb.op("tensor", lambda e, i=i, t=t, pb=pb, db=db: e.matmul(
                    b["pd"][pb][:],
                    b["sh"][:, i * D + t * 128:i * D + (t + 1) * 128],
                    b["sw2"][db][:, i * 512:(i + 1) * 512],
                    start=(i == 0), stop=(i == NMH - 1)),
                    inc=(mm2 if i == NMH - 1 else None),
                    waits=([(ld2[db], my2), (ev1s, h_done),
                            (ev2, max(0, ev2.n - 1))]
                           if i == 0 else ()))
            kb.op("scalar", lambda e, t=t, pb=pb, eb=eb,
                  ec=expert_col, nco=ncols: e.activation(
                      b["sev"][eb][:], b["pd"][pb][:], AF.Identity,
                      scale=b["spr"][:, t * nco + ec:t * nco + ec + 1]),
                  inc=ev2, waits=[(mm2, mm2.n), (ldpr, mypr),
                                  (stc[eb], stc[eb].n)])
            kb.dma("gpsimd", lambda e, e4=e4, t=t, eb=eb, out_ap=out_ap,
                   accum=accum: e.dma_start(
                       out=out_ap[t * 128:(t + 1) * 128,
                                  e4 * 512:(e4 + 1) * 512],
                       in_=b["sev"][eb][:],
                       **(dict(accum_op=ALU.add) if accum else {})),
                   inc=stc[eb], waits=[(ev2, ev2.n)])


def mk_moe_bufs(kb, st):
    nc = kb.nc
    b = {}
    for n in ("ldx", "ldpr", "mm1", "ev1a", "ev1v", "ev1s", "mm2",
              "ev2", "rmmg"):
        b[n] = kb.ctr(f"mu_{n}")
    b["ld2"] = [kb.ctr(f"mu_ld2{i}") for i in range(2)]
    b["myx"] = 0
    b["mypr"] = 0
    b["ld1"] = [kb.ctr(f"mu_ld1{i}") for i in range(2)]
    b["ld3"] = [kb.ctr(f"mu_ld3{i}") for i in range(2)]
    b["stc"] = [kb.ctr(f"mu_st{i}") for i in range(2)]
    b["sxn"] = sbuf(nc, st, "mu_xn", [128, NT * D], BF16)
    b["sw1"] = [sbuf(nc, st, f"mu_w1{i}", [128, 16 * 128], BF16)
                for i in range(2)]
    b["sw3"] = [sbuf(nc, st, f"mu_w3{i}", [128, 16 * 128], BF16)
                for i in range(2)]
    b["sw2"] = [sbuf(nc, st, f"mu_w2{i}", [128, NMH * 512], BF16)
                for i in range(2)]
    b["sh"] = sbuf(nc, st, "mu_h", [128, NMH * D], BF16)
    b["ssg"] = [sbuf(nc, st, f"mu_sg{i}", [128, 512], F32) for i in range(2)]
    b["spp"] = [sbuf(nc, st, f"mu_pp{i}", [128, 512], F32) for i in range(2)]
    b["sev"] = [sbuf(nc, st, f"mu_ev{i}", [128, 512], F32) for i in range(2)]
    b["spr"] = sbuf(nc, st, "mu_pr", [128, NT * 16], F32)
    b["pg"] = [psum(nc, st, f"mu_pg{i}", [128, 512], F32) for i in range(2)]
    b["pu"] = [psum(nc, st, f"mu_pu{i}", [128, 512], F32) for i in range(2)]
    b["pd"] = [psum(nc, st, f"mu_pd{i}", [128, 512], F32) for i in range(2)]
    return b


def build_moelocal():
    nc = bass.Bass("TRN2", target_bir_lowering=False)
    x1 = nc.dram_tensor("x1", [TLOC, D], F32, kind="ExternalInput")
    o1T = nc.dram_tensor("o1T", [D, TLOC], BF16, kind="ExternalInput")
    w_proj = nc.dram_tensor("w_proj", [D, D], F32, kind="ExternalInput")
    w1s = nc.dram_tensor("w1s", [D, MH], F32, kind="ExternalInput")
    w3s = nc.dram_tensor("w3s", [D, MH], F32, kind="ExternalInput")
    w2s = nc.dram_tensor("w2s", [MH, D], F32, kind="ExternalInput")
    gwT = nc.dram_tensor("gwT", [D, 16], F32, kind="ExternalInput")
    gbbc = nc.dram_tensor("gbbc", [128, 16], F32, kind="ExternalInput")
    xn1T = nc.dram_tensor("xn1T", [D, TLOC], BF16, kind="ExternalInput")
    ybase = nc.dram_tensor("ybase", [TLOC, D], F32, kind="ExternalOutput")
    probs = nc.dram_tensor("probs", [TLOC, 16], F32, kind="ExternalOutput")
    st = ExitStack()
    with st:
        kb = KB(nc, st)
        C = emit_consts(kb, st, need_ident=False)
        emit_moelocal(kb, st, C, x1, o1T, w_proj, w1s, w3s, w2s, gwT, gbbc,
                      xn1T, ybase, probs)
        kb.barrier()
        block = st.enter_context(nc.Block())
        kb.finalize(block)
    return nc


def build_moeroutd():
    nc = bass.Bass("TRN2", target_bir_lowering=False)
    xnf = nc.dram_tensor("xnf", [D, NCORES * TLOC], BF16,
                         kind="ExternalInput")
    prf = nc.dram_tensor("prf", [NCORES * TLOC, 2], F32,
                         kind="ExternalInput")
    w1r = nc.dram_tensor("w1r", [2, D, MH], F32, kind="ExternalInput")
    w3r = nc.dram_tensor("w3r", [2, D, MH], F32, kind="ExternalInput")
    w2r = nc.dram_tensor("w2r", [2, MH, D], F32, kind="ExternalInput")
    part = nc.dram_tensor("part", [NCORES * TLOC, D], F32,
                          kind="ExternalOutput")
    st = ExitStack()
    with st:
        kb = KB(nc, st)
        C = emit_consts(kb, st, need_ident=False)
        bufs = mk_moe_bufs(kb, st)
        for el in range(2):
            for ch in range(NCORES):
                emit_moe_unit(
                    kb, st, bufs, el * 8 + ch,
                    w1r.ap()[el], w3r.ap()[el], w2r.ap()[el],
                    xnf.ap()[:, ch * TLOC:(ch + 1) * TLOC],
                    prf.ap()[ch * TLOC:(ch + 1) * TLOC, :],
                    el,
                    part[ch * TLOC:(ch + 1) * TLOC, :],
                    accum=(el == 1))
        kb.barrier()
        block = st.enter_context(nc.Block())
        kb.finalize(block)
    return nc


def build_reduce():
    nc = bass.Bass("TRN2", target_bir_lowering=False)
    yb = nc.dram_tensor("ybase", [TLOC, D], F32, kind="ExternalInput")
    ps = [nc.dram_tensor(f"p{i}", [TLOC, D], F32, kind="ExternalInput")
          for i in range(NCORES)]
    out = nc.dram_tensor("out", [TLOC, D], F32, kind="ExternalOutput")
    st = ExitStack()
    with st:
        kb = KB(nc, st)
        ld = [kb.ctr(f"rd_ld{i}") for i in range(2)]
        dv = kb.ctr("rd_dv")
        stc = [kb.ctr(f"rd_st{i}") for i in range(2)]
        sa = [sbuf(nc, st, f"rd_a{i}", [128, D], F32) for i in range(2)]
        sb = [sbuf(nc, st, f"rd_b{i}", [128, D], F32) for i in range(2)]
        for t in range(NT):
            tb = t % 2
            kb.dma("sync", lambda e, t=t, tb=tb: e.dma_start(
                out=sa[tb][:], in_=yb[t * 128:(t + 1) * 128, :]),
                inc=ld[tb], waits=[(stc[tb], stc[tb].n)])
            for i in range(NCORES):
                kb.dma("sync",
                       lambda e, t=t, tb=tb, i=i: e.dma_start(
                           out=sb[tb][:],
                           in_=ps[i][t * 128:(t + 1) * 128, :]),
                       inc=ld[tb], waits=[(dv, dv.n)])
                myld = ld[tb].n
                kb.op("vector", lambda e, tb=tb: e.tensor_tensor(
                    sa[tb][:], sa[tb][:], sb[tb][:], ALU.add),
                    inc=dv, waits=[(ld[tb], myld)])
            kb.dma("sync", lambda e, t=t, tb=tb: e.dma_start(
                out=out[t * 128:(t + 1) * 128, :], in_=sa[tb][:]),
                inc=stc[tb], waits=[(dv, dv.n)])
        kb.barrier()
        block = st.enter_context(nc.Block())
        kb.finalize(block)
    return nc





class _APW:

    def __init__(self, ap):
        self._ap = ap

    def ap(self):
        return self._ap

    def __getitem__(self, k):
        return self._ap[k]


def build_fused():
    nc = bass.Bass("TRN2", target_bir_lowering=False)
    EI = dict(kind="ExternalInput")
    x_rows = nc.dram_tensor("x_rows", [TLOC, D], F32, **EI)
    wqT = nc.dram_tensor("wqT", [D, D], F32, **EI)
    wkT = nc.dram_tensor("wkT", [D, D], F32, **EI)
    wvT = nc.dram_tensor("wvT", [D, D], F32, **EI)
    woT = nc.dram_tensor("woT", [D, D], F32, **EI)
    gamma_bc = nc.dram_tensor("gamma_bc", [128, 512], F32, **EI)
    beta_bc = nc.dram_tensor("beta_bc", [128, 512], F32, **EI)
    masks_d = nc.dram_tensor("masks", [4, 128, 128], BF16, **EI)
    lut_d = nc.dram_tensor("lut", [16], F32, **EI)
    gT_sl = nc.dram_tensor("gT_sl", [256, INTER], I32, **EI)
    uT_sl = nc.dram_tensor("uT_sl", [256, INTER], I32, **EI)
    dT_sl = nc.dram_tensor("dT_sl", [INTER, 256], I32, **EI)
    w_attn = nc.dram_tensor("w_attn", [D, 3 * D], F32, **EI)
    w_proj = nc.dram_tensor("w_proj", [D, D], F32, **EI)
    gwT = nc.dram_tensor("gwT", [D, 16], F32, **EI)
    gbbc = nc.dram_tensor("gbbc", [128, 16], F32, **EI)
    w1s = nc.dram_tensor("w1s", [D, MH], F32, **EI)
    w2s = nc.dram_tensor("w2s", [MH, D], F32, **EI)
    w3s = nc.dram_tensor("w3s", [D, MH], F32, **EI)
    w1r = nc.dram_tensor("w1r", [2, D, MH], F32, **EI)
    w2r = nc.dram_tensor("w2r", [2, MH, D], F32, **EI)
    w3r = nc.dram_tensor("w3r", [2, D, MH], F32, **EI)
    out_d = nc.dram_tensor("out", [TLOC, D], F32, kind="ExternalOutput")

    wg_sl = nc.dram_tensor("wg_sl", [256, INTER], BF16)
    wu_sl = nc.dram_tensor("wu_sl", [256, INTER], BF16)
    wd_sl = nc.dram_tensor("wd_sl", [INTER, 256], BF16)
    wgT_g = nc.dram_tensor("wgT_g", [D, INTER], BF16, addr_space="Shared")
    wuT_g = nc.dram_tensor("wuT_g", [D, INTER], BF16, addr_space="Shared")
    wd_g = nc.dram_tensor("wd_g", [NCORES * INTER, 256], BF16,
                          addr_space="Shared")
    c13 = nc.dram_tensor("c13", [4 * D, MH], BF16)
    c2 = nc.dram_tensor("c2", [2 * MH, D], BF16)
    g13 = nc.dram_tensor("g13", [NCORES * 4 * D, MH], BF16,
                         addr_space="Shared")
    g2 = nc.dram_tensor("g2", [NCORES * 2 * MH, D], BF16,
                        addr_space="Shared")
    kv0c = nc.dram_tensor("kv0c", [2 * D, TLOC], BF16)
    kv0g = nc.dram_tensor("kv0g", [4 * 2 * D, TLOC], BF16)
    kv1c = nc.dram_tensor("kv1c", [2 * D + 16, TLOC], BF16)
    kv1g = nc.dram_tensor("kv1g", [4 * (2 * D + 16), TLOC], BF16)
    dbg = dict(kind="ExternalOutput") if os.environ.get("MK_DBG") else {}
    qT0 = nc.dram_tensor("qT0", [D, TLOC], BF16, **dbg)
    oT0 = nc.dram_tensor("oT0", [D, TLOC], BF16, **dbg)
    xnT_d = nc.dram_tensor("xnT", [D, TLOC], BF16, **dbg)
    ffn0_d = nc.dram_tensor("ffn0", [TLOC, D], F32, **dbg)
    x1_d = nc.dram_tensor("x1", [TLOC, D], F32, **dbg)
    xn1T_d = nc.dram_tensor("xn1T", [D, TLOC], BF16, **dbg)
    q1T = nc.dram_tensor("q1T", [D, TLOC], BF16, **dbg)
    o1T = nc.dram_tensor("o1T", [D, TLOC], BF16, **dbg)
    qn_d = nc.dram_tensor("qn", [16, TLOC], F32, **dbg)
    probs_d = nc.dram_tensor("probs", [TLOC, 16], F32, **dbg)

    wqT_b = nc.dram_tensor("wqT_b", [D, D], BF16)
    wkT_b = nc.dram_tensor("wkT_b", [D, D], BF16)
    wvT_b = nc.dram_tensor("wvT_b", [D, D], BF16)
    woT_b = nc.dram_tensor("woT_b", [D, D], BF16)
    w_attn_b = nc.dram_tensor("w_attn_b", [D, 3 * D], BF16)
    w_proj_b = nc.dram_tensor("w_proj_b", [D, D], BF16)
    w1s_b = nc.dram_tensor("w1s_b", [D, MH], BF16)
    w3s_b = nc.dram_tensor("w3s_b", [D, MH], BF16)
    w2s_b = nc.dram_tensor("w2s_b", [MH, D], BF16)
    gwT_b = nc.dram_tensor("gwT_b", [D, 16], BF16)

    st = ExitStack()
    with st:
        kb = KB(nc, st)
        C = emit_consts(kb, st, need_ident=True)

        stg = kb.pin_ctr()
        snap = {}
        for (nm, src_t, dst_t) in (
                ("wq", wqT, wqT_b), ("wk", wkT, wkT_b), ("wv", wvT, wvT_b),
                ("wat", w_attn, w_attn_b), ("wo", woT, woT_b),
                ("wp", w_proj, w_proj_b), ("w1s", w1s, w1s_b),
                ("w3s", w3s, w3s_b), ("w2s", w2s, w2s_b),
                ("gw", gwT, gwT_b)):
            kb.dma("gpsimd", lambda e, s=src_t, d=dst_t: e.dma_start(
                out=d[:, :], in_=s[:, :]), inc=stg)
            snap[nm] = stg.n

        kb.new_scope()
        with ExitStack() as psA:
            gh = GhostEmitter(kb, psA, lut_d, gT_sl, uT_sl, dT_sl,
                              wg_sl, wu_sl, wd_sl, C)
            xnT_res, lnc = emit_ln_xnt(kb, psA, x_rows, xnT_d, C)
            emit_qkv0(kb, psA, xnT_res, lnc["ev"], wqT_b, wkT_b, wvT_b,
                      gamma_bc, beta_bc, _APW(qT0.ap()),
                      _APW(kv0c.ap()[0:D, :]), _APW(kv0c.ap()[D:2 * D, :]),
                      C, pre=[(stg, snap["wv"])],
                      filler=lambda: gh.emit_next(8))
            ghstc = gh.finish()
            wst = kb.ctr()
            for el in range(2):
                kb.dma("gpsimd", lambda e, el=el: e.dma_start(
                    out=c13[el * 2 * D:el * 2 * D + D, :],
                    in_=w1r.ap()[el]), inc=wst)
                kb.dma("gpsimd", lambda e, el=el: e.dma_start(
                    out=c13[el * 2 * D + D:(el + 1) * 2 * D, :],
                    in_=w3r.ap()[el]), inc=wst)
                kb.dma("gpsimd", lambda e, el=el: e.dma_start(
                    out=c2[el * MH:(el + 1) * MH, :],
                    in_=w2r.ap()[el]), inc=wst)
            ccg = kb.ctr()
            allg = [(wg_sl, wgT_g), (wu_sl, wuT_g), (wd_sl, wd_g),
                    (c13, g13), (c2, g2)]
            for (i_, (cin, cout)) in enumerate(allg):
                kb.op("gpsimd", lambda e, cin=cin, cout=cout:
                      e.collective_compute(
                          "AllGather", ALU.bypass,
                          replica_groups=[list(range(NCORES))],
                          ins=[cin.ap().opt()], outs=[cout.ap().opt()]),
                      inc=ccg,
                      waits=[(ghstc, ghstc.n), (wst, wst.n)]
                      if i_ == 0 else ())
        kb.barrier()

        kb.new_scope()
        cc0 = kb.pin_ctr()
        kb.op("gpsimd", lambda e: e.collective_compute(
            "AllGather", ALU.bypass,
            replica_groups=[[0, 1, 2, 3], [4, 5, 6, 7]],
            ins=[kv0c.ap().opt()], outs=[kv0g.ap().opt()]), inc=cc0)
        mycc0 = cc0.n
        with ExitStack() as ps:
            wd_view = wd_g.ap().rearrange("(r i p) k -> p i r k",
                                          r=NCORES, p=128)
            emit_ffn0(kb, ps, C, xnT_d, wgT_g, wuT_g, wd_view, ffn0_d)
        kb.barrier()

        kb.new_scope()
        with ExitStack() as ps:
            kv0r = kv0g.ap().rearrange("(r z) t -> r z t", r=4)

            def qsrc0(h):
                return qT0.ap()[h * 128:(h + 1) * 128, :]

            def ksrc0(h):
                return [(r_, kv0r[r_, h * 128:(h + 1) * 128, :])
                        for r_ in range(4)]

            def vsrc0(h):
                return [(r_, kv0r[r_, D:2 * D, h * 128:(h + 1) * 128]
                         .rearrange("(jp tt) d -> tt jp d", tt=128))
                        for r_ in range(4)]

            emit_attn(kb, ps, C, "sq", qsrc0, ksrc0, vsrc0, masks_d,
                      _APW(oT0.ap()), tag="0", pre=[(cc0, mycc0)])
        kb.barrier()

        kb.new_scope()
        with ExitStack() as ps:
            r = emit_combine0(kb, ps, C, x_rows, oT0, woT_b, ffn0_d, x1_d)
            kb.wait("sync", r["stc"][0], r["stc"][0].n)
            kb.wait("sync", r["stc"][1], r["stc"][1].n)
            emit_ln_xnt(kb, ps, x1_d, xn1T_d, C, tag="1")
        kb.barrier()

        kb.new_scope()
        with ExitStack() as ps:
            emit_qkv1(kb, ps, C, xn1T_d, w_attn_b, _APW(q1T.ap()),
                      _APW(kv1c.ap()[0:D, :]), _APW(kv1c.ap()[D:2 * D, :]),
                      qn_d, _APW(kv1c.ap()[2 * D:2 * D + 16, :]))
        kb.barrier()

        kb.new_scope()
        cc1 = kb.ctr()
        kb.op("gpsimd", lambda e: e.collective_compute(
            "AllGather", ALU.bypass,
            replica_groups=[[0, 1, 2, 3], [4, 5, 6, 7]],
            ins=[kv1c.ap().opt()], outs=[kv1g.ap().opt()]), inc=cc1)
        kb.barrier()

        kb.new_scope()
        with ExitStack() as ps:
            Z = 2 * D + 16
            kv1r = kv1g.ap().rearrange("(r z) t -> r z t", r=4)
            knb_src = [(r_, kv1r[r_, 2 * D:Z, :]) for r_ in range(4)]
            snegc, ncdv, ncfin = emit_negc(kb, ps, C, qn_d, knb_src,
                                           knb_eng="gpsimd")
            kb.barrier()

            def qsrc1(h):
                return q1T.ap()[h * 128:(h + 1) * 128, :]

            def ksrc1(h):
                return [(r_, kv1r[r_, h * 128:(h + 1) * 128, :])
                        for r_ in range(4)]

            def vsrc1(h):
                return [(r_, kv1r[r_, D:2 * D, h * 128:(h + 1) * 128]
                         .rearrange("(jp tt) d -> tt jp d", tt=128))
                        for r_ in range(4)]

            emit_attn(kb, ps, C, "sm", qsrc1, ksrc1, vsrc1, masks_d,
                      _APW(o1T.ap()),
                      negc_col=lambda h: snegc[:, h:h + 1], tag="1")
        kb.barrier()

        kb.new_scope()
        with ExitStack() as ps:
            emit_moelocal(kb, ps, C, x1_d, o1T, w_proj_b, w1s_b, w3s_b,
                          w2s_b, gwT_b, gbbc, xn1T_d, out_d, probs_d)
        kb.barrier()

        kb.new_scope()
        with ExitStack() as ps:
            bufs = mk_moe_bufs(kb, ps)
            for e_ in range(NE):
                rk, el = divmod(e_, 2)
                first = (e_ == 0)
                w1_ap = g13.ap()[rk * 4 * D + el * 2 * D:
                                 rk * 4 * D + el * 2 * D + D, :]
                w3_ap = g13.ap()[rk * 4 * D + el * 2 * D + D:
                                 rk * 4 * D + (el + 1) * 2 * D, :]
                w2_ap = g2.ap()[rk * 2 * MH + el * MH:
                                rk * 2 * MH + (el + 1) * MH, :]
                emit_moe_unit(kb, ps, bufs, e_, w1_ap, w3_ap, w2_ap,
                              xn1T_d.ap() if first else None,
                              probs_d.ap() if first else None, e_,
                              _APW(out_d.ap()), accum=True, ncols=16)
        kb.barrier()

        block = st.enter_context(nc.Block())
        kb.finalize(block)
    return nc


def kernel_fused(inputs):
    H_ = host_slices(inputs)
    per, lut = H_["per"], H_["lut"]
    R = range(NCORES)
    nc = build_fused()
    ims = []
    for c in R:
        p = per[c]
        ims.append(dict(
            x_rows=p["x_rows"], wqT=p["wqT"], wkT=p["wkT"], wvT=p["wvT"],
            woT=p["woT"], gamma_bc=p["gamma_bc"], beta_bc=p["beta_bc"],
            masks=p["masks"], lut=lut, gT_sl=p["gT_sl"], uT_sl=p["uT_sl"],
            dT_sl=p["dT_sl"], w_attn=p["w_attn"], w_proj=p["w_proj"],
            gwT=p["gwT"], gbbc=p["gbbc"], w1s=p["w1s"], w2s=p["w2s"],
            w3s=p["w3s"], w1r=p["w1r_sl"], w2r=p["w2r_sl"],
            w3r=p["w3r_sl"]))
    res = _run(nc, ims)
    out = np.zeros((2, TGLOB, D), np.float32)
    for c in R:
        B, r = divmod(c, 4)
        out[B].reshape(64, 128, D)[r::4] = \
            np.asarray(res.results[c]["out"]).reshape(16, 128, D)
    return out


def host_slices(inputs):
    x = np.asarray(inputs["L0_attn_hidden_states"])
    per = []
    gidxT = np.ascontiguousarray(np.asarray(inputs["L0_ffn_gate_idx"]).T)
    uidxT = np.ascontiguousarray(np.asarray(inputs["L0_ffn_up_idx"]).T)
    didxT = np.ascontiguousarray(np.asarray(inputs["L0_ffn_down_idx"]).T)
    wqT = np.ascontiguousarray(np.asarray(inputs["L0_attn_Wq"]).T)
    wkT = np.ascontiguousarray(np.asarray(inputs["L0_attn_Wk"]).T)
    wvT = np.ascontiguousarray(np.asarray(inputs["L0_attn_Wv"]).T)
    woT = np.ascontiguousarray(np.asarray(inputs["L0_attn_Wo"]).T)
    gamma = np.asarray(inputs["L0_attn_gamma"])
    beta = np.asarray(inputs["L0_attn_beta"])
    gamma_bc = np.ascontiguousarray(np.tile(gamma[None, :], (128, 4)))
    beta_bc = np.ascontiguousarray(np.tile(beta[None, :], (128, 4)))
    w_attn = np.asarray(inputs["L1_attn_w_attn"])
    w_proj = np.asarray(inputs["L1_attn_w_proj"])
    gwT = np.ascontiguousarray(np.asarray(inputs["L1_ffn_gate_w"]).T)
    gbbc = np.ascontiguousarray(
        np.tile(np.asarray(inputs["L1_ffn_gate_b"])[None, :], (128, 1)))
    w1s = np.asarray(inputs["L1_ffn_w1_shared"])
    w2s = np.asarray(inputs["L1_ffn_w2_shared"])
    w3s = np.asarray(inputs["L1_ffn_w3_shared"])
    w1r = np.asarray(inputs["L1_ffn_w1_routed"])
    w2r = np.asarray(inputs["L1_ffn_w2_routed"])
    w3r = np.asarray(inputs["L1_ffn_w3_routed"])
    tri = np.triu(np.ones((128, 128), np.float32))
    for c in range(NCORES):
        B, r = divmod(c, 4)
        xr = np.ascontiguousarray(
            x[B].reshape(64, 128, D)[r::4].reshape(TLOC, D))
        m = np.zeros((4, 128, 128), np.float32)
        for ri in range(4):
            if ri < r:
                m[ri] = 1.0
            elif ri == r:
                m[ri] = tri
        per.append(dict(
            x_rows=xr,
            gT_sl=np.ascontiguousarray(gidxT[c * 256:(c + 1) * 256, :]),
            uT_sl=np.ascontiguousarray(uidxT[c * 256:(c + 1) * 256, :]),
            dT_sl=np.ascontiguousarray(didxT[:, c * 256:(c + 1) * 256]),
            wqT=wqT, wkT=wkT, wvT=wvT, woT=woT,
            gamma_bc=gamma_bc, beta_bc=beta_bc,
            masks=np.ascontiguousarray(m.astype(BF)),
            w_attn=w_attn, w_proj=w_proj, gwT=gwT, gbbc=gbbc,
            w1s=w1s, w2s=w2s, w3s=w3s,
            w1r_sl=np.ascontiguousarray(w1r[2 * c:2 * c + 2]),
            w2r_sl=np.ascontiguousarray(w2r[2 * c:2 * c + 2]),
            w3r_sl=np.ascontiguousarray(w3r[2 * c:2 * c + 2]),
        ))
    return dict(per=per, lut=np.ascontiguousarray(
        np.asarray(inputs["L0_ffn_lut"]).astype(np.float32)))


def _run(nc, ims):
    return run_bass_kernel_spmd(nc, ims, core_ids=list(range(NCORES)))


def _asm_kv(kT, vv):
    KTb, Vb = [], []
    for B in range(2):
        KTb.append(np.ascontiguousarray(
            np.concatenate([kT[B * 4 + r] for r in range(4)], axis=1)))
        Vb.append(np.ascontiguousarray(
            np.concatenate([vv[B * 4 + r] for r in range(4)], axis=0)))
    return KTb, Vb


def kernel(**inputs):
    if os.environ.get("MK_UNFUSED", "0") != "1":
        return kernel_fused(inputs)
    H_ = host_slices(inputs)
    per, lut = H_["per"], H_["lut"]
    R = range(NCORES)

    nc = build_ghost()
    res = _run(nc, [dict(lut=lut, gT_sl=per[c]["gT_sl"],
                         uT_sl=per[c]["uT_sl"], dT_sl=per[c]["dT_sl"])
                    for c in R])
    wgT = np.concatenate([np.asarray(res.results[c]["wg_sl"]) for c in R], 0)
    wuT = np.concatenate([np.asarray(res.results[c]["wu_sl"]) for c in R], 0)
    wdT = np.concatenate([np.asarray(res.results[c]["wd_sl"]) for c in R], 1)
    wgT = np.ascontiguousarray(wgT)
    wuT = np.ascontiguousarray(wuT)
    wdT = np.ascontiguousarray(wdT)

    nc = build_l0main()
    res = _run(nc, [dict(x_rows=per[c]["x_rows"], wqT=per[c]["wqT"],
                         wkT=per[c]["wkT"], wvT=per[c]["wvT"],
                         gamma_bc=per[c]["gamma_bc"],
                         beta_bc=per[c]["beta_bc"]) for c in R])
    xnT = [np.asarray(res.results[c]["xnT"]) for c in R]
    qT = [np.asarray(res.results[c]["qT"]) for c in R]
    kT = [np.asarray(res.results[c]["kT"]) for c in R]
    vv = [np.asarray(res.results[c]["v"]) for c in R]

    KTb, Vb = _asm_kv(kT, vv)
    nc = build_attn0()
    res = _run(nc, [dict(qT=qT[c], ktb=KTb[c // 4], vb=Vb[c // 4],
                         masks=per[c]["masks"]) for c in R])
    oT = [np.asarray(res.results[c]["oT"]) for c in R]

    nc = build_ffn0()
    res = _run(nc, [dict(xnT=xnT[c], wgT=wgT, wuT=wuT, wdT=wdT) for c in R])
    ffn0 = [np.asarray(res.results[c]["ffn0"]) for c in R]

    nc = build_combine0()
    res = _run(nc, [dict(x_rows=per[c]["x_rows"], oT=oT[c],
                         woT=per[c]["woT"], ffn0=ffn0[c]) for c in R])
    x1 = [np.asarray(res.results[c]["x1"]) for c in R]
    xn1T = [np.asarray(res.results[c]["xn1T"]) for c in R]

    nc = build_qkv1()
    res = _run(nc, [dict(xn1T=xn1T[c], w_attn=per[c]["w_attn"]) for c in R])
    q1T = [np.asarray(res.results[c]["q1T"]) for c in R]
    k1T = [np.asarray(res.results[c]["k1T"]) for c in R]
    v1 = [np.asarray(res.results[c]["v1"]) for c in R]
    qn = [np.asarray(res.results[c]["qn"]) for c in R]
    kn = [np.asarray(res.results[c]["kn"]) for c in R]

    K1Tb, V1b = _asm_kv(k1T, v1)
    knb = [np.ascontiguousarray(
        np.concatenate([kn[B * 4 + r] for r in range(4)], axis=1))
        for B in range(2)]
    nc = build_attn1()
    res = _run(nc, [dict(q1T=q1T[c], k1tb=K1Tb[c // 4], v1b=V1b[c // 4],
                         masks=per[c]["masks"], qn=qn[c], knb=knb[c // 4])
                    for c in R])
    o1T = [np.asarray(res.results[c]["o1T"]) for c in R]

    nc = build_moelocal()
    res = _run(nc, [dict(x1=x1[c], o1T=o1T[c], w_proj=per[c]["w_proj"],
                         w1s=per[c]["w1s"], w3s=per[c]["w3s"],
                         w2s=per[c]["w2s"], gwT=per[c]["gwT"],
                         gbbc=per[c]["gbbc"], xn1T=xn1T[c]) for c in R])
    ybase = [np.asarray(res.results[c]["ybase"]) for c in R]
    probs = [np.asarray(res.results[c]["probs"]) for c in R]

    xnf = np.ascontiguousarray(np.concatenate(xn1T, axis=1))
    prf = np.ascontiguousarray(np.concatenate(probs, axis=0))
    nc = build_moeroutd()
    res = _run(nc, [dict(xnf=xnf,
                         prf=np.ascontiguousarray(prf[:, 2 * c:2 * c + 2]),
                         w1r=per[c]["w1r_sl"], w3r=per[c]["w3r_sl"],
                         w2r=per[c]["w2r_sl"]) for c in R])
    part = [np.asarray(res.results[c]["part"]) for c in R]

    nc = build_reduce()
    ims = []
    for c in R:
        d = dict(ybase=ybase[c])
        for i in R:
            d[f"p{i}"] = np.ascontiguousarray(
                part[i][c * TLOC:(c + 1) * TLOC, :])
        ims.append(d)
    res = _run(nc, ims)
    outr = [np.asarray(res.results[c]["out"]) for c in R]

    out = np.zeros((2, TGLOB, D), np.float32)
    for c in R:
        B, r = divmod(c, 4)
        out[B].reshape(64, 128, D)[r::4] = outr[c].reshape(16, 128, D)
    return out
